# version 10
# baseline (speedup 1.0000x reference)
import math
import numpy as np
import concourse.bass as bass
import concourse.mybir as mybir
from concourse.bass_utils import run_bass_kernel_spmd

F32 = mybir.dt.float32
BF16 = mybir.dt.bfloat16
U8 = mybir.dt.uint8
I16 = mybir.dt.int16
AF = mybir.ActivationFunctionType
ALU = mybir.AluOpType
AX = mybir.AxisListType

D = 2048
T = 4096
NQ = 1024
NH = 16
VD = 128
NF = 16384
LAMBDA_INIT = 0.8 - 0.6 * math.exp(-0.3)
ONE_M_LI = 1.0 - LAMBDA_INIT
GC1 = 0.7978845608028654
GC3 = 0.7978845608028654 * 0.044715
EPS = 1e-5

CONST_VALS = [0.0, 1.0, 0.125, -12.0, 0.5, -0.25, EPS, 1.0 / D, 1.0 / VD,
              -0.5, math.log(ONE_M_LI), GC1, GC3, LAMBDA_INIT, 0.0625,
              -0.499999, 63.5, -1.0 / 256.0]

N_CORES = 8
TILES = [[c4, 4 + c4, 11 - c4, 15 - c4] for c4 in range(4)]
PC = [8, 16, 24, 32]
MOFF = [0, 8, 24, 48]
NCH_TOT = 80



def _nperm():
    p = []
    for b in range(NF // 256):
        p.extend(range(256 * b, 256 * (b + 1), 2))
        p.extend(range(256 * b + 1, 256 * (b + 1), 2))
    return np.array(p)


def _pack_u8(a):
    a = np.ascontiguousarray(np.asarray(a, np.int32))
    return np.ascontiguousarray(a.view(np.uint8).reshape(*a.shape, 4)[..., 0])


def _chunkmajor(w):
    Dr, C = w.shape
    return np.ascontiguousarray(w.reshape(Dr // 128, 128, C).transpose(1, 0, 2)
                                .reshape(128, (Dr // 128) * C))


def _host_inputs(inputs):
    x = np.asarray(inputs["L0_attn_x"], np.float32).reshape(2, T, D)
    wqkv = np.asarray(inputs["L0_attn_Wqkv"], np.float32)
    wproj = np.asarray(inputs["L0_attn_Wproj"], np.float32)

    kcols, vcols, qcols = [], [], []
    for h in range(NH):
        kcols.extend(range(1024 + 64 * h, 1024 + 64 * h + 64))
        kcols.extend(range(3072 + 64 * h, 3072 + 64 * h + 64))
    for h in range(NH):
        vcols.extend(range(4096 + 128 * h, 4096 + 128 * h + 128))
    for h in range(NH):
        qcols.extend(range(64 * h, 64 * h + 64))
        qcols.extend(range(2048 + 64 * h, 2048 + 64 * h + 64))
    wqkv_k = _chunkmajor(np.ascontiguousarray(wqkv[:, kcols]))
    wqkv_v = _chunkmajor(np.ascontiguousarray(wqkv[:, vcols]))
    wqkv_q = _chunkmajor(np.ascontiguousarray(wqkv[:, qcols]))
    wproj_cm = _chunkmajor(np.ascontiguousarray(wproj))

    lamv = np.zeros((128, 256), np.float32)
    lamv[:, 0:64] = np.asarray(inputs["L0_attn_lambda_q1"], np.float32)
    lamv[:, 64:128] = np.asarray(inputs["L0_attn_lambda_q2"], np.float32)
    lamv[:, 128:192] = np.asarray(inputs["L0_attn_lambda_k1"], np.float32)
    lamv[:, 192:256] = np.asarray(inputs["L0_attn_lambda_k2"], np.float32)

    import ml_dtypes
    consts = np.zeros((128, 18), np.float32)
    consts[:, :len(CONST_VALS)] = CONST_VALS
    ident = np.eye(128, dtype=ml_dtypes.bfloat16)

    perm = _nperm()
    gate_pk = np.ascontiguousarray(_pack_u8(inputs["L0_ffn_gate_W_packed"])[perm, :].T)
    up_pk = np.ascontiguousarray(_pack_u8(inputs["L0_ffn_up_W_packed"])[perm, :].T)
    down_pk = np.ascontiguousarray(_pack_u8(inputs["L0_ffn_down_W_packed"]).T)
    gsc = np.ascontiguousarray(np.asarray(inputs["L0_ffn_gate_scales"], np.float32)[perm, :].T)
    usc = np.ascontiguousarray(np.asarray(inputs["L0_ffn_up_scales"], np.float32)[perm, :].T)
    dsc = np.ascontiguousarray(np.asarray(inputs["L0_ffn_down_scales"], np.float32).T)

    shared = {
        "wqkv_k": wqkv_k, "wqkv_v": wqkv_v, "wqkv_q": wqkv_q, "wproj": wproj_cm,
        "lamv": lamv, "consts": consts, "ident": ident,
        "gate_pk": gate_pk, "up_pk": up_pk, "down_pk": down_pk,
        "gsc": gsc, "usc": usc, "dsc": dsc,
    }

    kt = np.arange(128)
    qt = np.arange(256)
    in_maps = []
    for c in range(N_CORES):
        b, c4 = divmod(c, 4)
        tiles = TILES[c4]
        m = np.zeros((32, 128, 512), ml_dtypes.bfloat16)
        for jt in range(4):
            for i8 in range(8):
                kc = PC[jt] - 8 + i8
                m1 = ((128 * kc + kt[:, None])
                      <= (256 * tiles[jt] + qt[None, :256])).astype(np.float32)
                m[jt * 8 + i8] = np.concatenate([m1, m1], 1)
        d = dict(shared)
        d["x_kv"] = np.ascontiguousarray(x[b])
        d["x_q"] = np.ascontiguousarray(
            np.concatenate([x[b, 256 * t:256 * (t + 1)] for t in tiles], 0))
        d["masks"] = np.ascontiguousarray(m.transpose(1, 0, 2).reshape(128, 32 * 512))
        in_maps.append(d)
    return in_maps



def _aslist(w):
    if w is None:
        return []
    if isinstance(w, tuple) and len(w) == 2 and isinstance(w[0], str):
        return [w]
    out = []
    for m in w:
        out.extend(_aslist(m))
    return out


class Prog:
    ENGINES = ("tensor", "vector", "scalar", "gpsimd", "sync")

    def __init__(self, nc, ctx):
        self.nc = nc
        self.ctx = ctx
        self.body = {e: [] for e in self.ENGINES}
        self.sems = {}
        self.waited = {e: {} for e in self.ENGINES}

    def sem(self, name):
        if name not in self.sems:
            h = self.ctx.enter_context(self.nc.semaphore(name))
            self.sems[name] = [h, 0]
        return self.sems[name]

    def wait(self, eng, mark):
        for m in _aslist(mark):
            if m[1] <= 0 or self.waited[eng].get(m[0], 0) >= m[1]:
                continue
            self.waited[eng][m[0]] = m[1]
            h = self.sems[m[0]][0]
            self.body[eng].append(lambda e, h=h, v=m[1]: e.wait_ge(h, v))

    def op(self, eng, fname, *a, inc=None, **k):
        if inc is not None:
            s = self.sem(inc[0])
            s[1] += inc[1]
            h, val = s[0], s[1]

            def run(e, fname=fname, a=a, k=k, h=h, amt=inc[1]):
                getattr(e, fname)(*a, **k).then_inc(h, amt)
            self.body[eng].append(run)
            return (inc[0], val)

        def run(e, fname=fname, a=a, k=k):
            getattr(e, fname)(*a, **k)
        self.body[eng].append(run)
        return None

    def mark(self, name):
        return (name, self.sems[name][1]) if name in self.sems else (name, 0)


SELF_SERIAL = True


class SerialEngine:
    def __init__(self, prog, eng, semname, serial=True):
        self.p, self.eng, self.semname = prog, eng, semname
        self.serial = serial
        self.p.sem(semname)

    def __init2__(self):
        pass

    def op(self, fname, *a, waits=(), noserial=False, **k):
        self.p.wait(self.eng, waits)
        if self.serial and not noserial:
            n = self.p.sems[self.semname][1]
            if n > 0:
                self.p.wait(self.eng, (self.semname, n))
        return self.p.op(self.eng, fname, *a, inc=(self.semname, 1), **k)

    def mark(self):
        return self.p.mark(self.semname)


class DmaQueue:
    def __init__(self, prog, eng):
        self.p, self.eng = prog, eng

    def dma(self, semname, out, in_, waits=(), transpose=False):
        self.p.wait(self.eng, waits)
        n = self.p.sems[semname][1] if semname in self.p.sems else 0
        if n > 0:
            self.p.wait(self.eng, (semname, n))
        if transpose:
            return self.p.op(self.eng, "dma_start_transpose", out, in_,
                             inc=(semname, 16))
        return self.p.op(self.eng, "dma_start", out=out, in_=in_,
                         inc=(semname, 16))

    def dma_group(self, semname, pairs, waits=()):
        self.p.wait(self.eng, waits)
        for out, in_ in pairs:
            self.p.op(self.eng, "dma_start", out=out, in_=in_, inc=(semname, 16))
        return self.p.mark(semname)


class TensorEng:
    def __init__(self, prog):
        self.p = prog
        self.rot = 0
        self.p.sem("st0")
        self.p.sem("st1")

    def wait(self, mark):
        self.p.wait("tensor", mark)

    def mm(self, *a, **k):
        self.p.op("tensor", "matmul", *a, **k)

    def mile(self, fname, *a, **k):
        name = f"st{self.rot}"
        self.rot ^= 1
        n = self.p.sems[name][1]
        if n > 0:
            self.p.wait("tensor", (name, n))
        return self.p.op("tensor", fname, *a, inc=(name, 1), **k)

    def fence(self):
        return [self.p.mark("st0"), self.p.mark("st1")]



def build_and_run(in_maps, trace=False):
    from contextlib import ExitStack
    nc = bass.Bass("TRN2", target_bir_lowering=False)

    di = lambda n, s, dt: nc.dram_tensor(n, s, dt, kind="ExternalInput")
    dd = lambda n, s, dt: nc.dram_tensor(n, s, dt, kind="Internal")
    x_kv = di("x_kv", [T, D], F32)
    x_q = di("x_q", [NQ, D], F32)
    wqkv_k = di("wqkv_k", [128, 16 * 2048], F32)
    wqkv_v = di("wqkv_v", [128, 16 * 2048], F32)
    wqkv_q = di("wqkv_q", [128, 16 * 2048], F32)
    wproj = di("wproj", [128, 16 * 2048], F32)
    lamv = di("lamv", [128, 256], F32)
    consts = di("consts", [128, 18], F32)
    ident = di("ident", [128, 128], BF16)
    masks = di("masks", [128, 32 * 512], BF16)
    gate_pk = di("gate_pk", [1024, NF], U8)
    up_pk = di("up_pk", [1024, NF], U8)
    down_pk = di("down_pk", [8192, D], U8)
    gsc = di("gsc", [64, NF], F32)
    usc = di("usc", [64, NF], F32)
    dsc = di("dsc", [512, D], F32)
    out = nc.dram_tensor("out", [NQ, D], F32, kind="ExternalOutput")

    xn_kv_d = dd("xn_kv_d", [T, D], BF16)
    xn_q_d = dd("xn_q_d", [NQ, D], BF16)
    xnqT_d = dd("xnqT_d", [D, NQ], BF16)
    k12T_d = dd("k12T_d", [NH, 128, T], BF16)
    vext_d = dd("vext_d", [NH, 128, 32, 132], BF16)
    q12T_d = dd("q12T_d", [NH, 128, NQ], BF16)
    s_d = dd("s_d", [NQ, D], F32)
    attnT_d = dd("attnT_d", [8, 128, NH, 128], BF16)
    wgT_d = dd("wgT_d", [16, 128, NF], BF16)
    wuT_d = dd("wuT_d", [16, 128, NF], BF16)
    wdT_d = dd("wdT_d", [128, 128, D], BF16)

    ctx = ExitStack()
    with ctx:
        sb = lambda n, s, dt: ctx.enter_context(nc.sbuf_tensor(n, s, dt))
        psal = lambda n, s, dt: ctx.enter_context(nc.psum_tensor(n, s, dt))

        AR0 = sb("AR0", [128, 65536], U8)
        AR1 = sb("AR1", [128, 65536], U8)
        AR2 = sb("AR2", [128, 49152], U8)
        AR3 = sb("AR3", [128, 22528], U8)

        def v8(ar, off, nbytes, dt, cols):
            return ar[:, off:off + nbytes].bitcast(dt)[:, 0:cols]

        wk_sb = v8(AR0, 0, 65536, BF16, 16 * 2048)
        wv_sb = wk_sb
        wq_sb = wk_sb
        wp_sb = wk_sb
        inter = v8(AR0, 0, 65536, BF16, 128 * 256)

        xt = [v8(AR1, 0, 8192, F32, D), v8(AR1, 8192, 8192, F32, D)]
        lnsq = v8(AR1, 16384, 8192, F32, D)
        xnt = [v8(AR1, 24576, 4096, BF16, D), v8(AR1, 28672, 4096, BF16, D)]
        xnT_sb = [v8(AR2, 0, 16384, BF16, 16 * 512),
                  v8(AR2, 16384, 16384, BF16, 16 * 512)]
        qkv_ev = [v8(AR1, 0, 1024, BF16, 512), v8(AR1, 1024, 1024, BF16, 512)]
        vtr_ev = [v8(AR1, 2048, 1024, BF16, 512), v8(AR1, 3072, 1024, BF16, 512)]
        vex_ev = [v8(AR1, 4096, 1088, BF16, 4 * 132), v8(AR1, 5248, 1088, BF16, 4 * 132)]
        wg_grp = [v8(AR1, 0, 16384, BF16, 16 * 512),
                  v8(AR1, 16384, 16384, BF16, 16 * 512)]
        wu_grp = [v8(AR1, 32768, 16384, BF16, 16 * 512),
                  v8(AR1, 49152, 16384, BF16, 16 * 512)]

        k12 = [v8(AR2, 0, 8192, BF16, T), v8(AR2, 8192, 8192, BF16, T)]
        vex = [v8(AR2, 16384, 8448, BF16, 32 * 132), v8(AR2, 24832, 8448, BF16, 32 * 132)]
        q12 = [v8(AR2, 33280, 2048, BF16, NQ), v8(AR2, 35328, 2048, BF16, NQ)]
        att_qs = [v8(AR2, 0, 4096, BF16, D), v8(AR2, 4096, 4096, BF16, D)]
        s_ev = [v8(AR2, 8192, 8192, F32, D)]
        fin_x = v8(AR2, 0, 8192, F32, D)
        fin_s = v8(AR2, 8192, 8192, F32, D)
        fin_sil = v8(AR2, 16384, 8192, F32, D)
        fin_o = v8(AR2, 24576, 8192, F32, D)
        xnq_sb = v8(AR2, 32768, 8192, BF16, 16 * 256)
        wd_st = [v8(AR2, 40960, 4096, BF16, 2048), v8(AR2, 45056, 4096, BF16, 2048)]

        DQW = 1024

        def dq_lane(specs):
            (ar_pk, o_pk), (ar_m, o_m) = specs
            return dict(
                pk=v8(ar_pk, o_pk, 1024, U8, DQW),
                c4=v8(ar_m, o_m, 4096, I16, 2 * DQW),
                s15=v8(ar_m, o_m + 4096, 4096, I16, 2 * DQW),
                xb=v8(ar_m, o_m + 8192, 4096, I16, 2 * DQW),
                r=v8(ar_m, o_m + 12288, 4096, BF16, 2 * DQW),
                scb2=v8(ar_m, o_m + 16384, 4096, BF16, 2 * DQW),
            )
        dq_lanes = [dq_lane([(AR2, 37888), (AR3, 0)]),
                    dq_lane([(AR1, 32768 + 5120), (AR1, 32768 + 10240)])]

        cst = sb("cst", [128, 18], F32)
        for i, v in enumerate(CONST_VALS):
            nc.const_aps.register(F32, v, cst[:, i:i + 1])
        lam_sb = sb("lam_sb", [128, 8], F32)
        lamv_sb = v8(AR1, 16384, 1024, F32, 256)
        lamx = v8(AR1, 20480, 1024, F32, 256)
        id_b = sb("id_b", [128, 128], BF16)
        lnst = sb("lnst", [128, 8], F32)
        ep_st = sb("ep_st", [128, 16], F32)
        p12 = [sb(f"p12_{i}", [128, 1024], BF16) for i in range(2)]
        scr = [sb(f"scr{i}", [128, 256], F32) for i in range(4)]
        ep_t2 = scr[0][:, 0:128]
        ep_diff = scr[1][:, 0:128]
        ep_sq = scr[2][:, 0:128]
        ep_on = scr[3][:, 0:128].bitcast(BF16)[:, 0:128]
        gl_sq, gl_v2, gl_v3, gl_t = (s[:] for s in scr)
        gl_up = scr[2][:]
        att_st = [sb(f"att_st{i}", [128, 128], BF16) for i in range(2)]

        ps_ab = psal("ps_ab", [128, 1024], F32)
        ps_cd = psal("ps_cd", [128, 1024], F32)
        ps_big = psal("ps_big", [128, 2048], F32)

        class _PV:
            def __init__(self, t, lo):
                self.t, self.lo = t, lo
            def __getitem__(self, key):
                assert isinstance(key, tuple) or key == slice(None)
                if not isinstance(key, tuple):
                    return self.t[:, self.lo:self.lo + 512]
                p, f = key
                assert p == slice(None)
                return self.t[:, self.lo + (f.start or 0):self.lo + f.stop]
        ps_a = _PV(ps_ab, 0)
        ps_b = _PV(ps_ab, 512)
        ps_c = _PV(ps_cd, 0)
        ps_d = _PV(ps_cd, 512)
        ps4 = [ps_a, ps_b, ps_c, ps_d]

        p = Prog(nc, ctx)
        V = SerialEngine(p, "vector", "sv")
        S = SerialEngine(p, "scalar", "ss", serial=False)
        G = SerialEngine(p, "gpsimd", "sg", serial=False)
        TE = TensorEng(p)
        Dsy = DmaQueue(p, "sync")
        Dsc = DmaQueue(p, "scalar")
        Dgp = DmaQueue(p, "gpsimd")

        m_ini = Dsy.dma_group("ld_ini", [
            (cst[:], consts.ap()),
            (id_b[:], ident.ap()),
            (lamv_sb, lamv.ap()),
        ])
        m_wk = Dgp.dma("ld_w0", wk_sb, wqkv_k.ap())

        S.op("copy", lam_sb[:, 1:2], cst[:, 0:1], waits=[m_ini])

        V.op("tensor_tensor", lamx[:, 0:128], lamv_sb[:, 0:128],
             lamv_sb[:, 128:256], ALU.mult, waits=[m_ini])
        V.op("reduce_sum", lam_sb[:, 2:3], lamx[:, 0:64], axis=AX.X)
        m_d12 = V.op("reduce_sum", lam_sb[:, 3:4], lamx[:, 64:128], axis=AX.X)
        S.op("activation", lam_sb[:, 4:5], lam_sb[:, 2:3], AF.Exp,
             waits=[m_d12])
        m_e2 = S.op("activation", lam_sb[:, 5:6], lam_sb[:, 3:4], AF.Exp)
        V.op("tensor_tensor", lam_sb[:, 6:7], lam_sb[:, 4:5],
             lam_sb[:, 5:6], ALU.subtract, waits=[m_e2])
        m_lam = V.op("tensor_scalar", lam_sb[:, 0:1], lam_sb[:, 6:7],
                     LAMBDA_INIT, None, ALU.add)

        sp_xn_mark = [None, None]
        ln_v_mark = [None, None]
        for i in range(40):
            par = i % 2
            if i < 32:
                src_ = x_kv.ap()[128 * i:128 * (i + 1), :]
                dst = xn_kv_d.ap()[128 * i:128 * (i + 1), :]
            else:
                j = i - 32
                src_ = x_q.ap()[128 * j:128 * (j + 1), :]
                dst = xn_q_d.ap()[128 * j:128 * (j + 1), :]
            m_ld = Dsy.dma(f"ld_x{par}", xt[par], src_,
                           waits=[ln_v_mark[par], m_lam if i == 0 else None])
            V.op("reduce_sum", lnst[:, 0:1], xt[par], axis=AX.X,
                 waits=[m_ld])
            m_sq = S.op("activation", lnsq, xt[par], AF.Square,
                        accum_out=lnst[:, 1:2], waits=[m_ld, V.mark()])
            V.op("tensor_scalar", lnst[:, 2:3], lnst[:, 0:1], 1.0 / D, None,
                 ALU.mult, waits=[m_sq])
            V.op("tensor_tensor", lnst[:, 3:4], lnst[:, 2:3], lnst[:, 2:3],
                 ALU.mult)
            V.op("tensor_scalar", lnst[:, 4:5], lnst[:, 1:2], 1.0 / D, EPS,
                 ALU.mult, ALU.add)
            m_var = V.op("tensor_tensor", lnst[:, 5:6], lnst[:, 4:5],
                         lnst[:, 3:4], ALU.subtract)
            S.op("activation", lnst[:, 6:7], lnst[:, 5:6], AF.Ln,
                 waits=[m_var])
            m_rstd = S.op("activation", lnst[:, 7:8], lnst[:, 6:7], AF.Exp,
                          scale=-0.5)
            m_xn = V.op("tensor_scalar", xnt[par], xt[par],
                        lnst[:, 2:3], lnst[:, 7:8], ALU.subtract, ALU.mult,
                        waits=[m_rstd, sp_xn_mark[par]])
            ln_v_mark[par] = m_xn
            sp_xn_mark[par] = Dsc.dma(f"sp_xn{par}", dst, xnt[par],
                                      waits=[m_xn])
        m_ln_done = [sp_xn_mark[0], sp_xn_mark[1]]
        m_ln_v = V.mark()

        dq_units = []
        for mat in range(3):
            if mat < 2:
                dq_units += [(mat, J, cc) for J in range(8) for cc in range(16)]
            else:
                dq_units += [(2, J, cc) for J in range(64) for cc in range(2)]
        dq_state = {"s1": 0, "s2": 0, "m": [dict(), dict()]}

        def dq_stage1(i):
            mat, J, cc = dq_units[i]
            L = i % 2
            t = dq_lanes[L]
            pm = dq_state["m"][L]
            W = DQW
            pk_t, sc_t, w_d = [(gate_pk, gsc, wgT_d), (up_pk, usc, wuT_d),
                               (down_pk, dsc, wdT_d)][mat]
            m_pk = Dsc.dma(f"ld_pk{L}", t["pk"],
                           pk_t.ap()[128 * J:128 * (J + 1), cc * W:(cc + 1) * W],
                           waits=[pm.get("chi"), pm.get("clo")])
            src_sc = (sc_t.ap()[8 * J:8 * (J + 1), cc * W:(cc + 1) * W]
                      .unsqueeze(1).broadcast_to([8, 16, W]))
            Dgp.dma(f"ld_sc{L}", t["scb2"][:, 0:W], src_sc,
                    waits=[pm.get("o")])
            m_sc = Dgp.dma(f"ld_sc{L}", t["scb2"][:, W:2 * W], src_sc)
            m_chi = S.op("activation", t["c4"][:, W:2 * W], t["pk"],
                         AF.Identity, scale=0.0625, bias=-0.499999,
                         waits=[m_pk, pm.get("xb")])
            m_clo = V.op("scalar_tensor_tensor", t["c4"][:, 0:W],
                         t["c4"][:, W:2 * W], -16.0, t["pk"],
                         ALU.mult, ALU.add, noserial=True,
                         waits=[m_chi, pm.get("xb")])
            m_s15 = V.op("tensor_scalar", t["s15"], t["c4"], 3, 15,
                         ALU.logical_shift_right, ALU.logical_shift_left,
                         noserial=True, waits=[m_clo, pm.get("orr")])
            m_u = V.op("tensor_single_scalar", t["c4"], t["c4"], 7,
                       ALU.bitwise_and, noserial=True, waits=[m_s15])
            m_xb = V.op("tensor_scalar", t["xb"], t["c4"], 64, 16128,
                        ALU.mult, ALU.add, noserial=True,
                        waits=[m_u] + _aslist(pm.get("sp")))
            pm.update(pk=m_pk, sc=m_sc, chi=m_chi, clo=m_clo, s15=m_s15,
                      xb=m_xb)

        def dq_stage2(i):
            mat, J, cc = dq_units[i]
            L = i % 2
            t = dq_lanes[L]
            pm = dq_state["m"][L]
            W = DQW
            w_d = [wgT_d, wuT_d, wdT_d][mat]
            Dsp = Dsc
            m_r = S.op("activation", t["r"], t["xb"], AF.Relu, bias=63.5,
                       scale=-1.0 / 256.0, waits=[pm["xb"], pm.get("val")])
            xb_bf = t["xb"].bitcast(BF16)[:, 0:2 * W]
            m_val = V.op("tensor_tensor", xb_bf, xb_bf, t["r"], ALU.subtract,
                         noserial=True, waits=[m_r])
            m_or = V.op("tensor_tensor", t["xb"], t["xb"], t["s15"],
                        ALU.bitwise_or, noserial=True, waits=[m_val])
            m_o = V.op("tensor_tensor", xb_bf, xb_bf, t["scb2"], ALU.mult,
                       noserial=True, waits=[m_or, pm["sc"]])
            m_spa = Dsp.dma(f"sp_dq{L}a",
                            w_d.ap()[2 * J, :, cc * W:(cc + 1) * W],
                            xb_bf[:, 0:W], waits=[m_o])
            m_spb = Dsp.dma(f"sp_dq{L}b",
                            w_d.ap()[2 * J + 1, :, cc * W:(cc + 1) * W],
                            xb_bf[:, W:2 * W])
            pm.update(r=m_r, val=m_val, orr=m_or, o=m_o, sp=[m_spa, m_spb])

        def emit_dq_unit():
            if dq_state["s1"] < len(dq_units):
                dq_stage1(dq_state["s1"])
                dq_state["s1"] += 1
            lim = dq_state["s1"] - (1 if dq_state["s1"] < len(dq_units) else 0)
            if dq_state["s2"] < lim:
                dq_stage2(dq_state["s2"])
                dq_state["s2"] += 1

        qkv_ps = [ps_a, ps_c]
        ps_war = [None, None]
        tr_bank_war = [None, None]
        xnT_war = [None, None]
        kq_war = [None, None]
        vtr_war = [None, None]
        vexev_war = [None, None]
        k_sp = v_sp = q_sp = None
        m_vex_init = None

        def qkv_pass(tiles, xn_src, m_w, mode):
            nonlocal k_sp, v_sp, q_sp, m_vex_init
            for jj, j in enumerate(tiles):
                par = jj % 2
                t0 = (j - 8) * 512 if mode == "q" else j * 512
                wlist = list(m_ln_done) + _aslist(xnT_war[par])
                for c0 in range(16):
                    Dsy.dma(f"ld_xnT{par}",
                            xnT_sb[par][:, 512 * c0:512 * (c0 + 1)],
                            xn_src.ap()[t0:t0 + 512, 128 * c0:128 * (c0 + 1)],
                            waits=wlist if c0 == 0 else (), transpose=True)
                m_xnT = p.mark(f"ld_xnT{par}")
                if mode == "q":
                    for c0 in range(16):
                        Dsc.dma("sp_xnqT",
                                xnqT_d.ap()[128 * c0:128 * (c0 + 1), t0:t0 + 512],
                                xnT_sb[par][:, 512 * c0:512 * (c0 + 1)],
                                waits=[m_xnT] if c0 == 0 else ())
                TE.wait(m_xnT)
                TE.wait(m_w)
                for cc in range(16):
                    ppi = cc % 2
                    pp = qkv_ps[ppi]
                    TE.wait(ps_war[ppi])
                    m_mm = None
                    for hc in range(16):
                        kw = dict(start=(hc == 0), stop=(hc == 15))
                        lhsT = wk_sb[:, 2048 * hc + 128 * cc:
                                     2048 * hc + 128 * (cc + 1)]
                        rhs = xnT_sb[par][:, 512 * hc:512 * (hc + 1)]
                        if hc == 15:
                            m_mm = TE.mile("matmul", pp[:], lhsT, rhs, **kw)
                        else:
                            TE.mm(pp[:], lhsT, rhs, **kw)
                    bufp = cc % 2
                    if mode in ("k", "q"):
                        m_ev = S.op("copy", qkv_ev[bufp], pp[:],
                                    waits=[m_mm, kq_war[bufp],
                                           m_ln_v if (jj == 0 and cc < 2) else None])
                        ps_war[ppi] = m_ev
                        if mode == "q":
                            m_sp = Dsc.dma("sp_q", q12T_d.ap()[cc, :, t0:t0 + 512],
                                           qkv_ev[bufp], waits=[m_ev])
                            q_sp = m_sp
                        else:
                            m_sp = Dsc.dma("sp_k", k12T_d.ap()[cc, :, t0:t0 + 512],
                                           qkv_ev[bufp], waits=[m_ev])
                            k_sp = m_sp
                        kq_war[bufp] = m_sp
                    else:
                        h = cc
                        m_ev = S.op("copy", vtr_ev[bufp], pp[:],
                                    waits=[m_mm, vtr_war[bufp]])
                        ps_war[ppi] = m_ev
                        cp_marks = [None] * 4
                        for kk in range(4):
                            bank = [ps_b, ps_d][kk % 2]
                            dstps = bank[:, 64 * (kk // 2):
                                         64 * (kk // 2 + 1)].bitcast(BF16)
                            if kk == 0:
                                TE.wait(m_ev)
                                TE.wait(m_ini)
                            TE.wait(cp_marks[kk - 2] if kk >= 2
                                    else tr_bank_war[kk % 2])
                            m_tr = TE.mile("transpose", dstps,
                                           vtr_ev[bufp][:, 128 * kk:128 * (kk + 1)],
                                           id_b[:])
                            cp_marks[kk] = S.op(
                                "copy", vex_ev[bufp][:, 132 * kk:132 * kk + 128],
                                dstps,
                                waits=[m_tr,
                                       vexev_war[bufp] if kk == 0 else None,
                                       m_vex_init if kk == 0 else None])
                        tr_bank_war[0] = cp_marks[2]
                        tr_bank_war[1] = cp_marks[3]
                        vtr_war[bufp] = cp_marks[3]
                        m_sp = Dsc.dma(
                            "sp_v", vext_d.ap()[h, :, 4 * j:4 * (j + 1), :],
                            vex_ev[bufp].rearrange("p (a f) -> p a f", a=4),
                            waits=[cp_marks[3]])
                        vexev_war[bufp] = m_sp
                        v_sp = m_sp
                    if cc % 2 == 1:
                        emit_dq_unit()
                xnT_war[par] = TE.fence()

        m_vex_init = None
        for i2 in range(2):
            V.op("memset", vex_ev[i2], 0.0, waits=[m_ln_v])
            for kk in range(4):
                m_vex_init = V.op("memset",
                                  vex_ev[i2][:, 132 * kk + 128:132 * kk + 129],
                                  1.0)
        qkv_pass(list(range(8)), xn_kv_d, m_wk, "k")
        m_wv = Dgp.dma("ld_w1", wk_sb, wqkv_v.ap(), waits=TE.fence())
        qkv_pass(list(range(8)), xn_kv_d, m_wv, "v")
        m_wq2 = Dgp.dma("ld_w2", wk_sb, wqkv_q.ap(), waits=TE.fence())
        qkv_pass([8, 9], xn_q_d, m_wq2, "q")
        m_wp = Dgp.dma("ld_w3", wk_sb, wproj.ap(), waits=TE.fence())

        def off(mat, sub):
            return (sub // 2 + 2 * mat) * 512 + (sub % 2) * 129

        m_qkv_done = [p.mark("sp_k"), p.mark("sp_v"), p.mark("sp_q"),
                      p.mark("sp_xnqT")]
        msk_sb = v8(AR1, 0, 32768, BF16, 32 * 512)
        m_msk = Dsy.dma("ld_msk", msk_sb, masks.ap(), waits=m_qkv_done)
        m_msk_last = None

        hd_war = [None, None]
        exp_war = {}
        av_war = {}
        ep_copies = []
        att_sp = [None, None]
        first_att = True
        OBASE = [0, 512, 1024, 1536]

        for h in range(NH):
            hb = h % 2
            m_k = Dsy.dma(f"ld_k{hb}", k12[hb], k12T_d.ap()[h],
                          waits=[k_sp, v_sp, q_sp] + _aslist(hd_war[hb]))
            m_v = Dsy.dma(f"ld_v{hb}",
                          vex[hb].rearrange("p (a f) -> p a f", a=32),
                          vext_d.ap()[h])
            m_q = Dsy.dma(f"ld_q{hb}", q12[hb], q12T_d.ap()[h])
            for jt in range(4):
                prev_ep = list(ep_copies)
                ep_copies = []
                msk_marks = {}
                last = None

                def emit_scores(kc):
                    sp = kc % 2
                    pst = [ps_ab, ps_cd][sp]
                    pbuf = p12[(kc % 4) // 2][:, 512 * sp:512 * (sp + 1)]
                    if kc == 0:
                        TE.wait(m_k)
                        TE.wait(m_q)
                        TE.wait(m_v)
                        for mk in prev_ep:
                            TE.wait(mk)
                    if first_att and kc < 2:
                        TE.wait(ps_war[sp])
                        TE.wait(tr_bank_war[0])
                        TE.wait(tr_bank_war[1])
                    TE.wait(exp_war.get(sp))
                    TE.mm(pst[:, 0:256], k12[hb][0:64, 128 * kc:128 * (kc + 1)],
                          q12[hb][0:64, 256 * jt:256 * (jt + 1)],
                          start=True, stop=True)
                    m_s2 = TE.mile("matmul", pst[:, 512:768],
                                   k12[hb][64:128, 128 * kc:128 * (kc + 1)],
                                   q12[hb][64:128, 256 * jt:256 * (jt + 1)],
                                   start=True, stop=True)
                    sview = pst[:].rearrange("p (b c) -> p b c", b=2)[:, :, 0:256]
                    m_exp = S.op("activation",
                                 pbuf.rearrange("p (b c) -> p b c", b=2),
                                 sview, AF.Exp, bias=-12.0, scale=0.125,
                                 waits=[m_s2, av_war.get(kc % 4)])
                    exp_war[sp] = m_exp
                    base = PC[jt] - 8
                    if kc >= base:
                        mslice = msk_sb[:, (jt * 8 + kc - base) * 512:
                                        (jt * 8 + kc - base + 1) * 512]
                        m2 = V.op("tensor_tensor", pbuf, pbuf, mslice,
                                  ALU.mult, noserial=True,
                                  waits=[m_exp, m_msk, av_war.get(kc % 4)])
                        msk_marks[kc] = m2
                    else:
                        msk_marks[kc] = m_exp

                def emit_av(kc):
                    nonlocal last
                    sp = kc % 2
                    pbuf = p12[(kc % 4) // 2][:, 512 * sp:512 * (sp + 1)]
                    TE.wait(msk_marks[kc])
                    for mat in range(2):
                        for sub in range(2):
                            o = OBASE[2 * mat + sub]
                            kw = dict(start=(kc == 0), stop=(kc == PC[jt] - 1))
                            lhsT = pbuf[:, 256 * mat + 128 * sub:
                                        256 * mat + 128 * (sub + 1)]
                            rhs = vex[hb][:, 132 * kc:132 * kc + 129]
                            if (mat, sub) == (1, 1):
                                last = TE.mile("matmul", ps_big[:, o:o + 129],
                                               lhsT, rhs, **kw)
                            else:
                                TE.mm(ps_big[:, o:o + 129], lhsT, rhs, **kw)
                    av_war[kc % 4] = last

                for kc in range(PC[jt] + 1):
                    if kc < PC[jt]:
                        emit_scores(kc)
                    if kc >= 1:
                        emit_av(kc - 1)
                m_msk_last = msk_marks[PC[jt] - 1]
                first_att = False
                m_av_done = last
                eps = []
                for sub in range(2):
                    c0 = 8 * sub
                    o1 = OBASE[sub]
                    o2 = OBASE[2 + sub]
                    t2v = scr[0][:, 128 * sub:128 * (sub + 1)]
                    dfv = scr[1][:, 128 * sub:128 * (sub + 1)]
                    sqv = scr[2][:, 128 * sub:128 * (sub + 1)]
                    onv = (scr[3][:, 128 * sub:128 * (sub + 1)]
                           .bitcast(BF16)[:, 0:128])
                    V.op("reciprocal", ep_st[:, c0:c0 + 1],
                         ps_big[:, o1 + 128:o1 + 129],
                         waits=[m_av_done, m_lam] if sub == 0 else None)
                    V.op("reciprocal", ep_st[:, c0 + 1:c0 + 2],
                         ps_big[:, o2 + 128:o2 + 129])
                    V.op("tensor_tensor", ep_st[:, c0 + 2:c0 + 3],
                         ep_st[:, c0 + 1:c0 + 2],
                         lam_sb[:, 0:1], ALU.mult)
                    V.op("tensor_scalar", t2v,
                         ps_big[:, o2:o2 + 128],
                         ep_st[:, c0 + 2:c0 + 3], None, ALU.mult)
                    m_diff = V.op("scalar_tensor_tensor", dfv,
                                  ps_big[:, o1:o1 + 128],
                                  ep_st[:, c0:c0 + 1], t2v, ALU.mult,
                                  ALU.subtract)
                    S.op("activation", sqv, dfv, AF.Square,
                         accum_out=ep_st[:, c0 + 3:c0 + 4], waits=[m_diff])
                    S.op("activation", ep_st[:, c0 + 4:c0 + 5],
                         ep_st[:, c0 + 3:c0 + 4], AF.Ln,
                         bias=EPS, scale=1.0 / VD)
                    m_rs = S.op("activation", ep_st[:, c0 + 5:c0 + 6],
                                ep_st[:, c0 + 4:c0 + 5],
                                AF.Exp, bias=math.log(ONE_M_LI), scale=-0.5)
                    eps.append((m_rs, dfv, onv, c0))
                for sub in range(2):
                    m_rs, dfv, onv, c0 = eps[sub]
                    stp = sub % 2
                    m_on = V.op("tensor_scalar", onv, dfv,
                                ep_st[:, c0 + 5:c0 + 6], None, ALU.mult,
                                waits=[m_rs])
                    bank = [ps_b, ps_d][stp]
                    dstps = bank[:, 384:448].bitcast(BF16)
                    TE.wait(m_on)
                    TE.wait(tr_bank_war[stp])
                    m_tr = TE.mile("transpose", dstps, onv, id_b[:])
                    m_cp = V.op("tensor_copy", att_st[stp][:], dstps,
                                waits=[m_tr, att_sp[stp]])
                    qs = 2 * jt + sub
                    att_sp[stp] = Dsc.dma("sp_att",
                                          attnT_d.ap()[qs, :, h, :],
                                          att_st[stp][:], waits=[m_cp])
                    ep_copies.append(m_cp)
                    tr_bank_war[stp] = m_cp
            hd_war[hb] = TE.fence()
            for _ in range(16):
                emit_dq_unit()
        while dq_state["s2"] < len(dq_units):
            emit_dq_unit()

        m_att_all = p.mark("sp_att")
        aq_war = [None, None]
        sev_war = None
        m_sp_s = None
        prev_guard = list(ep_copies)
        hd_fence = TE.fence()
        for qs in range(8):
            bufp = qs % 2
            m_aq = Dsy.dma(f"ld_aq{bufp}",
                           att_qs[bufp].rearrange("p (a f) -> p a f", a=16),
                           attnT_d.ap()[qs],
                           waits=[m_att_all] + (hd_fence if qs < 2 else [])
                                 + _aslist(aq_war[bufp]))
            TE.wait(m_aq)
            if qs == 0:
                TE.wait(m_wp)
            for mk in prev_guard:
                TE.wait(mk)
            m_pj = None
            for h in range(NH):
                for c4 in range(4):
                    kw = dict(start=(h == 0), stop=(h == 15))
                    lhsT = att_qs[bufp][:, 128 * h:128 * (h + 1)]
                    rhs = wk_sb[:, 2048 * h + 512 * c4:2048 * h + 512 * (c4 + 1)]
                    if (h, c4) == (15, 3):
                        m_pj = TE.mile("matmul",
                                       ps_big[:, 512 * c4:512 * (c4 + 1)],
                                       lhsT, rhs, **kw)
                    else:
                        TE.mm(ps_big[:, 512 * c4:512 * (c4 + 1)], lhsT, rhs,
                              **kw)
            aq_war[bufp] = TE.fence()
            m_ev = V.op("tensor_copy", s_ev[0], ps_big[:],
                        waits=[m_pj, sev_war])
            m_sp_s = Dsc.dma("sp_s", s_d.ap()[128 * qs:128 * (qs + 1), :],
                             s_ev[0], waits=[m_ev])
            sev_war = m_sp_s
            prev_guard = [m_ev]
        m_proj_ev = prev_guard[0]

        m_dq_all = [p.mark("sp_dq0a"), p.mark("sp_dq0b"),
                    p.mark("sp_dq1a"), p.mark("sp_dq1b")]
        gu_ps = [(ps_a, ps_b), (ps_c, ps_d)]
        wg_war = [None, None]
        wd_war = [None, None]
        gu_psum_war = {0: None, 1: None}
        sil_war = None
        fin_v_war = None
        out_war = None
        m_xnq_guard = None
        for tb in range(4):
            for g in range(16):
                J2, par2 = divmod(g, 2)
                src_ = xnqT_d.ap()[256 * J2 + par2:256 * J2 + 256:2,
                                   256 * tb:256 * (tb + 1)]
                w0 = []
                if g == 0:
                    if tb == 0:
                        w0 = [p.mark("sp_xnqT"), p.mark("sp_s")]
                    else:
                        w0 = _aslist(m_xnq_guard)
                Dsy.dma("ld_xnq", xnq_sb[:, 256 * g:256 * (g + 1)], src_,
                        waits=w0)
            m_xnq = p.mark("ld_xnq")
            for ng in range(32):
                wpar = ng % 2
                wl = (m_dq_all + [m_msk_last]) if (tb == 0 and ng < 2) else []
                wl = wl + _aslist(wg_war[wpar])
                if tb == 0 and ng < 2:
                    wl = wl + [m_proj_ev]
                pairs_g = [(wg_grp[wpar][:, 512 * g:512 * (g + 1)],
                            wgT_d.ap()[g, :, 512 * ng:512 * (ng + 1)])
                           for g in range(16)]
                pairs_u = [(wu_grp[wpar][:, 512 * g:512 * (g + 1)],
                            wuT_d.ap()[g, :, 512 * ng:512 * (ng + 1)])
                           for g in range(16)]
                m_wg = Dsy.dma_group(f"ld_wg{wpar}", pairs_g, waits=wl)
                m_wu = Dsy.dma_group(f"ld_wu{wpar}", pairs_u)
                TE.wait(m_xnq)
                TE.wait(m_wg)
                TE.wait(m_wu)
                for k in range(4):
                    nch = 4 * ng + k
                    pg, pu = gu_ps[nch % 2]
                    TE.wait(gu_psum_war[nch % 2])
                    if tb == 0 and ng == 0 and k < 2:
                        TE.wait(m_proj_ev)
                    if tb > 0 and ng == 0 and k < 2:
                        TE.wait(sil_war)
                    m_g = m_u2 = None
                    for g in range(16):
                        kw = dict(start=(g == 0), stop=(g == 15))
                        lhsT = wg_grp[wpar][:, 512 * g + 128 * k:
                                            512 * g + 128 * (k + 1)]
                        rhs = xnq_sb[:, 256 * g:256 * (g + 1)]
                        if g == 15:
                            m_g = TE.mile("matmul", pg[:, 0:256], lhsT, rhs, **kw)
                        else:
                            TE.mm(pg[:, 0:256], lhsT, rhs, **kw)
                    for g in range(16):
                        kw = dict(start=(g == 0), stop=(g == 15))
                        lhsT = wu_grp[wpar][:, 512 * g + 128 * k:
                                            512 * g + 128 * (k + 1)]
                        rhs = xnq_sb[:, 256 * g:256 * (g + 1)]
                        if g == 15:
                            m_u2 = TE.mile("matmul", pu[:, 0:256], lhsT, rhs, **kw)
                        else:
                            TE.mm(pu[:, 0:256], lhsT, rhs, **kw)
                    m_sq2 = S.op("activation", gl_sq, pg[:, 0:256],
                                 AF.Square, waits=[m_g])
                    V.op("tensor_scalar", gl_v2, gl_sq, GC3, GC1,
                         ALU.mult, ALU.add, waits=[m_sq2])
                    m_v3 = V.op("tensor_tensor", gl_v3, gl_v2,
                                pg[:, 0:256], ALU.mult)
                    m_t = S.op("activation", gl_t, gl_v3, AF.Tanh,
                               waits=[m_v3])
                    m_up = S.op("activation", gl_up, pu[:, 0:256],
                                AF.Identity, scale=0.5, waits=[m_u2])
                    V.op("tensor_tensor", gl_v2, pg[:, 0:256],
                         gl_up, ALU.mult, waits=[m_t, m_up])
                    m_int = V.op("scalar_tensor_tensor",
                                 inter[:, 256 * nch:256 * (nch + 1)],
                                 gl_t, 1.0, gl_v2, ALU.add, ALU.mult)
                    gu_psum_war[nch % 2] = m_int
                wg_war[wpar] = TE.fence()
            m_int_all = V.mark()
            m_xnq_guard = TE.fence()
            last_d = None
            for nch in range(128):
                wpar = nch % 2
                m_wd = Dsy.dma(f"ld_wd{wpar}", wd_st[wpar],
                               wdT_d.ap()[nch], waits=_aslist(wd_war[wpar]))
                TE.wait(m_wd)
                if nch == 0:
                    TE.wait(m_int_all)
                    TE.wait(sil_war)
                for tsub in range(2):
                    for c4 in range(4):
                        kw = dict(start=(nch == 0), stop=(nch == 127))
                        lhsT = inter[:, 256 * nch + 128 * tsub:
                                     256 * nch + 128 * (tsub + 1)]
                        rhs = wd_st[wpar][:, 512 * c4:512 * (c4 + 1)]
                        dst = (ps_big[:, 512 * c4:512 * (c4 + 1)] if tsub == 0
                               else ps4[c4][:])
                        if (tsub, c4) == (1, 3):
                            last_d = TE.mile("matmul", dst, lhsT, rhs, **kw)
                        else:
                            TE.mm(dst, lhsT, rhs, **kw)
                wd_war[wpar] = TE.fence()
            m_down = last_d
            for tsub in range(2):
                r0 = 256 * tb + 128 * tsub
                m_fx = Dsy.dma("ld_fx", fin_x, x_q.ap()[r0:r0 + 128, :],
                               waits=[fin_v_war,
                                      m_down if (tb == 0 and tsub == 0) else None])
                m_fs = Dsy.dma("ld_fs", fin_s, s_d.ap()[r0:r0 + 128, :],
                               waits=[m_sp_s] if (tb == 0 and tsub == 0) else ())
                if tsub == 0:
                    m_sil = S.op("activation", fin_sil, ps_big[:], AF.Silu,
                                 waits=[m_down])
                else:
                    for c4 in range(4):
                        m_sil = S.op("activation",
                                     fin_sil[:, 512 * c4:512 * (c4 + 1)],
                                     ps4[c4][:], AF.Silu,
                                     waits=[m_down] if c4 == 0 else ())
                V.op("tensor_tensor", fin_o, fin_sil, fin_s,
                     ALU.mult, waits=[m_sil, m_fs, m_fx, out_war])
                m_fin = V.op("tensor_tensor", fin_o, fin_o, fin_x, ALU.add)
                fin_v_war = m_fin
                out_war = Dsc.dma("sp_out", out.ap()[r0:r0 + 128, :],
                                  fin_o, waits=[m_fin])
            sil_war = m_sil
            gu_psum_war = {0: sil_war, 1: sil_war}

        p.wait("sync", p.mark("sp_out"))

        with nc.Block() as block:
            @block.tensor
            def _(eng):
                for f in p.body["tensor"]:
                    f(eng)

            @block.vector
            def _(eng):
                for f in p.body["vector"]:
                    f(eng)

            @block.scalar
            def _(eng):
                for f in p.body["scalar"]:
                    f(eng)

            @block.gpsimd
            def _(eng):
                for f in p.body["gpsimd"]:
                    f(eng)

            @block.sync
            def _(eng):
                for f in p.body["sync"]:
                    f(eng)

        res = run_bass_kernel_spmd(nc, in_maps, core_ids=list(range(N_CORES)),
                                   trace=trace)
    return res


def kernel(**inputs):
    in_maps = _host_inputs(inputs)
    res = build_and_run(in_maps)
    full = np.empty((2, T, D), np.float32)
    for c in range(N_CORES):
        b, q = divmod(c, 4)
        oc = np.asarray(res.results[c]["out"])
        for jt, t in enumerate(TILES[q]):
            full[b, 256 * t:256 * (t + 1)] = oc[256 * jt:256 * (jt + 1)]
    return full


# version 11
# speedup vs baseline: 1.1286x; 1.1286x over previous
import math
import numpy as np
import concourse.bass as bass
import concourse.mybir as mybir
from concourse.bass_utils import run_bass_kernel_spmd

F32 = mybir.dt.float32
BF16 = mybir.dt.bfloat16
U8 = mybir.dt.uint8
I16 = mybir.dt.int16
AF = mybir.ActivationFunctionType
ALU = mybir.AluOpType
AX = mybir.AxisListType

D = 2048
T = 4096
NQ = 1024
NH = 16
VD = 128
NF = 16384
LAMBDA_INIT = 0.8 - 0.6 * math.exp(-0.3)
ONE_M_LI = 1.0 - LAMBDA_INIT
GC1 = 0.7978845608028654
GC3 = 0.7978845608028654 * 0.044715
EPS = 1e-5

CONST_VALS = [0.0, 1.0, 0.125, -12.0, 0.5, -0.25, EPS, 1.0 / D, 1.0 / VD,
              -0.5, math.log(ONE_M_LI), GC1, GC3, LAMBDA_INIT, 0.0625,
              -0.499999, 63.5, -1.0 / 256.0]

N_CORES = 8
TILES = [[c4, 4 + c4, 11 - c4, 15 - c4] for c4 in range(4)]
PC = [8, 16, 24, 32]
MOFF = [0, 8, 24, 48]
NCH_TOT = 80



def _nperm():
    p = []
    for b in range(NF // 256):
        p.extend(range(256 * b, 256 * (b + 1), 2))
        p.extend(range(256 * b + 1, 256 * (b + 1), 2))
    return np.array(p)


def _pack_u8(a):
    a = np.ascontiguousarray(np.asarray(a, np.int32))
    return np.ascontiguousarray(a.view(np.uint8).reshape(*a.shape, 4)[..., 0])


def _chunkmajor(w):
    Dr, C = w.shape
    return np.ascontiguousarray(w.reshape(Dr // 128, 128, C).transpose(1, 0, 2)
                                .reshape(128, (Dr // 128) * C))


def _host_inputs(inputs):
    x = np.asarray(inputs["L0_attn_x"], np.float32).reshape(2, T, D)
    wqkv = np.asarray(inputs["L0_attn_Wqkv"], np.float32)
    wproj = np.asarray(inputs["L0_attn_Wproj"], np.float32)

    kcols, vcols, qcols = [], [], []
    for h in range(NH):
        kcols.extend(range(1024 + 64 * h, 1024 + 64 * h + 64))
        kcols.extend(range(3072 + 64 * h, 3072 + 64 * h + 64))
    for h in range(NH):
        vcols.extend(range(4096 + 128 * h, 4096 + 128 * h + 128))
    for h in range(NH):
        qcols.extend(range(64 * h, 64 * h + 64))
        qcols.extend(range(2048 + 64 * h, 2048 + 64 * h + 64))
    wqkv_k = _chunkmajor(np.ascontiguousarray(wqkv[:, kcols]))
    wqkv_v = _chunkmajor(np.ascontiguousarray(wqkv[:, vcols]))
    wqkv_q = _chunkmajor(np.ascontiguousarray(wqkv[:, qcols]))
    wproj_cm = _chunkmajor(np.ascontiguousarray(wproj))

    lamv = np.zeros((128, 256), np.float32)
    lamv[:, 0:64] = np.asarray(inputs["L0_attn_lambda_q1"], np.float32)
    lamv[:, 64:128] = np.asarray(inputs["L0_attn_lambda_q2"], np.float32)
    lamv[:, 128:192] = np.asarray(inputs["L0_attn_lambda_k1"], np.float32)
    lamv[:, 192:256] = np.asarray(inputs["L0_attn_lambda_k2"], np.float32)

    import ml_dtypes
    consts = np.zeros((128, 18), np.float32)
    consts[:, :len(CONST_VALS)] = CONST_VALS
    ident = np.eye(128, dtype=ml_dtypes.bfloat16)

    perm = _nperm()
    gate_pk = np.ascontiguousarray(_pack_u8(inputs["L0_ffn_gate_W_packed"])[perm, :].T)
    up_pk = np.ascontiguousarray(_pack_u8(inputs["L0_ffn_up_W_packed"])[perm, :].T)
    down_pk = np.ascontiguousarray(_pack_u8(inputs["L0_ffn_down_W_packed"]).T)
    gsc = np.ascontiguousarray(np.asarray(inputs["L0_ffn_gate_scales"], np.float32)[perm, :].T)
    usc = np.ascontiguousarray(np.asarray(inputs["L0_ffn_up_scales"], np.float32)[perm, :].T)
    dsc = np.ascontiguousarray(np.asarray(inputs["L0_ffn_down_scales"], np.float32).T)

    shared = {
        "wqkv_k": wqkv_k, "wqkv_v": wqkv_v, "wqkv_q": wqkv_q, "wproj": wproj_cm,
        "lamv": lamv, "consts": consts, "ident": ident,
        "gate_pk": gate_pk, "up_pk": up_pk, "down_pk": down_pk,
        "gsc": gsc, "usc": usc, "dsc": dsc,
    }

    kt = np.arange(128)
    qt = np.arange(256)
    in_maps = []
    for c in range(N_CORES):
        b, c4 = divmod(c, 4)
        tiles = TILES[c4]
        m = np.zeros((32, 128, 512), ml_dtypes.bfloat16)
        for jt in range(4):
            for i8 in range(8):
                kc = PC[jt] - 8 + i8
                m1 = ((128 * kc + kt[:, None])
                      <= (256 * tiles[jt] + qt[None, :256])).astype(np.float32)
                m[jt * 8 + i8] = np.concatenate([m1, m1], 1)
        d = dict(shared)
        d["x_kv"] = np.ascontiguousarray(x[b])
        d["x_q"] = np.ascontiguousarray(
            np.concatenate([x[b, 256 * t:256 * (t + 1)] for t in tiles], 0))
        d["masks"] = np.ascontiguousarray(m.transpose(1, 0, 2).reshape(128, 32 * 512))
        in_maps.append(d)
    return in_maps



def _aslist(w):
    if w is None:
        return []
    if isinstance(w, tuple) and len(w) == 2 and isinstance(w[0], str):
        return [w]
    out = []
    for m in w:
        out.extend(_aslist(m))
    return out


class Prog:
    ENGINES = ("tensor", "vector", "scalar", "gpsimd", "sync")

    def __init__(self, nc, ctx):
        self.nc = nc
        self.ctx = ctx
        self.body = {e: [] for e in self.ENGINES}
        self.sems = {}
        self.waited = {e: {} for e in self.ENGINES}

    def sem(self, name):
        if name not in self.sems:
            h = self.ctx.enter_context(self.nc.semaphore(name))
            self.sems[name] = [h, 0]
        return self.sems[name]

    def wait(self, eng, mark):
        for m in _aslist(mark):
            if m[1] <= 0 or self.waited[eng].get(m[0], 0) >= m[1]:
                continue
            self.waited[eng][m[0]] = m[1]
            h = self.sems[m[0]][0]
            self.body[eng].append(lambda e, h=h, v=m[1]: e.wait_ge(h, v))

    def op(self, eng, fname, *a, inc=None, **k):
        if inc is not None:
            s = self.sem(inc[0])
            s[1] += inc[1]
            h, val = s[0], s[1]

            def run(e, fname=fname, a=a, k=k, h=h, amt=inc[1]):
                getattr(e, fname)(*a, **k).then_inc(h, amt)
            self.body[eng].append(run)
            return (inc[0], val)

        def run(e, fname=fname, a=a, k=k):
            getattr(e, fname)(*a, **k)
        self.body[eng].append(run)
        return None

    def mark(self, name):
        return (name, self.sems[name][1]) if name in self.sems else (name, 0)


SELF_SERIAL = True


class SerialEngine:
    def __init__(self, prog, eng, semname, serial=True):
        self.p, self.eng, self.semname = prog, eng, semname
        self.serial = serial
        self.p.sem(semname)

    def __init2__(self):
        pass

    def op(self, fname, *a, waits=(), noserial=False, **k):
        self.p.wait(self.eng, waits)
        if self.serial and not noserial:
            n = self.p.sems[self.semname][1]
            if n > 0:
                self.p.wait(self.eng, (self.semname, n))
        return self.p.op(self.eng, fname, *a, inc=(self.semname, 1), **k)

    def mark(self):
        return self.p.mark(self.semname)


class DmaQueue:
    def __init__(self, prog, eng):
        self.p, self.eng = prog, eng

    def dma(self, semname, out, in_, waits=(), transpose=False):
        self.p.wait(self.eng, waits)
        n = self.p.sems[semname][1] if semname in self.p.sems else 0
        if n > 0:
            self.p.wait(self.eng, (semname, n))
        if transpose:
            return self.p.op(self.eng, "dma_start_transpose", out, in_,
                             inc=(semname, 16))
        return self.p.op(self.eng, "dma_start", out=out, in_=in_,
                         inc=(semname, 16))

    def dma_group(self, semname, pairs, waits=()):
        self.p.wait(self.eng, waits)
        for out, in_ in pairs:
            self.p.op(self.eng, "dma_start", out=out, in_=in_, inc=(semname, 16))
        return self.p.mark(semname)


class TensorEng:
    def __init__(self, prog):
        self.p = prog
        self.rot = 0
        self.p.sem("st0")
        self.p.sem("st1")

    def wait(self, mark):
        self.p.wait("tensor", mark)

    def mm(self, *a, **k):
        self.p.op("tensor", "matmul", *a, **k)

    def mile(self, fname, *a, **k):
        name = f"st{self.rot}"
        self.rot ^= 1
        n = self.p.sems[name][1]
        if n > 0:
            self.p.wait("tensor", (name, n))
        return self.p.op("tensor", fname, *a, inc=(name, 1), **k)

    def fence(self):
        return [self.p.mark("st0"), self.p.mark("st1")]



def build_and_run(in_maps, trace=False):
    from contextlib import ExitStack
    nc = bass.Bass("TRN2", target_bir_lowering=False)

    di = lambda n, s, dt: nc.dram_tensor(n, s, dt, kind="ExternalInput")
    dd = lambda n, s, dt: nc.dram_tensor(n, s, dt, kind="Internal")
    x_kv = di("x_kv", [T, D], F32)
    x_q = di("x_q", [NQ, D], F32)
    wqkv_k = di("wqkv_k", [128, 16 * 2048], F32)
    wqkv_v = di("wqkv_v", [128, 16 * 2048], F32)
    wqkv_q = di("wqkv_q", [128, 16 * 2048], F32)
    wproj = di("wproj", [128, 16 * 2048], F32)
    lamv = di("lamv", [128, 256], F32)
    consts = di("consts", [128, 18], F32)
    ident = di("ident", [128, 128], BF16)
    masks = di("masks", [128, 32 * 512], BF16)
    gate_pk = di("gate_pk", [1024, NF], U8)
    up_pk = di("up_pk", [1024, NF], U8)
    down_pk = di("down_pk", [8192, D], U8)
    gsc = di("gsc", [64, NF], F32)
    usc = di("usc", [64, NF], F32)
    dsc = di("dsc", [512, D], F32)
    out = nc.dram_tensor("out", [NQ, D], F32, kind="ExternalOutput")

    xn_kv_d = dd("xn_kv_d", [T, D], BF16)
    xn_q_d = dd("xn_q_d", [NQ, D], BF16)
    xnqT_d = dd("xnqT_d", [D, NQ], BF16)
    k12T_d = dd("k12T_d", [NH, 128, T], BF16)
    vext_d = dd("vext_d", [NH, 128, 32, 132], BF16)
    q12T_d = dd("q12T_d", [NH, 128, NQ], BF16)
    s_d = dd("s_d", [NQ, D], F32)
    attnT_d = dd("attnT_d", [8, 128, NH, 128], BF16)
    wgT_d = dd("wgT_d", [16, 128, NF], BF16)
    wuT_d = dd("wuT_d", [16, 128, NF], BF16)
    wdT_d = dd("wdT_d", [128, 128, D], BF16)

    ctx = ExitStack()
    with ctx:
        sb = lambda n, s, dt: ctx.enter_context(nc.sbuf_tensor(n, s, dt))
        psal = lambda n, s, dt: ctx.enter_context(nc.psum_tensor(n, s, dt))

        AR0 = sb("AR0", [128, 65536], U8)
        AR1 = sb("AR1", [128, 65536], U8)
        AR2 = sb("AR2", [128, 49152], U8)
        AR3 = sb("AR3", [128, 22528], U8)

        def v8(ar, off, nbytes, dt, cols):
            return ar[:, off:off + nbytes].bitcast(dt)[:, 0:cols]

        wk_sb = v8(AR0, 0, 65536, BF16, 16 * 2048)
        wv_sb = wk_sb
        wq_sb = wk_sb
        wp_sb = wk_sb
        inter = v8(AR0, 0, 65536, BF16, 128 * 256)

        xt = [v8(AR1, 0, 8192, F32, D), v8(AR1, 8192, 8192, F32, D)]
        lnsq = v8(AR1, 16384, 8192, F32, D)
        xnt = [v8(AR1, 24576, 4096, BF16, D), v8(AR1, 28672, 4096, BF16, D)]
        xnT_sb = [v8(AR2, 0, 16384, BF16, 16 * 512),
                  v8(AR2, 16384, 16384, BF16, 16 * 512)]
        qkv_ev = [v8(AR1, 0, 1024, BF16, 512), v8(AR1, 1024, 1024, BF16, 512)]
        vtr_ev = [v8(AR1, 2048, 1024, BF16, 512), v8(AR1, 3072, 1024, BF16, 512)]
        vex_ev = [v8(AR1, 4096, 1088, BF16, 4 * 132), v8(AR1, 5248, 1088, BF16, 4 * 132)]
        wg_grp = [v8(AR1, 0, 16384, BF16, 16 * 512),
                  v8(AR1, 16384, 16384, BF16, 16 * 512)]
        wu_grp = [v8(AR1, 32768, 16384, BF16, 16 * 512),
                  v8(AR1, 49152, 16384, BF16, 16 * 512)]

        k12 = [v8(AR2, 0, 8192, BF16, T), v8(AR2, 8192, 8192, BF16, T)]
        vex = [v8(AR2, 16384, 8448, BF16, 32 * 132), v8(AR2, 24832, 8448, BF16, 32 * 132)]
        q12 = [v8(AR2, 33280, 2048, BF16, NQ), v8(AR2, 35328, 2048, BF16, NQ)]
        att_qs = [v8(AR2, 0, 4096, BF16, D), v8(AR2, 4096, 4096, BF16, D)]
        s_ev = [v8(AR2, 8192, 8192, F32, D)]
        fin_x = v8(AR2, 0, 8192, F32, D)
        fin_s = v8(AR2, 8192, 8192, F32, D)
        fin_sil = v8(AR2, 16384, 8192, F32, D)
        fin_o = v8(AR2, 24576, 8192, F32, D)
        xnq_sb = v8(AR2, 32768, 8192, BF16, 16 * 256)
        wd_st = [v8(AR2, 40960, 4096, BF16, 2048), v8(AR2, 45056, 4096, BF16, 2048)]

        DQW = 1024

        def dq_lane(specs):
            (ar_pk, o_pk), (ar_m, o_m) = specs
            return dict(
                pk=v8(ar_pk, o_pk, 1024, U8, DQW),
                c4=v8(ar_m, o_m, 4096, I16, 2 * DQW),
                s15=v8(ar_m, o_m + 4096, 4096, I16, 2 * DQW),
                xb=v8(ar_m, o_m + 8192, 4096, I16, 2 * DQW),
                r=v8(ar_m, o_m + 12288, 4096, BF16, 2 * DQW),
                scb2=v8(ar_m, o_m + 16384, 4096, BF16, 2 * DQW),
            )
        dq_lanes = [dq_lane([(AR2, 37888), (AR3, 0)]),
                    dq_lane([(AR1, 32768 + 5120), (AR1, 32768 + 10240)])]

        cst = sb("cst", [128, 18], F32)
        for i, v in enumerate(CONST_VALS):
            nc.const_aps.register(F32, v, cst[:, i:i + 1])
        lam_sb = sb("lam_sb", [128, 8], F32)
        lamv_sb = v8(AR1, 16384, 1024, F32, 256)
        lamx = v8(AR1, 20480, 1024, F32, 256)
        id_b = sb("id_b", [128, 128], BF16)
        lnst = sb("lnst", [128, 8], F32)
        ep_st = sb("ep_st", [128, 16], F32)
        p12 = [sb(f"p12_{i}", [128, 1024], BF16) for i in range(2)]
        scr = [sb(f"scr{i}", [128, 256], F32) for i in range(4)]
        ep_t2 = scr[0][:, 0:128]
        ep_diff = scr[1][:, 0:128]
        ep_sq = scr[2][:, 0:128]
        ep_on = scr[3][:, 0:128].bitcast(BF16)[:, 0:128]
        gl_sq, gl_v2, gl_v3, gl_t = (s[:] for s in scr)
        gl_up = scr[2][:]
        att_st = [sb(f"att_st{i}", [128, 128], BF16) for i in range(2)]

        ps_ab = psal("ps_ab", [128, 1024], F32)
        ps_cd = psal("ps_cd", [128, 1024], F32)
        ps_big = psal("ps_big", [128, 2048], F32)

        class _PV:
            def __init__(self, t, lo):
                self.t, self.lo = t, lo
            def __getitem__(self, key):
                assert isinstance(key, tuple) or key == slice(None)
                if not isinstance(key, tuple):
                    return self.t[:, self.lo:self.lo + 512]
                p, f = key
                assert p == slice(None)
                return self.t[:, self.lo + (f.start or 0):self.lo + f.stop]
        ps_a = _PV(ps_ab, 0)
        ps_b = _PV(ps_ab, 512)
        ps_c = _PV(ps_cd, 0)
        ps_d = _PV(ps_cd, 512)
        ps4 = [ps_a, ps_b, ps_c, ps_d]

        p = Prog(nc, ctx)
        V = SerialEngine(p, "vector", "sv")
        S = SerialEngine(p, "scalar", "ss", serial=False)
        G = SerialEngine(p, "gpsimd", "sg", serial=False)
        TE = TensorEng(p)
        Dsy = DmaQueue(p, "sync")
        Dsc = DmaQueue(p, "scalar")
        Dgp = DmaQueue(p, "gpsimd")

        m_ini = Dsy.dma_group("ld_ini", [
            (cst[:], consts.ap()),
            (id_b[:], ident.ap()),
            (lamv_sb, lamv.ap()),
        ])
        m_wk = Dgp.dma("ld_w0", wk_sb, wqkv_k.ap())

        S.op("copy", lam_sb[:, 1:2], cst[:, 0:1], waits=[m_ini])

        V.op("tensor_tensor", lamx[:, 0:128], lamv_sb[:, 0:128],
             lamv_sb[:, 128:256], ALU.mult, waits=[m_ini])
        V.op("reduce_sum", lam_sb[:, 2:3], lamx[:, 0:64], axis=AX.X)
        m_d12 = V.op("reduce_sum", lam_sb[:, 3:4], lamx[:, 64:128], axis=AX.X)
        S.op("activation", lam_sb[:, 4:5], lam_sb[:, 2:3], AF.Exp,
             waits=[m_d12])
        m_e2 = S.op("activation", lam_sb[:, 5:6], lam_sb[:, 3:4], AF.Exp)
        V.op("tensor_tensor", lam_sb[:, 6:7], lam_sb[:, 4:5],
             lam_sb[:, 5:6], ALU.subtract, waits=[m_e2])
        m_lam = V.op("tensor_scalar", lam_sb[:, 0:1], lam_sb[:, 6:7],
                     LAMBDA_INIT, None, ALU.add)

        sp_xn_mark = [None, None]
        ln_v_mark = [None, None]
        for i in range(40):
            par = i % 2
            if i < 32:
                src_ = x_kv.ap()[128 * i:128 * (i + 1), :]
                dst = xn_kv_d.ap()[128 * i:128 * (i + 1), :]
            else:
                j = i - 32
                src_ = x_q.ap()[128 * j:128 * (j + 1), :]
                dst = xn_q_d.ap()[128 * j:128 * (j + 1), :]
            m_ld = Dsy.dma(f"ld_x{par}", xt[par], src_,
                           waits=[ln_v_mark[par], m_lam if i == 0 else None])
            V.op("reduce_sum", lnst[:, 0:1], xt[par], axis=AX.X,
                 waits=[m_ld])
            m_sq = S.op("activation", lnsq, xt[par], AF.Square,
                        accum_out=lnst[:, 1:2], waits=[m_ld, V.mark()])
            V.op("tensor_scalar", lnst[:, 2:3], lnst[:, 0:1], 1.0 / D, None,
                 ALU.mult, waits=[m_sq])
            V.op("tensor_tensor", lnst[:, 3:4], lnst[:, 2:3], lnst[:, 2:3],
                 ALU.mult)
            V.op("tensor_scalar", lnst[:, 4:5], lnst[:, 1:2], 1.0 / D, EPS,
                 ALU.mult, ALU.add)
            m_var = V.op("tensor_tensor", lnst[:, 5:6], lnst[:, 4:5],
                         lnst[:, 3:4], ALU.subtract)
            S.op("activation", lnst[:, 6:7], lnst[:, 5:6], AF.Ln,
                 waits=[m_var])
            m_rstd = S.op("activation", lnst[:, 7:8], lnst[:, 6:7], AF.Exp,
                          scale=-0.5)
            m_xn = V.op("tensor_scalar", xnt[par], xt[par],
                        lnst[:, 2:3], lnst[:, 7:8], ALU.subtract, ALU.mult,
                        waits=[m_rstd, sp_xn_mark[par]])
            ln_v_mark[par] = m_xn
            sp_xn_mark[par] = Dsc.dma(f"sp_xn{par}", dst, xnt[par],
                                      waits=[m_xn])
        m_ln_done = [sp_xn_mark[0], sp_xn_mark[1]]
        m_ln_v = V.mark()

        dq_units = []
        for mat in range(3):
            if mat < 2:
                dq_units += [(mat, J, cc) for J in range(8) for cc in range(16)]
            else:
                dq_units += [(2, J, cc) for J in range(64) for cc in range(2)]
        dq_state = {"s1": 0, "s2": 0, "m": [dict(), dict()]}

        def dq_stage1(i):
            mat, J, cc = dq_units[i]
            L = i % 2
            t = dq_lanes[L]
            pm = dq_state["m"][L]
            W = DQW
            pk_t, sc_t, w_d = [(gate_pk, gsc, wgT_d), (up_pk, usc, wuT_d),
                               (down_pk, dsc, wdT_d)][mat]
            Dpk = [Dsy, Dsc][L]
            m_pk = Dpk.dma(f"ld_pk{L}", t["pk"],
                           pk_t.ap()[128 * J:128 * (J + 1), cc * W:(cc + 1) * W],
                           waits=[pm.get("chi"), pm.get("clo")])
            src_sc = (sc_t.ap()[8 * J:8 * (J + 1), cc * W:(cc + 1) * W]
                      .unsqueeze(1).broadcast_to([8, 16, W]))
            Dgp.dma(f"ld_sc{L}", t["scb2"][:, 0:W], src_sc,
                    waits=[pm.get("o")])
            m_sc = Dgp.dma(f"ld_sc{L}", t["scb2"][:, W:2 * W], src_sc)
            m_chi = S.op("activation", t["c4"][:, W:2 * W], t["pk"],
                         AF.Identity, scale=0.0625, bias=-0.499999,
                         waits=[m_pk, pm.get("xb")])
            m_clo = V.op("scalar_tensor_tensor", t["c4"][:, 0:W],
                         t["c4"][:, W:2 * W], -16.0, t["pk"],
                         ALU.mult, ALU.add, noserial=True,
                         waits=[m_chi, pm.get("xb")])
            m_s15 = V.op("tensor_scalar", t["s15"], t["c4"], 3, 15,
                         ALU.logical_shift_right, ALU.logical_shift_left,
                         noserial=True, waits=[m_clo, pm.get("orr")])
            m_u = V.op("tensor_single_scalar", t["c4"], t["c4"], 7,
                       ALU.bitwise_and, noserial=True, waits=[m_s15])
            m_xb = V.op("tensor_scalar", t["xb"], t["c4"], 64, 16128,
                        ALU.mult, ALU.add, noserial=True,
                        waits=[m_u] + _aslist(pm.get("sp")))
            pm.update(pk=m_pk, sc=m_sc, chi=m_chi, clo=m_clo, s15=m_s15,
                      xb=m_xb)

        def dq_stage2(i):
            mat, J, cc = dq_units[i]
            L = i % 2
            t = dq_lanes[L]
            pm = dq_state["m"][L]
            W = DQW
            w_d = [wgT_d, wuT_d, wdT_d][mat]
            Dsp = [Dsc, Dsy][L]
            m_r = S.op("activation", t["r"], t["xb"], AF.Relu, bias=63.5,
                       scale=-1.0 / 256.0, waits=[pm["xb"], pm.get("val")])
            xb_bf = t["xb"].bitcast(BF16)[:, 0:2 * W]
            m_val = V.op("tensor_tensor", xb_bf, xb_bf, t["r"], ALU.subtract,
                         noserial=True, waits=[m_r])
            m_or = V.op("tensor_tensor", t["xb"], t["xb"], t["s15"],
                        ALU.bitwise_or, noserial=True, waits=[m_val])
            m_o = V.op("tensor_tensor", xb_bf, xb_bf, t["scb2"], ALU.mult,
                       noserial=True, waits=[m_or, pm["sc"]])
            m_spa = Dsp.dma(f"sp_dq{L}a",
                            w_d.ap()[2 * J, :, cc * W:(cc + 1) * W],
                            xb_bf[:, 0:W], waits=[m_o])
            m_spb = Dsp.dma(f"sp_dq{L}b",
                            w_d.ap()[2 * J + 1, :, cc * W:(cc + 1) * W],
                            xb_bf[:, W:2 * W])
            pm.update(r=m_r, val=m_val, orr=m_or, o=m_o, sp=[m_spa, m_spb])

        def emit_dq_unit():
            if dq_state["s1"] < len(dq_units):
                dq_stage1(dq_state["s1"])
                dq_state["s1"] += 1
            lim = dq_state["s1"] - (1 if dq_state["s1"] < len(dq_units) else 0)
            if dq_state["s2"] < lim:
                dq_stage2(dq_state["s2"])
                dq_state["s2"] += 1

        qkv_ps = [ps_a, ps_c]
        ps_war = [None, None]
        tr_bank_war = [None, None]
        xnT_war = [None, None]
        kq_war = [None, None]
        vtr_war = [None, None]
        vexev_war = [None, None]
        k_sp = v_sp = q_sp = None
        m_vex_init = None

        def qkv_pass(tiles, xn_src, m_w, mode):
            nonlocal k_sp, v_sp, q_sp, m_vex_init
            for jj, j in enumerate(tiles):
                par = jj % 2
                t0 = (j - 8) * 512 if mode == "q" else j * 512
                wlist = list(m_ln_done) + _aslist(xnT_war[par])
                for c0 in range(16):
                    Dsy.dma(f"ld_xnT{par}",
                            xnT_sb[par][:, 512 * c0:512 * (c0 + 1)],
                            xn_src.ap()[t0:t0 + 512, 128 * c0:128 * (c0 + 1)],
                            waits=wlist if c0 == 0 else (), transpose=True)
                m_xnT = p.mark(f"ld_xnT{par}")
                if mode == "q":
                    for c0 in range(16):
                        Dsc.dma("sp_xnqT",
                                xnqT_d.ap()[128 * c0:128 * (c0 + 1), t0:t0 + 512],
                                xnT_sb[par][:, 512 * c0:512 * (c0 + 1)],
                                waits=[m_xnT] if c0 == 0 else ())
                TE.wait(m_xnT)
                TE.wait(m_w)
                for cc in range(16):
                    ppi = cc % 2
                    pp = qkv_ps[ppi]
                    TE.wait(ps_war[ppi])
                    m_mm = None
                    for hc in range(16):
                        kw = dict(start=(hc == 0), stop=(hc == 15))
                        lhsT = wk_sb[:, 2048 * hc + 128 * cc:
                                     2048 * hc + 128 * (cc + 1)]
                        rhs = xnT_sb[par][:, 512 * hc:512 * (hc + 1)]
                        if hc == 15:
                            m_mm = TE.mile("matmul", pp[:], lhsT, rhs, **kw)
                        else:
                            TE.mm(pp[:], lhsT, rhs, **kw)
                    bufp = cc % 2
                    if mode in ("k", "q"):
                        m_ev = S.op("copy", qkv_ev[bufp], pp[:],
                                    waits=[m_mm, kq_war[bufp],
                                           m_ln_v if (jj == 0 and cc < 2) else None])
                        ps_war[ppi] = m_ev
                        if mode == "q":
                            m_sp = Dsc.dma("sp_q", q12T_d.ap()[cc, :, t0:t0 + 512],
                                           qkv_ev[bufp], waits=[m_ev])
                            q_sp = m_sp
                        else:
                            m_sp = Dsc.dma("sp_k", k12T_d.ap()[cc, :, t0:t0 + 512],
                                           qkv_ev[bufp], waits=[m_ev])
                            k_sp = m_sp
                        kq_war[bufp] = m_sp
                    else:
                        h = cc
                        m_ev = S.op("copy", vtr_ev[bufp], pp[:],
                                    waits=[m_mm, vtr_war[bufp]])
                        ps_war[ppi] = m_ev
                        cp_marks = [None] * 4
                        for kk in range(4):
                            bank = [ps_b, ps_d][kk % 2]
                            dstps = bank[:, 64 * (kk // 2):
                                         64 * (kk // 2 + 1)].bitcast(BF16)
                            if kk == 0:
                                TE.wait(m_ev)
                                TE.wait(m_ini)
                            TE.wait(cp_marks[kk - 2] if kk >= 2
                                    else tr_bank_war[kk % 2])
                            m_tr = TE.mile("transpose", dstps,
                                           vtr_ev[bufp][:, 128 * kk:128 * (kk + 1)],
                                           id_b[:])
                            cp_marks[kk] = S.op(
                                "copy", vex_ev[bufp][:, 132 * kk:132 * kk + 128],
                                dstps,
                                waits=[m_tr,
                                       vexev_war[bufp] if kk == 0 else None,
                                       m_vex_init if kk == 0 else None])
                        tr_bank_war[0] = cp_marks[2]
                        tr_bank_war[1] = cp_marks[3]
                        vtr_war[bufp] = cp_marks[3]
                        m_sp = Dsc.dma(
                            "sp_v", vext_d.ap()[h, :, 4 * j:4 * (j + 1), :],
                            vex_ev[bufp].rearrange("p (a f) -> p a f", a=4),
                            waits=[cp_marks[3]])
                        vexev_war[bufp] = m_sp
                        v_sp = m_sp
                    if cc % 2 == 1:
                        emit_dq_unit()
                xnT_war[par] = TE.fence()

        m_vex_init = None
        for i2 in range(2):
            V.op("memset", vex_ev[i2], 0.0, waits=[m_ln_v])
            for kk in range(4):
                m_vex_init = V.op("memset",
                                  vex_ev[i2][:, 132 * kk + 128:132 * kk + 129],
                                  1.0)
        qkv_pass(list(range(8)), xn_kv_d, m_wk, "k")
        m_wv = Dgp.dma("ld_w1", wk_sb, wqkv_v.ap(), waits=TE.fence())
        qkv_pass(list(range(8)), xn_kv_d, m_wv, "v")
        m_wq2 = Dgp.dma("ld_w2", wk_sb, wqkv_q.ap(), waits=TE.fence())
        qkv_pass([8, 9], xn_q_d, m_wq2, "q")
        m_wp = Dgp.dma("ld_w3", wk_sb, wproj.ap(), waits=TE.fence())

        def off(mat, sub):
            return (sub // 2 + 2 * mat) * 512 + (sub % 2) * 129

        m_qkv_done = [p.mark("sp_k"), p.mark("sp_v"), p.mark("sp_q"),
                      p.mark("sp_xnqT")]
        msk_sb = v8(AR1, 0, 32768, BF16, 32 * 512)
        m_msk = Dsy.dma("ld_msk", msk_sb, masks.ap(), waits=m_qkv_done)
        m_msk_last = None

        hd_war = [None, None]
        exp_war = {}
        av_war = {}
        ep_copies = []
        att_sp = [None, None]
        first_att = True
        OBASE = [0, 512, 1024, 1536]

        for h in range(NH):
            hb = h % 2
            m_k = Dsy.dma(f"ld_k{hb}", k12[hb], k12T_d.ap()[h],
                          waits=[k_sp, v_sp, q_sp] + _aslist(hd_war[hb]))
            m_v = Dsy.dma(f"ld_v{hb}",
                          vex[hb].rearrange("p (a f) -> p a f", a=32),
                          vext_d.ap()[h])
            m_q = Dsy.dma(f"ld_q{hb}", q12[hb], q12T_d.ap()[h])
            for jt in range(4):
                prev_ep = list(ep_copies)
                ep_copies = []
                msk_marks = {}
                last = None

                def emit_scores(kc):
                    sp = kc % 2
                    pst = [ps_ab, ps_cd][sp]
                    pbuf = p12[(kc % 4) // 2][:, 512 * sp:512 * (sp + 1)]
                    if kc == 0:
                        TE.wait(m_k)
                        TE.wait(m_q)
                        TE.wait(m_v)
                        for mk in prev_ep:
                            TE.wait(mk)
                    if first_att and kc < 2:
                        TE.wait(ps_war[sp])
                        TE.wait(tr_bank_war[0])
                        TE.wait(tr_bank_war[1])
                    TE.wait(exp_war.get(sp))
                    TE.mm(pst[:, 0:256], k12[hb][0:64, 128 * kc:128 * (kc + 1)],
                          q12[hb][0:64, 256 * jt:256 * (jt + 1)],
                          start=True, stop=True)
                    m_s2 = TE.mile("matmul", pst[:, 512:768],
                                   k12[hb][64:128, 128 * kc:128 * (kc + 1)],
                                   q12[hb][64:128, 256 * jt:256 * (jt + 1)],
                                   start=True, stop=True)
                    sview = pst[:].rearrange("p (b c) -> p b c", b=2)[:, :, 0:256]
                    m_exp = S.op("activation",
                                 pbuf.rearrange("p (b c) -> p b c", b=2),
                                 sview, AF.Exp, bias=-12.0, scale=0.125,
                                 waits=[m_s2, av_war.get(kc % 4)])
                    exp_war[sp] = m_exp
                    base = PC[jt] - 8
                    if kc >= base:
                        mslice = msk_sb[:, (jt * 8 + kc - base) * 512:
                                        (jt * 8 + kc - base + 1) * 512]
                        m2 = V.op("tensor_tensor", pbuf, pbuf, mslice,
                                  ALU.mult, noserial=True,
                                  waits=[m_exp, m_msk, av_war.get(kc % 4)])
                        msk_marks[kc] = m2
                    else:
                        msk_marks[kc] = m_exp

                def emit_av(kc):
                    nonlocal last
                    sp = kc % 2
                    pbuf = p12[(kc % 4) // 2][:, 512 * sp:512 * (sp + 1)]
                    TE.wait(msk_marks[kc])
                    for mat in range(2):
                        for sub in range(2):
                            o = OBASE[2 * mat + sub]
                            kw = dict(start=(kc == 0), stop=(kc == PC[jt] - 1))
                            lhsT = pbuf[:, 256 * mat + 128 * sub:
                                        256 * mat + 128 * (sub + 1)]
                            rhs = vex[hb][:, 132 * kc:132 * kc + 129]
                            if (mat, sub) == (1, 1):
                                last = TE.mile("matmul", ps_big[:, o:o + 129],
                                               lhsT, rhs, **kw)
                            else:
                                TE.mm(ps_big[:, o:o + 129], lhsT, rhs, **kw)
                    av_war[kc % 4] = last

                for kc in range(PC[jt] + 1):
                    if kc < PC[jt]:
                        emit_scores(kc)
                    if kc >= 1:
                        emit_av(kc - 1)
                m_msk_last = msk_marks[PC[jt] - 1]
                first_att = False
                m_av_done = last
                eps = []
                for sub in range(2):
                    c0 = 8 * sub
                    o1 = OBASE[sub]
                    o2 = OBASE[2 + sub]
                    t2v = scr[0][:, 128 * sub:128 * (sub + 1)]
                    dfv = scr[1][:, 128 * sub:128 * (sub + 1)]
                    sqv = scr[2][:, 128 * sub:128 * (sub + 1)]
                    onv = (scr[3][:, 128 * sub:128 * (sub + 1)]
                           .bitcast(BF16)[:, 0:128])
                    V.op("reciprocal", ep_st[:, c0:c0 + 1],
                         ps_big[:, o1 + 128:o1 + 129],
                         waits=[m_av_done, m_lam] if sub == 0 else None)
                    V.op("reciprocal", ep_st[:, c0 + 1:c0 + 2],
                         ps_big[:, o2 + 128:o2 + 129])
                    V.op("tensor_tensor", ep_st[:, c0 + 2:c0 + 3],
                         ep_st[:, c0 + 1:c0 + 2],
                         lam_sb[:, 0:1], ALU.mult)
                    V.op("tensor_scalar", t2v,
                         ps_big[:, o2:o2 + 128],
                         ep_st[:, c0 + 2:c0 + 3], None, ALU.mult)
                    m_diff = V.op("scalar_tensor_tensor", dfv,
                                  ps_big[:, o1:o1 + 128],
                                  ep_st[:, c0:c0 + 1], t2v, ALU.mult,
                                  ALU.subtract)
                    S.op("activation", sqv, dfv, AF.Square,
                         accum_out=ep_st[:, c0 + 3:c0 + 4], waits=[m_diff])
                    S.op("activation", ep_st[:, c0 + 4:c0 + 5],
                         ep_st[:, c0 + 3:c0 + 4], AF.Ln,
                         bias=EPS, scale=1.0 / VD)
                    m_rs = S.op("activation", ep_st[:, c0 + 5:c0 + 6],
                                ep_st[:, c0 + 4:c0 + 5],
                                AF.Exp, bias=math.log(ONE_M_LI), scale=-0.5)
                    eps.append((m_rs, dfv, onv, c0))
                for sub in range(2):
                    m_rs, dfv, onv, c0 = eps[sub]
                    stp = sub % 2
                    m_on = V.op("tensor_scalar", onv, dfv,
                                ep_st[:, c0 + 5:c0 + 6], None, ALU.mult,
                                waits=[m_rs])
                    bank = [ps_b, ps_d][stp]
                    dstps = bank[:, 384:448].bitcast(BF16)
                    TE.wait(m_on)
                    TE.wait(tr_bank_war[stp])
                    m_tr = TE.mile("transpose", dstps, onv, id_b[:])
                    m_cp = V.op("tensor_copy", att_st[stp][:], dstps,
                                waits=[m_tr, att_sp[stp]])
                    qs = 2 * jt + sub
                    att_sp[stp] = Dsc.dma("sp_att",
                                          attnT_d.ap()[qs, :, h, :],
                                          att_st[stp][:], waits=[m_cp])
                    ep_copies.append(m_cp)
                    tr_bank_war[stp] = m_cp
            hd_war[hb] = TE.fence()
            for _ in range(16):
                emit_dq_unit()
        while dq_state["s2"] < len(dq_units):
            emit_dq_unit()

        m_att_all = p.mark("sp_att")
        aq_war = [None, None]
        sev_war = None
        m_sp_s = None
        prev_guard = list(ep_copies)
        hd_fence = TE.fence()
        for qs in range(8):
            bufp = qs % 2
            m_aq = Dsy.dma(f"ld_aq{bufp}",
                           att_qs[bufp].rearrange("p (a f) -> p a f", a=16),
                           attnT_d.ap()[qs],
                           waits=[m_att_all] + (hd_fence if qs < 2 else [])
                                 + _aslist(aq_war[bufp]))
            TE.wait(m_aq)
            if qs == 0:
                TE.wait(m_wp)
            for mk in prev_guard:
                TE.wait(mk)
            m_pj = None
            for h in range(NH):
                for c4 in range(4):
                    kw = dict(start=(h == 0), stop=(h == 15))
                    lhsT = att_qs[bufp][:, 128 * h:128 * (h + 1)]
                    rhs = wk_sb[:, 2048 * h + 512 * c4:2048 * h + 512 * (c4 + 1)]
                    if (h, c4) == (15, 3):
                        m_pj = TE.mile("matmul",
                                       ps_big[:, 512 * c4:512 * (c4 + 1)],
                                       lhsT, rhs, **kw)
                    else:
                        TE.mm(ps_big[:, 512 * c4:512 * (c4 + 1)], lhsT, rhs,
                              **kw)
            aq_war[bufp] = TE.fence()
            m_ev = V.op("tensor_copy", s_ev[0], ps_big[:],
                        waits=[m_pj, sev_war])
            m_sp_s = Dsc.dma("sp_s", s_d.ap()[128 * qs:128 * (qs + 1), :],
                             s_ev[0], waits=[m_ev])
            sev_war = m_sp_s
            prev_guard = [m_ev]
        m_proj_ev = prev_guard[0]

        m_dq_all = [p.mark("sp_dq0a"), p.mark("sp_dq0b"),
                    p.mark("sp_dq1a"), p.mark("sp_dq1b")]
        gu_ps = [(ps_a, ps_b), (ps_c, ps_d)]
        wg_war = [None, None]
        wd_war = [None, None]
        gu_psum_war = {0: None, 1: None}
        sil_war = None
        fin_v_war = None
        out_war = None
        m_xnq_guard = None
        for tb in range(4):
            for g in range(16):
                J2, par2 = divmod(g, 2)
                src_ = xnqT_d.ap()[256 * J2 + par2:256 * J2 + 256:2,
                                   256 * tb:256 * (tb + 1)]
                w0 = []
                if g == 0:
                    if tb == 0:
                        w0 = [p.mark("sp_xnqT"), p.mark("sp_s")]
                    else:
                        w0 = _aslist(m_xnq_guard)
                Dsy.dma("ld_xnq", xnq_sb[:, 256 * g:256 * (g + 1)], src_,
                        waits=w0)
            m_xnq = p.mark("ld_xnq")
            for ng in range(32):
                wpar = ng % 2
                wl = (m_dq_all + [m_msk_last]) if (tb == 0 and ng < 2) else []
                wl = wl + _aslist(wg_war[wpar])
                if tb == 0 and ng < 2:
                    wl = wl + [m_proj_ev]
                pairs_g = [(wg_grp[wpar][:, 512 * g:512 * (g + 1)],
                            wgT_d.ap()[g, :, 512 * ng:512 * (ng + 1)])
                           for g in range(16)]
                pairs_u = [(wu_grp[wpar][:, 512 * g:512 * (g + 1)],
                            wuT_d.ap()[g, :, 512 * ng:512 * (ng + 1)])
                           for g in range(16)]
                m_wg = Dsy.dma_group(f"ld_wg{wpar}", pairs_g, waits=wl)
                m_wu = Dsy.dma_group(f"ld_wu{wpar}", pairs_u)
                TE.wait(m_xnq)
                TE.wait(m_wg)
                TE.wait(m_wu)
                for k in range(4):
                    nch = 4 * ng + k
                    pg, pu = gu_ps[nch % 2]
                    TE.wait(gu_psum_war[nch % 2])
                    if tb == 0 and ng == 0 and k < 2:
                        TE.wait(m_proj_ev)
                    if tb > 0 and ng == 0 and k < 2:
                        TE.wait(sil_war)
                    m_g = m_u2 = None
                    for g in range(16):
                        kw = dict(start=(g == 0), stop=(g == 15))
                        lhsT = wg_grp[wpar][:, 512 * g + 128 * k:
                                            512 * g + 128 * (k + 1)]
                        rhs = xnq_sb[:, 256 * g:256 * (g + 1)]
                        if g == 15:
                            m_g = TE.mile("matmul", pg[:, 0:256], lhsT, rhs, **kw)
                        else:
                            TE.mm(pg[:, 0:256], lhsT, rhs, **kw)
                    for g in range(16):
                        kw = dict(start=(g == 0), stop=(g == 15))
                        lhsT = wu_grp[wpar][:, 512 * g + 128 * k:
                                            512 * g + 128 * (k + 1)]
                        rhs = xnq_sb[:, 256 * g:256 * (g + 1)]
                        if g == 15:
                            m_u2 = TE.mile("matmul", pu[:, 0:256], lhsT, rhs, **kw)
                        else:
                            TE.mm(pu[:, 0:256], lhsT, rhs, **kw)
                    m_sq2 = S.op("activation", gl_sq, pg[:, 0:256],
                                 AF.Square, waits=[m_g])
                    V.op("tensor_scalar", gl_v2, gl_sq, GC3, GC1,
                         ALU.mult, ALU.add, waits=[m_sq2])
                    m_v3 = V.op("tensor_tensor", gl_v3, gl_v2,
                                pg[:, 0:256], ALU.mult)
                    m_t = S.op("activation", gl_t, gl_v3, AF.Tanh,
                               waits=[m_v3])
                    m_up = S.op("activation", gl_up, pu[:, 0:256],
                                AF.Identity, scale=0.5, waits=[m_u2])
                    V.op("tensor_tensor", gl_v2, pg[:, 0:256],
                         gl_up, ALU.mult, waits=[m_t, m_up])
                    m_int = V.op("scalar_tensor_tensor",
                                 inter[:, 256 * nch:256 * (nch + 1)],
                                 gl_t, 1.0, gl_v2, ALU.add, ALU.mult)
                    gu_psum_war[nch % 2] = m_int
                wg_war[wpar] = TE.fence()
            m_int_all = V.mark()
            m_xnq_guard = TE.fence()
            last_d = None
            for nch in range(128):
                wpar = nch % 2
                m_wd = Dsy.dma(f"ld_wd{wpar}", wd_st[wpar],
                               wdT_d.ap()[nch], waits=_aslist(wd_war[wpar]))
                TE.wait(m_wd)
                if nch == 0:
                    TE.wait(m_int_all)
                    TE.wait(sil_war)
                for tsub in range(2):
                    for c4 in range(4):
                        kw = dict(start=(nch == 0), stop=(nch == 127))
                        lhsT = inter[:, 256 * nch + 128 * tsub:
                                     256 * nch + 128 * (tsub + 1)]
                        rhs = wd_st[wpar][:, 512 * c4:512 * (c4 + 1)]
                        dst = (ps_big[:, 512 * c4:512 * (c4 + 1)] if tsub == 0
                               else ps4[c4][:])
                        if (tsub, c4) == (1, 3):
                            last_d = TE.mile("matmul", dst, lhsT, rhs, **kw)
                        else:
                            TE.mm(dst, lhsT, rhs, **kw)
                wd_war[wpar] = TE.fence()
            m_down = last_d
            for tsub in range(2):
                r0 = 256 * tb + 128 * tsub
                m_fx = Dsy.dma("ld_fx", fin_x, x_q.ap()[r0:r0 + 128, :],
                               waits=[fin_v_war,
                                      m_down if (tb == 0 and tsub == 0) else None])
                m_fs = Dsy.dma("ld_fs", fin_s, s_d.ap()[r0:r0 + 128, :],
                               waits=[m_sp_s] if (tb == 0 and tsub == 0) else ())
                if tsub == 0:
                    m_sil = S.op("activation", fin_sil, ps_big[:], AF.Silu,
                                 waits=[m_down])
                else:
                    for c4 in range(4):
                        m_sil = S.op("activation",
                                     fin_sil[:, 512 * c4:512 * (c4 + 1)],
                                     ps4[c4][:], AF.Silu,
                                     waits=[m_down] if c4 == 0 else ())
                V.op("tensor_tensor", fin_o, fin_sil, fin_s,
                     ALU.mult, waits=[m_sil, m_fs, m_fx, out_war])
                m_fin = V.op("tensor_tensor", fin_o, fin_o, fin_x, ALU.add)
                fin_v_war = m_fin
                out_war = Dsc.dma("sp_out", out.ap()[r0:r0 + 128, :],
                                  fin_o, waits=[m_fin])
            sil_war = m_sil
            gu_psum_war = {0: sil_war, 1: sil_war}

        p.wait("sync", p.mark("sp_out"))

        with nc.Block() as block:
            @block.tensor
            def _(eng):
                for f in p.body["tensor"]:
                    f(eng)

            @block.vector
            def _(eng):
                for f in p.body["vector"]:
                    f(eng)

            @block.scalar
            def _(eng):
                for f in p.body["scalar"]:
                    f(eng)

            @block.gpsimd
            def _(eng):
                for f in p.body["gpsimd"]:
                    f(eng)

            @block.sync
            def _(eng):
                for f in p.body["sync"]:
                    f(eng)

        res = run_bass_kernel_spmd(nc, in_maps, core_ids=list(range(N_CORES)),
                                   trace=trace)
    return res


def kernel(**inputs):
    in_maps = _host_inputs(inputs)
    res = build_and_run(in_maps)
    full = np.empty((2, T, D), np.float32)
    for c in range(N_CORES):
        b, q = divmod(c, 4)
        oc = np.asarray(res.results[c]["out"])
        for jt, t in enumerate(TILES[q]):
            full[b, 256 * t:256 * (t + 1)] = oc[256 * jt:256 * (jt + 1)]
    return full


# version 12
# speedup vs baseline: 1.1903x; 1.0547x over previous
import math
import numpy as np
import concourse.bass as bass
import concourse.mybir as mybir
from concourse.bass_utils import run_bass_kernel_spmd

F32 = mybir.dt.float32
BF16 = mybir.dt.bfloat16
U8 = mybir.dt.uint8
I16 = mybir.dt.int16
AF = mybir.ActivationFunctionType
ALU = mybir.AluOpType
AX = mybir.AxisListType

D = 2048
T = 4096
NQ = 1024
NH = 16
VD = 128
NF = 16384
LAMBDA_INIT = 0.8 - 0.6 * math.exp(-0.3)
ONE_M_LI = 1.0 - LAMBDA_INIT
GC1 = 0.7978845608028654
GC3 = 0.7978845608028654 * 0.044715
EPS = 1e-5

CONST_VALS = [0.0, 1.0, 0.125, -12.0, 0.5, -0.25, EPS, 1.0 / D, 1.0 / VD,
              -0.5, math.log(ONE_M_LI), GC1, GC3, LAMBDA_INIT, 0.0625,
              -0.499999, 63.5, -1.0 / 256.0]

N_CORES = 8
TILES = [[c4, 4 + c4, 11 - c4, 15 - c4] for c4 in range(4)]
PC = [8, 16, 24, 32]
MOFF = [0, 8, 24, 48]
NCH_TOT = 80



def _nperm():
    p = []
    for b in range(NF // 256):
        p.extend(range(256 * b, 256 * (b + 1), 2))
        p.extend(range(256 * b + 1, 256 * (b + 1), 2))
    return np.array(p)


def _pack_u8(a):
    a = np.ascontiguousarray(np.asarray(a, np.int32))
    return np.ascontiguousarray(a.view(np.uint8).reshape(*a.shape, 4)[..., 0])


def _chunkmajor(w):
    Dr, C = w.shape
    return np.ascontiguousarray(w.reshape(Dr // 128, 128, C).transpose(1, 0, 2)
                                .reshape(128, (Dr // 128) * C))


def _host_inputs(inputs):
    x = np.asarray(inputs["L0_attn_x"], np.float32).reshape(2, T, D)
    wqkv = np.asarray(inputs["L0_attn_Wqkv"], np.float32)
    wproj = np.asarray(inputs["L0_attn_Wproj"], np.float32)

    kcols, vcols, qcols = [], [], []
    for h in range(NH):
        kcols.extend(range(1024 + 64 * h, 1024 + 64 * h + 64))
        kcols.extend(range(3072 + 64 * h, 3072 + 64 * h + 64))
    for h in range(NH):
        vcols.extend(range(4096 + 128 * h, 4096 + 128 * h + 128))
    for h in range(NH):
        qcols.extend(range(64 * h, 64 * h + 64))
        qcols.extend(range(2048 + 64 * h, 2048 + 64 * h + 64))
    wqkv_k = _chunkmajor(np.ascontiguousarray(wqkv[:, kcols]))
    wqkv_v = _chunkmajor(np.ascontiguousarray(wqkv[:, vcols]))
    wqkv_q = _chunkmajor(np.ascontiguousarray(wqkv[:, qcols]))
    wproj_cm = _chunkmajor(np.ascontiguousarray(wproj))

    lamv = np.zeros((128, 256), np.float32)
    lamv[:, 0:64] = np.asarray(inputs["L0_attn_lambda_q1"], np.float32)
    lamv[:, 64:128] = np.asarray(inputs["L0_attn_lambda_q2"], np.float32)
    lamv[:, 128:192] = np.asarray(inputs["L0_attn_lambda_k1"], np.float32)
    lamv[:, 192:256] = np.asarray(inputs["L0_attn_lambda_k2"], np.float32)

    import ml_dtypes
    consts = np.zeros((128, 18), np.float32)
    consts[:, :len(CONST_VALS)] = CONST_VALS
    ident = np.eye(128, dtype=ml_dtypes.bfloat16)

    perm = _nperm()
    gate_pk = np.ascontiguousarray(_pack_u8(inputs["L0_ffn_gate_W_packed"])[perm, :].T)
    up_pk = np.ascontiguousarray(_pack_u8(inputs["L0_ffn_up_W_packed"])[perm, :].T)
    down_pk = np.ascontiguousarray(_pack_u8(inputs["L0_ffn_down_W_packed"]).T)
    gsc = np.ascontiguousarray(np.asarray(inputs["L0_ffn_gate_scales"], np.float32)[perm, :].T)
    usc = np.ascontiguousarray(np.asarray(inputs["L0_ffn_up_scales"], np.float32)[perm, :].T)
    dsc = np.ascontiguousarray(np.asarray(inputs["L0_ffn_down_scales"], np.float32).T)

    shared = {
        "wqkv_k": wqkv_k, "wqkv_v": wqkv_v, "wqkv_q": wqkv_q, "wproj": wproj_cm,
        "lamv": lamv, "consts": consts, "ident": ident,
        "gate_pk": gate_pk, "up_pk": up_pk, "down_pk": down_pk,
        "gsc": gsc, "usc": usc, "dsc": dsc,
    }

    kt = np.arange(128)
    qt = np.arange(256)
    in_maps = []
    for c in range(N_CORES):
        b, c4 = divmod(c, 4)
        tiles = TILES[c4]
        m = np.zeros((32, 128, 512), ml_dtypes.bfloat16)
        for jt in range(4):
            for i8 in range(8):
                kc = PC[jt] - 8 + i8
                m1 = ((128 * kc + kt[:, None])
                      <= (256 * tiles[jt] + qt[None, :256])).astype(np.float32)
                m[jt * 8 + i8] = np.concatenate([m1, m1], 1)
        d = dict(shared)
        d["x_kv"] = np.ascontiguousarray(x[b])
        d["x_q"] = np.ascontiguousarray(
            np.concatenate([x[b, 256 * t:256 * (t + 1)] for t in tiles], 0))
        d["masks"] = np.ascontiguousarray(m.transpose(1, 0, 2).reshape(128, 32 * 512))
        in_maps.append(d)
    return in_maps



def _aslist(w):
    if w is None:
        return []
    if isinstance(w, tuple) and len(w) == 2 and isinstance(w[0], str):
        return [w]
    out = []
    for m in w:
        out.extend(_aslist(m))
    return out


class Prog:
    ENGINES = ("tensor", "vector", "scalar", "gpsimd", "sync")

    def __init__(self, nc, ctx):
        self.nc = nc
        self.ctx = ctx
        self.body = {e: [] for e in self.ENGINES}
        self.sems = {}
        self.waited = {e: {} for e in self.ENGINES}

    def sem(self, name):
        if name not in self.sems:
            h = self.ctx.enter_context(self.nc.semaphore(name))
            self.sems[name] = [h, 0]
        return self.sems[name]

    def wait(self, eng, mark):
        for m in _aslist(mark):
            if m[1] <= 0 or self.waited[eng].get(m[0], 0) >= m[1]:
                continue
            self.waited[eng][m[0]] = m[1]
            h = self.sems[m[0]][0]
            self.body[eng].append(lambda e, h=h, v=m[1]: e.wait_ge(h, v))

    def op(self, eng, fname, *a, inc=None, **k):
        if inc is not None:
            s = self.sem(inc[0])
            s[1] += inc[1]
            h, val = s[0], s[1]

            def run(e, fname=fname, a=a, k=k, h=h, amt=inc[1]):
                getattr(e, fname)(*a, **k).then_inc(h, amt)
            self.body[eng].append(run)
            return (inc[0], val)

        def run(e, fname=fname, a=a, k=k):
            getattr(e, fname)(*a, **k)
        self.body[eng].append(run)
        return None

    def mark(self, name):
        return (name, self.sems[name][1]) if name in self.sems else (name, 0)


SELF_SERIAL = True


class SerialEngine:
    def __init__(self, prog, eng, semname, serial=True):
        self.p, self.eng, self.semname = prog, eng, semname
        self.serial = serial
        self.p.sem(semname)

    def __init2__(self):
        pass

    def op(self, fname, *a, waits=(), noserial=False, **k):
        self.p.wait(self.eng, waits)
        if self.serial and not noserial:
            n = self.p.sems[self.semname][1]
            if n > 0:
                self.p.wait(self.eng, (self.semname, n))
        return self.p.op(self.eng, fname, *a, inc=(self.semname, 1), **k)

    def mark(self):
        return self.p.mark(self.semname)


class DmaQueue:
    def __init__(self, prog, eng):
        self.p, self.eng = prog, eng

    def dma(self, semname, out, in_, waits=(), transpose=False):
        self.p.wait(self.eng, waits)
        n = self.p.sems[semname][1] if semname in self.p.sems else 0
        if n > 0:
            self.p.wait(self.eng, (semname, n))
        if transpose:
            return self.p.op(self.eng, "dma_start_transpose", out, in_,
                             inc=(semname, 16))
        return self.p.op(self.eng, "dma_start", out=out, in_=in_,
                         inc=(semname, 16))

    def dma_group(self, semname, pairs, waits=()):
        self.p.wait(self.eng, waits)
        for out, in_ in pairs:
            self.p.op(self.eng, "dma_start", out=out, in_=in_, inc=(semname, 16))
        return self.p.mark(semname)


class TensorEng:
    def __init__(self, prog):
        self.p = prog
        self.rot = 0
        self.p.sem("st0")
        self.p.sem("st1")

    def wait(self, mark):
        self.p.wait("tensor", mark)

    def mm(self, *a, **k):
        self.p.op("tensor", "matmul", *a, **k)

    def mile(self, fname, *a, **k):
        name = f"st{self.rot}"
        self.rot ^= 1
        n = self.p.sems[name][1]
        if n > 0:
            self.p.wait("tensor", (name, n))
        return self.p.op("tensor", fname, *a, inc=(name, 1), **k)

    def fence(self):
        return [self.p.mark("st0"), self.p.mark("st1")]



def build_and_run(in_maps, trace=False):
    from contextlib import ExitStack
    nc = bass.Bass("TRN2", target_bir_lowering=False)

    di = lambda n, s, dt: nc.dram_tensor(n, s, dt, kind="ExternalInput")
    dd = lambda n, s, dt: nc.dram_tensor(n, s, dt, kind="Internal")
    x_kv = di("x_kv", [T, D], F32)
    x_q = di("x_q", [NQ, D], F32)
    wqkv_k = di("wqkv_k", [128, 16 * 2048], F32)
    wqkv_v = di("wqkv_v", [128, 16 * 2048], F32)
    wqkv_q = di("wqkv_q", [128, 16 * 2048], F32)
    wproj = di("wproj", [128, 16 * 2048], F32)
    lamv = di("lamv", [128, 256], F32)
    consts = di("consts", [128, 18], F32)
    ident = di("ident", [128, 128], BF16)
    masks = di("masks", [128, 32 * 512], BF16)
    gate_pk = di("gate_pk", [1024, NF], U8)
    up_pk = di("up_pk", [1024, NF], U8)
    down_pk = di("down_pk", [8192, D], U8)
    gsc = di("gsc", [64, NF], F32)
    usc = di("usc", [64, NF], F32)
    dsc = di("dsc", [512, D], F32)
    out = nc.dram_tensor("out", [NQ, D], F32, kind="ExternalOutput")

    xn_kv_d = dd("xn_kv_d", [T, D], BF16)
    xn_q_d = dd("xn_q_d", [NQ, D], BF16)
    xnqT_d = dd("xnqT_d", [D, NQ], BF16)
    xnT2 = dd("xnT2_d", [D, T], BF16)
    inter_d = dd("inter_d", [128, 128, NQ], BF16)
    k12T_d = dd("k12T_d", [NH, 128, T], BF16)
    vext_d = dd("vext_d", [NH, 128, 32, 132], BF16)
    q12T_d = dd("q12T_d", [NH, 128, NQ], BF16)
    s_d = dd("s_d", [NQ, D], F32)
    attnT_d = dd("attnT_d", [8, 128, NH, 128], BF16)
    wgT_d = dd("wgT_d", [16, 128, NF], BF16)
    wuT_d = dd("wuT_d", [16, 128, NF], BF16)
    wdT_d = dd("wdT_d", [128, 128, D], BF16)

    ctx = ExitStack()
    with ctx:
        sb = lambda n, s, dt: ctx.enter_context(nc.sbuf_tensor(n, s, dt))
        psal = lambda n, s, dt: ctx.enter_context(nc.psum_tensor(n, s, dt))

        AR0 = sb("AR0", [128, 65536], U8)
        AR1 = sb("AR1", [128, 65536], U8)
        AR2 = sb("AR2", [128, 49152], U8)
        AR3 = sb("AR3", [128, 22528], U8)

        def v8(ar, off, nbytes, dt, cols):
            return ar[:, off:off + nbytes].bitcast(dt)[:, 0:cols]

        wk_sb = v8(AR0, 0, 65536, BF16, 16 * 2048)
        wv_sb = wk_sb
        wq_sb = wk_sb
        wp_sb = wk_sb
        inter = v8(AR0, 0, 65536, BF16, 128 * 256)

        xt = [v8(AR1, 0, 8192, F32, D), v8(AR1, 8192, 8192, F32, D)]
        lnsq = v8(AR1, 16384, 8192, F32, D)
        xnt = [v8(AR1, 24576, 4096, BF16, D), v8(AR1, 28672, 4096, BF16, D)]
        xnT_sb = [v8(AR2, 0, 16384, BF16, 16 * 512),
                  v8(AR2, 16384, 16384, BF16, 16 * 512)]
        qkv_ev = [v8(AR1, 0, 1024, BF16, 512), v8(AR1, 1024, 1024, BF16, 512)]
        vtr_ev = [v8(AR1, 2048, 1024, BF16, 512), v8(AR1, 3072, 1024, BF16, 512)]
        vex_ev = [v8(AR1, 4096, 1088, BF16, 4 * 132), v8(AR1, 5248, 1088, BF16, 4 * 132)]
        wg_grp = [v8(AR1, 0, 16384, BF16, 16 * 512),
                  v8(AR1, 16384, 16384, BF16, 16 * 512)]
        wu_grp = [v8(AR1, 32768, 16384, BF16, 16 * 512),
                  v8(AR1, 49152, 16384, BF16, 16 * 512)]

        k12 = [v8(AR2, 0, 8192, BF16, T), v8(AR2, 8192, 8192, BF16, T)]
        vex = [v8(AR2, 16384, 8448, BF16, 32 * 132), v8(AR2, 24832, 8448, BF16, 32 * 132)]
        q12 = [v8(AR2, 33280, 2048, BF16, NQ), v8(AR2, 35328, 2048, BF16, NQ)]
        att_qs = [v8(AR2, 0, 4096, BF16, D), v8(AR2, 4096, 4096, BF16, D)]
        s_ev = [v8(AR2, 8192, 8192, F32, D)]
        fin_x = v8(AR2, 0, 8192, F32, D)
        fin_s = v8(AR2, 8192, 8192, F32, D)
        fin_sil = v8(AR2, 16384, 8192, F32, D)
        fin_o = v8(AR2, 24576, 8192, F32, D)
        xnq_sb = v8(AR2, 32768, 8192, BF16, 16 * 256)
        wd_st = [v8(AR2, 40960, 4096, BF16, 2048), v8(AR2, 45056, 4096, BF16, 2048)]

        DQW = 1024

        def dq_lane(specs):
            (ar_pk, o_pk), (ar_m, o_m) = specs
            return dict(
                pk=v8(ar_pk, o_pk, 1024, U8, DQW),
                c4=v8(ar_m, o_m, 4096, I16, 2 * DQW),
                s15=v8(ar_m, o_m + 4096, 4096, I16, 2 * DQW),
                xb=v8(ar_m, o_m + 8192, 4096, I16, 2 * DQW),
                r=v8(ar_m, o_m + 12288, 4096, BF16, 2 * DQW),
                scb2=v8(ar_m, o_m + 16384, 4096, BF16, 2 * DQW),
            )
        dq_lanes = [dq_lane([(AR2, 37888), (AR3, 0)]),
                    dq_lane([(AR1, 32768 + 5120), (AR1, 32768 + 10240)])]

        cst = sb("cst", [128, 18], F32)
        for i, v in enumerate(CONST_VALS):
            nc.const_aps.register(F32, v, cst[:, i:i + 1])
        lam_sb = sb("lam_sb", [128, 8], F32)
        lamv_sb = v8(AR1, 16384, 1024, F32, 256)
        lamx = v8(AR1, 20480, 1024, F32, 256)
        id_b = sb("id_b", [128, 128], BF16)
        lnst = sb("lnst", [128, 8], F32)
        ep_st = sb("ep_st", [128, 16], F32)
        p12 = [sb(f"p12_{i}", [128, 1024], BF16) for i in range(2)]
        scr = [sb(f"scr{i}", [128, 256], F32) for i in range(4)]
        ep_t2 = scr[0][:, 0:128]
        ep_diff = scr[1][:, 0:128]
        ep_sq = scr[2][:, 0:128]
        ep_on = scr[3][:, 0:128].bitcast(BF16)[:, 0:128]
        gl_sq, gl_v2, gl_v3, gl_t = (s[:] for s in scr)
        gl_up = scr[2][:]
        att_st = [sb(f"att_st{i}", [128, 128], BF16) for i in range(2)]

        ps_ab = psal("ps_ab", [128, 1024], F32)
        ps_cd = psal("ps_cd", [128, 1024], F32)
        ps_big = psal("ps_big", [128, 2048], F32)

        class _PV:
            def __init__(self, t, lo):
                self.t, self.lo = t, lo
            def __getitem__(self, key):
                assert isinstance(key, tuple) or key == slice(None)
                if not isinstance(key, tuple):
                    return self.t[:, self.lo:self.lo + 512]
                p, f = key
                assert p == slice(None)
                return self.t[:, self.lo + (f.start or 0):self.lo + f.stop]
        ps_a = _PV(ps_ab, 0)
        ps_b = _PV(ps_ab, 512)
        ps_c = _PV(ps_cd, 0)
        ps_d = _PV(ps_cd, 512)
        ps4 = [ps_a, ps_b, ps_c, ps_d]

        p = Prog(nc, ctx)
        V = SerialEngine(p, "vector", "sv")
        S = SerialEngine(p, "scalar", "ss", serial=False)
        G = SerialEngine(p, "gpsimd", "sg", serial=False)
        TE = TensorEng(p)
        Dsy = DmaQueue(p, "sync")
        Dsc = DmaQueue(p, "scalar")
        Dgp = DmaQueue(p, "gpsimd")

        m_ini = Dsy.dma_group("ld_ini", [
            (cst[:], consts.ap()),
            (id_b[:], ident.ap()),
            (lamv_sb, lamv.ap()),
        ])
        m_wk = Dgp.dma("ld_w0", wk_sb, wqkv_k.ap())

        S.op("copy", lam_sb[:, 1:2], cst[:, 0:1], waits=[m_ini])

        V.op("tensor_tensor", lamx[:, 0:128], lamv_sb[:, 0:128],
             lamv_sb[:, 128:256], ALU.mult, waits=[m_ini])
        V.op("reduce_sum", lam_sb[:, 2:3], lamx[:, 0:64], axis=AX.X)
        m_d12 = V.op("reduce_sum", lam_sb[:, 3:4], lamx[:, 64:128], axis=AX.X)
        S.op("activation", lam_sb[:, 4:5], lam_sb[:, 2:3], AF.Exp,
             waits=[m_d12])
        m_e2 = S.op("activation", lam_sb[:, 5:6], lam_sb[:, 3:4], AF.Exp)
        V.op("tensor_tensor", lam_sb[:, 6:7], lam_sb[:, 4:5],
             lam_sb[:, 5:6], ALU.subtract, waits=[m_e2])
        m_lam = V.op("tensor_scalar", lam_sb[:, 0:1], lam_sb[:, 6:7],
                     LAMBDA_INIT, None, ALU.add)

        sp_xn_mark = [None, None]
        ln_v_mark = [None, None]
        for i in range(40):
            par = i % 2
            if i < 32:
                src_ = x_kv.ap()[128 * i:128 * (i + 1), :]
                dst = xn_kv_d.ap()[128 * i:128 * (i + 1), :]
            else:
                j = i - 32
                src_ = x_q.ap()[128 * j:128 * (j + 1), :]
                dst = xn_q_d.ap()[128 * j:128 * (j + 1), :]
            m_ld = Dsy.dma(f"ld_x{par}", xt[par], src_,
                           waits=[ln_v_mark[par], m_lam if i == 0 else None])
            V.op("reduce_sum", lnst[:, 0:1], xt[par], axis=AX.X,
                 waits=[m_ld])
            m_sq = S.op("activation", lnsq, xt[par], AF.Square,
                        accum_out=lnst[:, 1:2], waits=[m_ld, V.mark()])
            V.op("tensor_scalar", lnst[:, 2:3], lnst[:, 0:1], 1.0 / D, None,
                 ALU.mult, waits=[m_sq])
            V.op("tensor_tensor", lnst[:, 3:4], lnst[:, 2:3], lnst[:, 2:3],
                 ALU.mult)
            V.op("tensor_scalar", lnst[:, 4:5], lnst[:, 1:2], 1.0 / D, EPS,
                 ALU.mult, ALU.add)
            m_var = V.op("tensor_tensor", lnst[:, 5:6], lnst[:, 4:5],
                         lnst[:, 3:4], ALU.subtract)
            S.op("activation", lnst[:, 6:7], lnst[:, 5:6], AF.Ln,
                 waits=[m_var])
            m_rstd = S.op("activation", lnst[:, 7:8], lnst[:, 6:7], AF.Exp,
                          scale=-0.5)
            m_xn = V.op("tensor_scalar", xnt[par], xt[par],
                        lnst[:, 2:3], lnst[:, 7:8], ALU.subtract, ALU.mult,
                        waits=[m_rstd, sp_xn_mark[par]])
            ln_v_mark[par] = m_xn
            sp_xn_mark[par] = Dsc.dma(f"sp_xn{par}", dst, xnt[par],
                                      waits=[m_xn])
        m_ln_done = [sp_xn_mark[0], sp_xn_mark[1]]
        m_ln_v = V.mark()

        dq_units = []
        for mat in range(3):
            if mat < 2:
                dq_units += [(mat, J, cc) for J in range(8) for cc in range(16)]
            else:
                dq_units += [(2, J, cc) for J in range(64) for cc in range(2)]
        dq_state = {"s1": 0, "s2": 0, "m": [dict(), dict()]}

        def dq_stage1(i):
            mat, J, cc = dq_units[i]
            L = i % 2
            t = dq_lanes[L]
            pm = dq_state["m"][L]
            W = DQW
            pk_t, sc_t, w_d = [(gate_pk, gsc, wgT_d), (up_pk, usc, wuT_d),
                               (down_pk, dsc, wdT_d)][mat]
            Dpk = [Dsy, Dsc][L]
            m_pk = Dpk.dma(f"ld_pk{L}", t["pk"],
                           pk_t.ap()[128 * J:128 * (J + 1), cc * W:(cc + 1) * W],
                           waits=[pm.get("chi"), pm.get("clo")])
            src_sc = (sc_t.ap()[8 * J:8 * (J + 1), cc * W:(cc + 1) * W]
                      .unsqueeze(1).broadcast_to([8, 16, W]))
            Dgp.dma(f"ld_sc{L}", t["scb2"][:, 0:W], src_sc,
                    waits=[pm.get("o")])
            m_sc = Dgp.dma(f"ld_sc{L}", t["scb2"][:, W:2 * W], src_sc)
            m_chi = S.op("activation", t["c4"][:, W:2 * W], t["pk"],
                         AF.Identity, scale=0.0625, bias=-0.499999,
                         waits=[m_pk, pm.get("xb")])
            m_clo = V.op("scalar_tensor_tensor", t["c4"][:, 0:W],
                         t["c4"][:, W:2 * W], -16.0, t["pk"],
                         ALU.mult, ALU.add, noserial=True,
                         waits=[m_chi, pm.get("xb")])
            m_s15 = V.op("tensor_scalar", t["s15"], t["c4"], 3, 15,
                         ALU.logical_shift_right, ALU.logical_shift_left,
                         noserial=True, waits=[m_clo, pm.get("orr")])
            m_u = V.op("tensor_single_scalar", t["c4"], t["c4"], 7,
                       ALU.bitwise_and, noserial=True, waits=[m_s15])
            m_xb = V.op("tensor_scalar", t["xb"], t["c4"], 64, 16128,
                        ALU.mult, ALU.add, noserial=True,
                        waits=[m_u] + _aslist(pm.get("sp")))
            pm.update(pk=m_pk, sc=m_sc, chi=m_chi, clo=m_clo, s15=m_s15,
                      xb=m_xb)

        def dq_stage2(i):
            mat, J, cc = dq_units[i]
            L = i % 2
            t = dq_lanes[L]
            pm = dq_state["m"][L]
            W = DQW
            w_d = [wgT_d, wuT_d, wdT_d][mat]
            Dsp = [Dsc, Dsy][L]
            m_r = S.op("activation", t["r"], t["xb"], AF.Relu, bias=63.5,
                       scale=-1.0 / 256.0, waits=[pm["xb"], pm.get("val")])
            xb_bf = t["xb"].bitcast(BF16)[:, 0:2 * W]
            m_val = V.op("tensor_tensor", xb_bf, xb_bf, t["r"], ALU.subtract,
                         noserial=True, waits=[m_r])
            m_or = V.op("tensor_tensor", t["xb"], t["xb"], t["s15"],
                        ALU.bitwise_or, noserial=True, waits=[m_val])
            m_o = V.op("tensor_tensor", xb_bf, xb_bf, t["scb2"], ALU.mult,
                       noserial=True, waits=[m_or, pm["sc"]])
            m_spa = Dsp.dma(f"sp_dq{L}a",
                            w_d.ap()[2 * J, :, cc * W:(cc + 1) * W],
                            xb_bf[:, 0:W], waits=[m_o])
            m_spb = Dsp.dma(f"sp_dq{L}b",
                            w_d.ap()[2 * J + 1, :, cc * W:(cc + 1) * W],
                            xb_bf[:, W:2 * W])
            pm.update(r=m_r, val=m_val, orr=m_or, o=m_o, sp=[m_spa, m_spb])

        def emit_dq_unit():
            if dq_state["s1"] < len(dq_units):
                dq_stage1(dq_state["s1"])
                dq_state["s1"] += 1
            lim = dq_state["s1"] - (1 if dq_state["s1"] < len(dq_units) else 0)
            if dq_state["s2"] < lim:
                dq_stage2(dq_state["s2"])
                dq_state["s2"] += 1

        qkv_ps = [ps_a, ps_c]
        ps_war = [None, None]
        tr_bank_war = [None, None]
        xnT_war = [None, None]
        kq_war = [None, None]
        vtr_war = [None, None]
        vexev_war = [None, None]
        k_sp = v_sp = q_sp = None
        m_vex_init = None

        def qkv_pass(tiles, xn_src, m_w, mode):
            nonlocal k_sp, v_sp, q_sp, m_vex_init
            for jj, j in enumerate(tiles):
                par = jj % 2
                t0 = (j - 8) * 512 if mode == "q" else j * 512
                wlist = list(m_ln_done) + _aslist(xnT_war[par])
                if mode in ("k", "q"):
                    for c0 in range(16):
                        Dq = [Dsy, Dsc][c0 % 2]
                        Dq.dma(f"ld_xnT{par}",
                               xnT_sb[par][:, 512 * c0:512 * (c0 + 1)],
                               xn_src.ap()[t0:t0 + 512, 128 * c0:128 * (c0 + 1)],
                               waits=wlist if c0 < 2 else (), transpose=True)
                else:
                    for c0 in range(16):
                        Dsy.dma(f"ld_xnT{par}",
                                xnT_sb[par][:, 512 * c0:512 * (c0 + 1)],
                                xnT2.ap()[128 * c0:128 * (c0 + 1), t0:t0 + 512],
                                waits=(wlist + [p.mark("sp_xnT")])
                                      if c0 == 0 else ())
                m_xnT = p.mark(f"ld_xnT{par}")
                if mode == "k":
                    for c0 in range(16):
                        Dsc.dma("sp_xnT",
                                xnT2.ap()[128 * c0:128 * (c0 + 1), t0:t0 + 512],
                                xnT_sb[par][:, 512 * c0:512 * (c0 + 1)],
                                waits=[m_xnT] if c0 == 0 else ())
                if mode == "q":
                    for c0 in range(16):
                        Dsc.dma("sp_xnqT",
                                xnqT_d.ap()[128 * c0:128 * (c0 + 1), t0:t0 + 512],
                                xnT_sb[par][:, 512 * c0:512 * (c0 + 1)],
                                waits=[m_xnT] if c0 == 0 else ())
                TE.wait(m_xnT)
                TE.wait(m_w)
                for cc in range(16):
                    ppi = cc % 2
                    pp = qkv_ps[ppi]
                    TE.wait(ps_war[ppi])
                    m_mm = None
                    for hc in range(16):
                        kw = dict(start=(hc == 0), stop=(hc == 15))
                        lhsT = wk_sb[:, 2048 * hc + 128 * cc:
                                     2048 * hc + 128 * (cc + 1)]
                        rhs = xnT_sb[par][:, 512 * hc:512 * (hc + 1)]
                        if hc == 15:
                            m_mm = TE.mile("matmul", pp[:], lhsT, rhs, **kw)
                        else:
                            TE.mm(pp[:], lhsT, rhs, **kw)
                    bufp = cc % 2
                    if mode in ("k", "q"):
                        m_ev = S.op("copy", qkv_ev[bufp], pp[:],
                                    waits=[m_mm, kq_war[bufp],
                                           m_ln_v if (jj == 0 and cc < 2) else None])
                        ps_war[ppi] = m_ev
                        if mode == "q":
                            m_sp = Dsc.dma("sp_q", q12T_d.ap()[cc, :, t0:t0 + 512],
                                           qkv_ev[bufp], waits=[m_ev])
                            q_sp = m_sp
                        else:
                            m_sp = Dsc.dma("sp_k", k12T_d.ap()[cc, :, t0:t0 + 512],
                                           qkv_ev[bufp], waits=[m_ev])
                            k_sp = m_sp
                        kq_war[bufp] = m_sp
                    else:
                        h = cc
                        m_ev = S.op("copy", vtr_ev[bufp], pp[:],
                                    waits=[m_mm, vtr_war[bufp]])
                        ps_war[ppi] = m_ev
                        cp_marks = [None] * 4
                        for kk in range(4):
                            bank = [ps_b, ps_d][kk % 2]
                            dstps = bank[:, 64 * (kk // 2):
                                         64 * (kk // 2 + 1)].bitcast(BF16)
                            if kk == 0:
                                TE.wait(m_ev)
                                TE.wait(m_ini)
                            TE.wait(cp_marks[kk - 2] if kk >= 2
                                    else tr_bank_war[kk % 2])
                            m_tr = TE.mile("transpose", dstps,
                                           vtr_ev[bufp][:, 128 * kk:128 * (kk + 1)],
                                           id_b[:])
                            cp_marks[kk] = S.op(
                                "copy", vex_ev[bufp][:, 132 * kk:132 * kk + 128],
                                dstps,
                                waits=[m_tr,
                                       vexev_war[bufp] if kk == 0 else None,
                                       m_vex_init if kk == 0 else None])
                        tr_bank_war[0] = cp_marks[2]
                        tr_bank_war[1] = cp_marks[3]
                        vtr_war[bufp] = cp_marks[3]
                        m_sp = Dsc.dma(
                            "sp_v", vext_d.ap()[h, :, 4 * j:4 * (j + 1), :],
                            vex_ev[bufp].rearrange("p (a f) -> p a f", a=4),
                            waits=[cp_marks[3]])
                        vexev_war[bufp] = m_sp
                        v_sp = m_sp
                    if cc % 2 == 1:
                        emit_dq_unit()
                xnT_war[par] = ([TE.fence(), p.mark("sp_xnT")]
                                if mode == "k" else TE.fence())

        m_vex_init = None
        for i2 in range(2):
            V.op("memset", vex_ev[i2], 0.0, waits=[m_ln_v])
            for kk in range(4):
                m_vex_init = V.op("memset",
                                  vex_ev[i2][:, 132 * kk + 128:132 * kk + 129],
                                  1.0)
        qkv_pass(list(range(8)), xn_kv_d, m_wk, "k")
        m_wv = Dgp.dma("ld_w1", wk_sb, wqkv_v.ap(), waits=TE.fence())
        qkv_pass(list(range(8)), xn_kv_d, m_wv, "v")
        m_wq2 = Dgp.dma("ld_w2", wk_sb, wqkv_q.ap(), waits=TE.fence())
        qkv_pass([8, 9], xn_q_d, m_wq2, "q")
        m_wp = Dgp.dma("ld_w3", wk_sb, wproj.ap(), waits=TE.fence())

        def off(mat, sub):
            return (sub // 2 + 2 * mat) * 512 + (sub % 2) * 129

        m_qkv_done = [p.mark("sp_k"), p.mark("sp_v"), p.mark("sp_q"),
                      p.mark("sp_xnqT")]
        msk_sb = v8(AR1, 0, 32768, BF16, 32 * 512)
        m_msk = Dsy.dma("ld_msk", msk_sb, masks.ap(), waits=m_qkv_done)
        m_msk_last = None

        hd_war = [None, None]
        exp_war = {}
        av_war = {}
        ep_copies = []
        att_sp = [None, None]
        first_att = True
        OBASE = [0, 512, 1024, 1536]

        for h in range(NH):
            hb = h % 2
            m_k = Dsy.dma(f"ld_k{hb}", k12[hb], k12T_d.ap()[h],
                          waits=[k_sp, v_sp, q_sp] + _aslist(hd_war[hb]))
            m_v = Dsy.dma(f"ld_v{hb}",
                          vex[hb].rearrange("p (a f) -> p a f", a=32),
                          vext_d.ap()[h])
            m_q = Dsy.dma(f"ld_q{hb}", q12[hb], q12T_d.ap()[h])
            for jt in range(4):
                prev_ep = list(ep_copies)
                ep_copies = []
                msk_marks = {}
                last = None

                def emit_scores(kc):
                    sp = kc % 2
                    pst = [ps_ab, ps_cd][sp]
                    pbuf = p12[(kc % 4) // 2][:, 512 * sp:512 * (sp + 1)]
                    if kc == 0:
                        TE.wait(m_k)
                        TE.wait(m_q)
                        TE.wait(m_v)
                        for mk in prev_ep:
                            TE.wait(mk)
                    if first_att and kc < 2:
                        TE.wait(ps_war[sp])
                        TE.wait(tr_bank_war[0])
                        TE.wait(tr_bank_war[1])
                    TE.wait(exp_war.get(sp))
                    TE.mm(pst[:, 0:256], k12[hb][0:64, 128 * kc:128 * (kc + 1)],
                          q12[hb][0:64, 256 * jt:256 * (jt + 1)],
                          start=True, stop=True)
                    m_s2 = TE.mile("matmul", pst[:, 512:768],
                                   k12[hb][64:128, 128 * kc:128 * (kc + 1)],
                                   q12[hb][64:128, 256 * jt:256 * (jt + 1)],
                                   start=True, stop=True)
                    sview = pst[:].rearrange("p (b c) -> p b c", b=2)[:, :, 0:256]
                    m_exp = S.op("activation",
                                 pbuf.rearrange("p (b c) -> p b c", b=2),
                                 sview, AF.Exp, bias=-12.0, scale=0.125,
                                 waits=[m_s2, av_war.get(kc % 4)])
                    exp_war[sp] = m_exp
                    base = PC[jt] - 8
                    if kc >= base:
                        mslice = msk_sb[:, (jt * 8 + kc - base) * 512:
                                        (jt * 8 + kc - base + 1) * 512]
                        m2 = V.op("tensor_tensor", pbuf, pbuf, mslice,
                                  ALU.mult, noserial=True,
                                  waits=[m_exp, m_msk, av_war.get(kc % 4)])
                        msk_marks[kc] = m2
                    else:
                        msk_marks[kc] = m_exp

                def emit_av(kc):
                    nonlocal last
                    sp = kc % 2
                    pbuf = p12[(kc % 4) // 2][:, 512 * sp:512 * (sp + 1)]
                    TE.wait(msk_marks[kc])
                    for mat in range(2):
                        for sub in range(2):
                            o = OBASE[2 * mat + sub]
                            kw = dict(start=(kc == 0), stop=(kc == PC[jt] - 1))
                            lhsT = pbuf[:, 256 * mat + 128 * sub:
                                        256 * mat + 128 * (sub + 1)]
                            rhs = vex[hb][:, 132 * kc:132 * kc + 129]
                            if (mat, sub) == (1, 1):
                                last = TE.mile("matmul", ps_big[:, o:o + 129],
                                               lhsT, rhs, **kw)
                            else:
                                TE.mm(ps_big[:, o:o + 129], lhsT, rhs, **kw)
                    av_war[kc % 4] = last

                for kc in range(PC[jt] + 1):
                    if kc < PC[jt]:
                        emit_scores(kc)
                    if kc >= 1:
                        emit_av(kc - 1)
                m_msk_last = msk_marks[PC[jt] - 1]
                first_att = False
                m_av_done = last
                eps = []
                for sub in range(2):
                    c0 = 8 * sub
                    o1 = OBASE[sub]
                    o2 = OBASE[2 + sub]
                    t2v = scr[0][:, 128 * sub:128 * (sub + 1)]
                    dfv = scr[1][:, 128 * sub:128 * (sub + 1)]
                    sqv = scr[2][:, 128 * sub:128 * (sub + 1)]
                    onv = (scr[3][:, 128 * sub:128 * (sub + 1)]
                           .bitcast(BF16)[:, 0:128])
                    V.op("reciprocal", ep_st[:, c0:c0 + 1],
                         ps_big[:, o1 + 128:o1 + 129],
                         waits=[m_av_done, m_lam] if sub == 0 else None)
                    V.op("reciprocal", ep_st[:, c0 + 1:c0 + 2],
                         ps_big[:, o2 + 128:o2 + 129])
                    V.op("tensor_tensor", ep_st[:, c0 + 2:c0 + 3],
                         ep_st[:, c0 + 1:c0 + 2],
                         lam_sb[:, 0:1], ALU.mult)
                    V.op("tensor_scalar", t2v,
                         ps_big[:, o2:o2 + 128],
                         ep_st[:, c0 + 2:c0 + 3], None, ALU.mult)
                    m_diff = V.op("scalar_tensor_tensor", dfv,
                                  ps_big[:, o1:o1 + 128],
                                  ep_st[:, c0:c0 + 1], t2v, ALU.mult,
                                  ALU.subtract)
                    S.op("activation", sqv, dfv, AF.Square,
                         accum_out=ep_st[:, c0 + 3:c0 + 4], waits=[m_diff])
                    S.op("activation", ep_st[:, c0 + 4:c0 + 5],
                         ep_st[:, c0 + 3:c0 + 4], AF.Ln,
                         bias=EPS, scale=1.0 / VD)
                    m_rs = S.op("activation", ep_st[:, c0 + 5:c0 + 6],
                                ep_st[:, c0 + 4:c0 + 5],
                                AF.Exp, bias=math.log(ONE_M_LI), scale=-0.5)
                    eps.append((m_rs, dfv, onv, c0))
                for sub in range(2):
                    m_rs, dfv, onv, c0 = eps[sub]
                    stp = sub % 2
                    m_on = V.op("tensor_scalar", onv, dfv,
                                ep_st[:, c0 + 5:c0 + 6], None, ALU.mult,
                                waits=[m_rs])
                    bank = [ps_b, ps_d][stp]
                    dstps = bank[:, 384:448].bitcast(BF16)
                    TE.wait(m_on)
                    TE.wait(tr_bank_war[stp])
                    m_tr = TE.mile("transpose", dstps, onv, id_b[:])
                    m_cp = V.op("tensor_copy", att_st[stp][:], dstps,
                                waits=[m_tr, att_sp[stp]])
                    qs = 2 * jt + sub
                    att_sp[stp] = Dsc.dma("sp_att",
                                          attnT_d.ap()[qs, :, h, :],
                                          att_st[stp][:], waits=[m_cp])
                    ep_copies.append(m_cp)
                    tr_bank_war[stp] = m_cp
            hd_war[hb] = TE.fence()
            for _ in range(16):
                emit_dq_unit()
        while dq_state["s2"] < len(dq_units):
            emit_dq_unit()

        m_att_all = p.mark("sp_att")
        aq_war = [None, None]
        sev_war = None
        m_sp_s = None
        prev_guard = list(ep_copies)
        hd_fence = TE.fence()
        for qs in range(8):
            bufp = qs % 2
            m_aq = Dsy.dma(f"ld_aq{bufp}",
                           att_qs[bufp].rearrange("p (a f) -> p a f", a=16),
                           attnT_d.ap()[qs],
                           waits=[m_att_all] + (hd_fence if qs < 2 else [])
                                 + _aslist(aq_war[bufp]))
            TE.wait(m_aq)
            if qs == 0:
                TE.wait(m_wp)
            for mk in prev_guard:
                TE.wait(mk)
            m_pj = None
            for h in range(NH):
                for c4 in range(4):
                    kw = dict(start=(h == 0), stop=(h == 15))
                    lhsT = att_qs[bufp][:, 128 * h:128 * (h + 1)]
                    rhs = wk_sb[:, 2048 * h + 512 * c4:2048 * h + 512 * (c4 + 1)]
                    if (h, c4) == (15, 3):
                        m_pj = TE.mile("matmul",
                                       ps_big[:, 512 * c4:512 * (c4 + 1)],
                                       lhsT, rhs, **kw)
                    else:
                        TE.mm(ps_big[:, 512 * c4:512 * (c4 + 1)], lhsT, rhs,
                              **kw)
            aq_war[bufp] = TE.fence()
            m_ev = V.op("tensor_copy", s_ev[0], ps_big[:],
                        waits=[m_pj, sev_war])
            m_sp_s = Dsc.dma("sp_s", s_d.ap()[128 * qs:128 * (qs + 1), :],
                             s_ev[0], waits=[m_ev])
            sev_war = m_sp_s
            prev_guard = [m_ev]
        m_proj_ev = prev_guard[0]

        m_dq_all = [p.mark("sp_dq0a"), p.mark("sp_dq0b"),
                    p.mark("sp_dq1a"), p.mark("sp_dq1b")]
        xnq_all = v8(AR0, 0, 32768, BF16, 16 * NQ)
        int_st = [v8(AR0, 32768, 2048, BF16, NQ),
                  v8(AR0, 34816, 2048, BF16, NQ)]
        glA = [v8(AR0, 36864 + 4096 * i, 4096, F32, NQ) for i in range(4)]
        in_ch = [v8(AR0, 53248, 512, BF16, 256),
                 v8(AR0, 53760, 512, BF16, 256)]

        m_p5 = TE.fence()
        for g in range(16):
            J2, par2 = divmod(g, 2)
            Dsy.dma("ld_xq",
                    xnq_all[:, NQ * g:NQ * (g + 1)],
                    xnqT_d.ap()[256 * J2 + par2:256 * J2 + 256:2, 0:NQ],
                    waits=[p.mark("sp_xnqT"), p.mark("sp_s"), m_p5]
                          if g == 0 else ())
        m_xq = p.mark("ld_xq")

        wg_war = [None, None]
        apsum_war = {0: None, 1: None}
        int_sp_war = [None, None]
        m_int_sp = [None] * 128
        for ng in range(32):
            wpar = ng % 2
            wl = (m_dq_all + [m_msk_last]) if ng < 2 else []
            wl = wl + _aslist(wg_war[wpar])
            pairs_g = [(wg_grp[wpar][:, 512 * g:512 * (g + 1)],
                        wgT_d.ap()[g, :, 512 * ng:512 * (ng + 1)])
                       for g in range(16)]
            pairs_u = [(wu_grp[wpar][:, 512 * g:512 * (g + 1)],
                        wuT_d.ap()[g, :, 512 * ng:512 * (ng + 1)])
                       for g in range(16)]
            m_wg = Dsy.dma_group(f"ld_wg{wpar}", pairs_g, waits=wl)
            m_wu = Dsy.dma_group(f"ld_wu{wpar}", pairs_u)
            TE.wait(m_xq)
            TE.wait(m_wg)
            TE.wait(m_wu)
            for k in range(4):
                nch = 4 * ng + k
                ppar = nch % 2
                if ppar == 0:
                    pgv, puv = ps_ab[:], ps_cd[:]
                else:
                    pgv, puv = ps_big[:, 0:1024], ps_big[:, 1024:2048]
                TE.wait(apsum_war[ppar])
                if nch < 2:
                    TE.wait(m_proj_ev)
                m_g = m_u2 = None
                for half in range(2):
                    for g in range(16):
                        kw = dict(start=(g == 0), stop=(g == 15))
                        lhsT = wg_grp[wpar][:, 512 * g + 128 * k:
                                            512 * g + 128 * (k + 1)]
                        rhs = xnq_all[:, NQ * g + 512 * half:
                                      NQ * g + 512 * (half + 1)]
                        dst = pgv[:, 512 * half:512 * (half + 1)]
                        if (half, g) == (1, 15):
                            m_g = TE.mile("matmul", dst, lhsT, rhs, **kw)
                        else:
                            TE.mm(dst, lhsT, rhs, **kw)
                for half in range(2):
                    for g in range(16):
                        kw = dict(start=(g == 0), stop=(g == 15))
                        lhsT = wu_grp[wpar][:, 512 * g + 128 * k:
                                            512 * g + 128 * (k + 1)]
                        rhs = xnq_all[:, NQ * g + 512 * half:
                                      NQ * g + 512 * (half + 1)]
                        dst = puv[:, 512 * half:512 * (half + 1)]
                        if (half, g) == (1, 15):
                            m_u2 = TE.mile("matmul", dst, lhsT, rhs, **kw)
                        else:
                            TE.mm(dst, lhsT, rhs, **kw)
                m_sq2 = S.op("activation", glA[0], pgv, AF.Square,
                             waits=[m_g])
                V.op("tensor_scalar", glA[1], glA[0], GC3, GC1,
                     ALU.mult, ALU.add, waits=[m_sq2])
                m_v3 = V.op("tensor_tensor", glA[1], glA[1], pgv, ALU.mult)
                m_t = S.op("activation", glA[2], glA[1], AF.Tanh,
                           waits=[m_v3])
                m_up = S.op("activation", glA[3], puv, AF.Identity,
                            scale=0.5, waits=[m_u2])
                m_gu = V.op("tensor_tensor", glA[3], glA[3], pgv, ALU.mult,
                            waits=[m_up])
                ipar = nch % 2
                m_int = V.op("scalar_tensor_tensor", int_st[ipar],
                             glA[2], 1.0, glA[3], ALU.add, ALU.mult,
                             waits=[m_t, int_sp_war[ipar]])
                apsum_war[ppar] = [m_gu, m_int]
                m_sp_i = Dsc.dma(f"sp_int{ipar}", inter_d.ap()[nch],
                                 int_st[ipar], waits=[m_int])
                m_int_sp[nch] = m_sp_i
                int_sp_war[ipar] = m_sp_i
            wg_war[wpar] = TE.fence()

        wd_war = [None, None]
        ic_war = [None, None]
        sil_war = None
        fin_v_war = None
        out_war = None
        a_done = _aslist(apsum_war[0]) + _aslist(apsum_war[1])
        for tb in range(4):
            last_d = None
            for nch in range(128):
                wpar = nch % 2
                m_wd = Dsy.dma(f"ld_wd{wpar}", wd_st[wpar],
                               wdT_d.ap()[nch],
                               waits=_aslist(wd_war[wpar]) +
                                     (m_dq_all if (tb == 0 and nch < 2)
                                      else []))
                m_ic = Dgp.dma(f"ld_ic{wpar}", in_ch[wpar],
                               inter_d.ap()[nch, :, 256 * tb:256 * (tb + 1)],
                               waits=_aslist(ic_war[wpar]) +
                                     ([m_int_sp[nch]] if tb == 0 else []))
                TE.wait(m_wd)
                TE.wait(m_ic)
                if nch == 0:
                    TE.wait(sil_war)
                    if tb == 0:
                        for mk in a_done:
                            TE.wait(mk)
                for tsub in range(2):
                    for c4 in range(4):
                        kw = dict(start=(nch == 0), stop=(nch == 127))
                        lhsT = in_ch[wpar][:, 128 * tsub:128 * (tsub + 1)]
                        rhs = wd_st[wpar][:, 512 * c4:512 * (c4 + 1)]
                        dst = (ps_big[:, 512 * c4:512 * (c4 + 1)] if tsub == 0
                               else ps4[c4][:])
                        if (tsub, c4) == (1, 3):
                            last_d = TE.mile("matmul", dst, lhsT, rhs, **kw)
                        else:
                            TE.mm(dst, lhsT, rhs, **kw)
                fnc = TE.fence()
                wd_war[wpar] = fnc
                ic_war[wpar] = fnc
            m_down = last_d
            for tsub in range(2):
                r0 = 256 * tb + 128 * tsub
                m_fx = Dsy.dma("ld_fx", fin_x, x_q.ap()[r0:r0 + 128, :],
                               waits=[fin_v_war,
                                      m_down if (tb == 0 and tsub == 0) else None])
                m_fs = Dsy.dma("ld_fs", fin_s, s_d.ap()[r0:r0 + 128, :],
                               waits=[m_sp_s] if (tb == 0 and tsub == 0) else ())
                if tsub == 0:
                    m_sil = S.op("activation", fin_sil, ps_big[:], AF.Silu,
                                 waits=[m_down])
                else:
                    for c4 in range(4):
                        m_sil = S.op("activation",
                                     fin_sil[:, 512 * c4:512 * (c4 + 1)],
                                     ps4[c4][:], AF.Silu,
                                     waits=[m_down] if c4 == 0 else ())
                V.op("tensor_tensor", fin_o, fin_sil, fin_s,
                     ALU.mult, waits=[m_sil, m_fs, m_fx, out_war])
                m_fin = V.op("tensor_tensor", fin_o, fin_o, fin_x, ALU.add)
                fin_v_war = m_fin
                out_war = Dsc.dma("sp_out", out.ap()[r0:r0 + 128, :],
                                  fin_o, waits=[m_fin])
            sil_war = m_sil

        p.wait("sync", p.mark("sp_out"))

        with nc.Block() as block:
            @block.tensor
            def _(eng):
                for f in p.body["tensor"]:
                    f(eng)

            @block.vector
            def _(eng):
                for f in p.body["vector"]:
                    f(eng)

            @block.scalar
            def _(eng):
                for f in p.body["scalar"]:
                    f(eng)

            @block.gpsimd
            def _(eng):
                for f in p.body["gpsimd"]:
                    f(eng)

            @block.sync
            def _(eng):
                for f in p.body["sync"]:
                    f(eng)

        res = run_bass_kernel_spmd(nc, in_maps, core_ids=list(range(N_CORES)),
                                   trace=trace)
    return res


def kernel(**inputs):
    in_maps = _host_inputs(inputs)
    res = build_and_run(in_maps)
    full = np.empty((2, T, D), np.float32)
    for c in range(N_CORES):
        b, q = divmod(c, 4)
        oc = np.asarray(res.results[c]["out"])
        for jt, t in enumerate(TILES[q]):
            full[b, 256 * t:256 * (t + 1)] = oc[256 * jt:256 * (jt + 1)]
    return full


# version 13
# speedup vs baseline: 1.4206x; 1.1935x over previous
import math
import numpy as np
import concourse.bass as bass
import concourse.mybir as mybir
from concourse.bass_utils import run_bass_kernel_spmd

F32 = mybir.dt.float32
BF16 = mybir.dt.bfloat16
U8 = mybir.dt.uint8
I16 = mybir.dt.int16
AF = mybir.ActivationFunctionType
ALU = mybir.AluOpType
AX = mybir.AxisListType

D = 2048
T = 4096
NQ = 1024
NH = 16
VD = 128
NF = 16384
LAMBDA_INIT = 0.8 - 0.6 * math.exp(-0.3)
ONE_M_LI = 1.0 - LAMBDA_INIT
GC1 = 0.7978845608028654
GC3 = 0.7978845608028654 * 0.044715
EPS = 1e-5

CONST_VALS = [0.0, 1.0, 0.125, -12.0, 0.5, -0.25, EPS, 1.0 / D, 1.0 / VD,
              -0.5, math.log(ONE_M_LI), GC1, GC3, LAMBDA_INIT, 0.0625,
              -0.499999, 63.5, -1.0 / 256.0]

N_CORES = 8
TILES = [[c4, 4 + c4, 11 - c4, 15 - c4] for c4 in range(4)]
PC = [8, 16, 24, 32]
MOFF = [0, 8, 24, 48]
NCH_TOT = 80



def _nperm():
    p = []
    for b in range(NF // 256):
        p.extend(range(256 * b, 256 * (b + 1), 2))
        p.extend(range(256 * b + 1, 256 * (b + 1), 2))
    return np.array(p)


def _pack_u8(a):
    a = np.ascontiguousarray(np.asarray(a, np.int32))
    return np.ascontiguousarray(a.view(np.uint8).reshape(*a.shape, 4)[..., 0])


def _chunkmajor(w):
    Dr, C = w.shape
    return np.ascontiguousarray(w.reshape(Dr // 128, 128, C).transpose(1, 0, 2)
                                .reshape(128, (Dr // 128) * C))


def _host_inputs(inputs):
    x = np.asarray(inputs["L0_attn_x"], np.float32).reshape(2, T, D)
    wqkv = np.asarray(inputs["L0_attn_Wqkv"], np.float32)
    wproj = np.asarray(inputs["L0_attn_Wproj"], np.float32)

    kcols, vcols, qcols = [], [], []
    for h in range(NH):
        kcols.extend(range(1024 + 64 * h, 1024 + 64 * h + 64))
        kcols.extend(range(3072 + 64 * h, 3072 + 64 * h + 64))
    for h in range(NH):
        vcols.extend(range(4096 + 128 * h, 4096 + 128 * h + 128))
    for h in range(NH):
        qcols.extend(range(64 * h, 64 * h + 64))
        qcols.extend(range(2048 + 64 * h, 2048 + 64 * h + 64))
    wqkv_k = _chunkmajor(np.ascontiguousarray(wqkv[:, kcols]))
    wqkv_v = _chunkmajor(np.ascontiguousarray(wqkv[:, vcols]))
    wqkv_q = _chunkmajor(np.ascontiguousarray(wqkv[:, qcols]))
    wproj_cm = _chunkmajor(np.ascontiguousarray(wproj))

    lamv = np.zeros((128, 256), np.float32)
    lamv[:, 0:64] = np.asarray(inputs["L0_attn_lambda_q1"], np.float32)
    lamv[:, 64:128] = np.asarray(inputs["L0_attn_lambda_q2"], np.float32)
    lamv[:, 128:192] = np.asarray(inputs["L0_attn_lambda_k1"], np.float32)
    lamv[:, 192:256] = np.asarray(inputs["L0_attn_lambda_k2"], np.float32)

    import ml_dtypes
    consts = np.zeros((128, 18), np.float32)
    consts[:, :len(CONST_VALS)] = CONST_VALS
    ident = np.eye(128, dtype=ml_dtypes.bfloat16)

    perm = _nperm()

    def _gu_blocks(pk_t):
        a = pk_t.reshape(8, 128, 32, 512)
        return np.ascontiguousarray(a.transpose(2, 1, 0, 3).reshape(32, 128, 4096))

    gate_pk3 = _gu_blocks(_pack_u8(inputs["L0_ffn_gate_W_packed"])[perm, :].T)
    up_pk3 = _gu_blocks(_pack_u8(inputs["L0_ffn_up_W_packed"])[perm, :].T)
    d8 = _pack_u8(inputs["L0_ffn_down_W_packed"]).T
    down_pk3 = np.ascontiguousarray(
        d8.reshape(32, 2, 128, 2, 1024).transpose(0, 2, 1, 3, 4)
        .reshape(32, 128, 4096))
    gsc_t = np.asarray(inputs["L0_ffn_gate_scales"], np.float32)[perm, :].T
    usc_t = np.asarray(inputs["L0_ffn_up_scales"], np.float32)[perm, :].T
    gu_sc = np.ascontiguousarray(np.concatenate([gsc_t, usc_t], 0))
    dsc_t = np.asarray(inputs["L0_ffn_down_scales"], np.float32).T
    ds_sc = np.ascontiguousarray(
        dsc_t.reshape(4, 128, 2048).transpose(1, 0, 2).reshape(128, 8192))

    shared = {
        "wqkv_k": wqkv_k, "wqkv_v": wqkv_v, "wqkv_q": wqkv_q, "wproj": wproj_cm,
        "lamv": lamv, "consts": consts, "ident": ident,
        "gate_pk3": gate_pk3, "up_pk3": up_pk3, "down_pk3": down_pk3,
        "gu_sc": gu_sc, "ds_sc": ds_sc,
    }

    kt = np.arange(128)
    qt = np.arange(256)
    in_maps = []
    for c in range(N_CORES):
        b, c4 = divmod(c, 4)
        tiles = TILES[c4]
        m = np.zeros((32, 128, 512), ml_dtypes.bfloat16)
        for jt in range(4):
            for i8 in range(8):
                kc = PC[jt] - 8 + i8
                m1 = ((128 * kc + kt[:, None])
                      <= (256 * tiles[jt] + qt[None, :256])).astype(np.float32)
                m[jt * 8 + i8] = np.concatenate([m1, m1], 1)
        d = dict(shared)
        d["x_kv"] = np.ascontiguousarray(x[b])
        d["x_q"] = np.ascontiguousarray(
            np.concatenate([x[b, 256 * t:256 * (t + 1)] for t in tiles], 0))
        d["masks"] = np.ascontiguousarray(m.transpose(1, 0, 2).reshape(128, 32 * 512))
        in_maps.append(d)
    return in_maps



def _aslist(w):
    if w is None:
        return []
    if isinstance(w, tuple) and len(w) == 2 and isinstance(w[0], str):
        return [w]
    out = []
    for m in w:
        out.extend(_aslist(m))
    return out


class Prog:
    ENGINES = ("tensor", "vector", "scalar", "gpsimd", "sync")

    def __init__(self, nc, ctx):
        self.nc = nc
        self.ctx = ctx
        self.body = {e: [] for e in self.ENGINES}
        self.sems = {}
        self.waited = {e: {} for e in self.ENGINES}

    def sem(self, name):
        if name not in self.sems:
            h = self.ctx.enter_context(self.nc.semaphore(name))
            self.sems[name] = [h, 0]
        return self.sems[name]

    def wait(self, eng, mark):
        for m in _aslist(mark):
            if m[1] <= 0 or self.waited[eng].get(m[0], 0) >= m[1]:
                continue
            self.waited[eng][m[0]] = m[1]
            h = self.sems[m[0]][0]
            self.body[eng].append(lambda e, h=h, v=m[1]: e.wait_ge(h, v))

    def op(self, eng, fname, *a, inc=None, **k):
        if inc is not None:
            s = self.sem(inc[0])
            s[1] += inc[1]
            h, val = s[0], s[1]

            def run(e, fname=fname, a=a, k=k, h=h, amt=inc[1]):
                getattr(e, fname)(*a, **k).then_inc(h, amt)
            self.body[eng].append(run)
            return (inc[0], val)

        def run(e, fname=fname, a=a, k=k):
            getattr(e, fname)(*a, **k)
        self.body[eng].append(run)
        return None

    def mark(self, name):
        return (name, self.sems[name][1]) if name in self.sems else (name, 0)


SELF_SERIAL = True


class SerialEngine:
    def __init__(self, prog, eng, semname, serial=True):
        self.p, self.eng, self.semname = prog, eng, semname
        self.serial = serial
        self.p.sem(semname)

    def __init2__(self):
        pass

    def op(self, fname, *a, waits=(), noserial=False, **k):
        self.p.wait(self.eng, waits)
        if self.serial and not noserial:
            n = self.p.sems[self.semname][1]
            if n > 0:
                self.p.wait(self.eng, (self.semname, n))
        return self.p.op(self.eng, fname, *a, inc=(self.semname, 1), **k)

    def mark(self):
        return self.p.mark(self.semname)


class DmaQueue:
    def __init__(self, prog, eng):
        self.p, self.eng = prog, eng

    def dma(self, semname, out, in_, waits=(), transpose=False):
        self.p.wait(self.eng, waits)
        n = self.p.sems[semname][1] if semname in self.p.sems else 0
        if n > 0:
            self.p.wait(self.eng, (semname, n))
        if transpose:
            return self.p.op(self.eng, "dma_start_transpose", out, in_,
                             inc=(semname, 16))
        return self.p.op(self.eng, "dma_start", out=out, in_=in_,
                         inc=(semname, 16))

    def dma_group(self, semname, pairs, waits=()):
        self.p.wait(self.eng, waits)
        for out, in_ in pairs:
            self.p.op(self.eng, "dma_start", out=out, in_=in_, inc=(semname, 16))
        return self.p.mark(semname)


class TensorEng:
    def __init__(self, prog):
        self.p = prog
        self.rot = 0
        self.p.sem("st0")
        self.p.sem("st1")

    def wait(self, mark):
        self.p.wait("tensor", mark)

    def mm(self, *a, **k):
        self.p.op("tensor", "matmul", *a, **k)

    def mile(self, fname, *a, **k):
        name = f"st{self.rot}"
        self.rot ^= 1
        n = self.p.sems[name][1]
        if n > 0:
            self.p.wait("tensor", (name, n))
        return self.p.op("tensor", fname, *a, inc=(name, 1), **k)

    def fence(self):
        return [self.p.mark("st0"), self.p.mark("st1")]



def build_and_run(in_maps, trace=False):
    from contextlib import ExitStack
    nc = bass.Bass("TRN2", target_bir_lowering=False)

    di = lambda n, s, dt: nc.dram_tensor(n, s, dt, kind="ExternalInput")
    dd = lambda n, s, dt: nc.dram_tensor(n, s, dt, kind="Internal")
    x_kv = di("x_kv", [T, D], F32)
    x_q = di("x_q", [NQ, D], F32)
    wqkv_k = di("wqkv_k", [128, 16 * 2048], F32)
    wqkv_v = di("wqkv_v", [128, 16 * 2048], F32)
    wqkv_q = di("wqkv_q", [128, 16 * 2048], F32)
    wproj = di("wproj", [128, 16 * 2048], F32)
    lamv = di("lamv", [128, 256], F32)
    consts = di("consts", [128, 18], F32)
    ident = di("ident", [128, 128], BF16)
    masks = di("masks", [128, 32 * 512], BF16)
    gate_pk3 = di("gate_pk3", [32, 128, 4096], U8)
    up_pk3 = di("up_pk3", [32, 128, 4096], U8)
    down_pk3 = di("down_pk3", [32, 128, 4096], U8)
    gu_sc = di("gu_sc", [128, NF], F32)
    ds_sc = di("ds_sc", [128, 8192], F32)
    out = nc.dram_tensor("out", [NQ, D], F32, kind="ExternalOutput")

    xn_kv_d = dd("xn_kv_d", [T, D], BF16)
    xn_q_d = dd("xn_q_d", [NQ, D], BF16)
    xnqT_d = dd("xnqT_d", [D, NQ], BF16)
    k12T_d = dd("k12T_d", [NH, 128, T], BF16)
    vext_d = dd("vext_d", [NH, 128, 32, 132], BF16)
    q12T_d = dd("q12T_d", [NH, 128, NQ], BF16)
    s_d = dd("s_d", [NQ, D], F32)
    attnT_d = dd("attnT_d", [8, 128, NH, 128], BF16)
    inter_d = dd("inter_d", [128, 128, NQ], BF16)

    ctx = ExitStack()
    with ctx:
        sb = lambda n, s, dt: ctx.enter_context(nc.sbuf_tensor(n, s, dt))
        psal = lambda n, s, dt: ctx.enter_context(nc.psum_tensor(n, s, dt))

        AR0 = sb("AR0", [128, 65536], U8)
        AR1 = sb("AR1", [128, 65536], U8)
        AR2 = sb("AR2", [128, 49152], U8)
        AR3 = sb("AR3", [128, 22528], U8)

        def v8(ar, off, nbytes, dt, cols):
            return ar[:, off:off + nbytes].bitcast(dt)[:, 0:cols]

        wk_sb = v8(AR0, 0, 65536, BF16, 16 * 2048)
        wv_sb = wk_sb
        wq_sb = wk_sb
        wp_sb = wk_sb

        xt = [v8(AR1, 0, 8192, F32, D), v8(AR1, 8192, 8192, F32, D)]
        lnsq = v8(AR1, 16384, 8192, F32, D)
        xnt = [v8(AR1, 24576, 4096, BF16, D), v8(AR1, 28672, 4096, BF16, D)]
        xnT_sb = [v8(AR2, 0, 16384, BF16, 16 * 512),
                  v8(AR2, 16384, 16384, BF16, 16 * 512)]
        qkv_ev = [v8(AR1, 0, 1024, BF16, 512), v8(AR1, 1024, 1024, BF16, 512)]
        vtr_ev = [v8(AR1, 2048, 1024, BF16, 512), v8(AR1, 3072, 1024, BF16, 512)]
        vex_ev = [v8(AR1, 4096, 1088, BF16, 4 * 132), v8(AR1, 5248, 1088, BF16, 4 * 132)]

        k12 = [v8(AR2, 0, 8192, BF16, T), v8(AR2, 8192, 8192, BF16, T)]
        vex = [v8(AR2, 16384, 8448, BF16, 32 * 132), v8(AR2, 24832, 8448, BF16, 32 * 132)]
        q12 = [v8(AR2, 33280, 2048, BF16, NQ), v8(AR2, 35328, 2048, BF16, NQ)]
        att_qs = [v8(AR2, 0, 4096, BF16, D), v8(AR2, 4096, 4096, BF16, D)]
        s_ev = [v8(AR2, 8192, 8192, F32, D)]

        cst = sb("cst", [128, 18], F32)
        for i, v in enumerate(CONST_VALS):
            nc.const_aps.register(F32, v, cst[:, i:i + 1])
        lam_sb = sb("lam_sb", [128, 8], F32)
        lamv_sb = v8(AR1, 16384, 1024, F32, 256)
        lamx = v8(AR1, 20480, 1024, F32, 256)
        id_b = sb("id_b", [128, 128], BF16)
        lnst = sb("lnst", [128, 8], F32)
        ep_st = sb("ep_st", [128, 16], F32)
        p12 = [sb(f"p12_{i}", [128, 1024], BF16) for i in range(2)]
        scr = [sb(f"scr{i}", [128, 256], F32) for i in range(4)]
        ep_t2 = scr[0][:, 0:128]
        ep_diff = scr[1][:, 0:128]
        ep_sq = scr[2][:, 0:128]
        ep_on = scr[3][:, 0:128].bitcast(BF16)[:, 0:128]
        gl_sq, gl_v2, gl_v3, gl_t = (s[:] for s in scr)
        gl_up = scr[2][:]
        att_st = [sb(f"att_st{i}", [128, 128], BF16) for i in range(2)]

        ps_ab = psal("ps_ab", [128, 1024], F32)
        ps_cd = psal("ps_cd", [128, 1024], F32)
        ps_big = psal("ps_big", [128, 2048], F32)

        class _PV:
            def __init__(self, t, lo):
                self.t, self.lo = t, lo
            def __getitem__(self, key):
                assert isinstance(key, tuple) or key == slice(None)
                if not isinstance(key, tuple):
                    return self.t[:, self.lo:self.lo + 512]
                p, f = key
                assert p == slice(None)
                return self.t[:, self.lo + (f.start or 0):self.lo + f.stop]
        ps_a = _PV(ps_ab, 0)
        ps_b = _PV(ps_ab, 512)
        ps_c = _PV(ps_cd, 0)
        ps_d = _PV(ps_cd, 512)
        ps4 = [ps_a, ps_b, ps_c, ps_d]

        p = Prog(nc, ctx)
        V = SerialEngine(p, "vector", "sv")
        S = SerialEngine(p, "scalar", "ss", serial=False)
        G = SerialEngine(p, "gpsimd", "sg", serial=False)
        TE = TensorEng(p)
        Dsy = DmaQueue(p, "sync")
        Dsc = DmaQueue(p, "scalar")
        Dgp = DmaQueue(p, "gpsimd")

        m_ini = Dsy.dma_group("ld_ini", [
            (cst[:], consts.ap()),
            (id_b[:], ident.ap()),
            (lamv_sb, lamv.ap()),
        ])
        m_wk = Dgp.dma("ld_w0", wk_sb, wqkv_k.ap())

        S.op("copy", lam_sb[:, 1:2], cst[:, 0:1], waits=[m_ini])

        V.op("tensor_tensor", lamx[:, 0:128], lamv_sb[:, 0:128],
             lamv_sb[:, 128:256], ALU.mult, waits=[m_ini])
        V.op("reduce_sum", lam_sb[:, 2:3], lamx[:, 0:64], axis=AX.X)
        m_d12 = V.op("reduce_sum", lam_sb[:, 3:4], lamx[:, 64:128], axis=AX.X)
        S.op("activation", lam_sb[:, 4:5], lam_sb[:, 2:3], AF.Exp,
             waits=[m_d12])
        m_e2 = S.op("activation", lam_sb[:, 5:6], lam_sb[:, 3:4], AF.Exp)
        V.op("tensor_tensor", lam_sb[:, 6:7], lam_sb[:, 4:5],
             lam_sb[:, 5:6], ALU.subtract, waits=[m_e2])
        m_lam = V.op("tensor_scalar", lam_sb[:, 0:1], lam_sb[:, 6:7],
                     LAMBDA_INIT, None, ALU.add)

        sp_xn_mark = [None, None]
        ln_v_mark = [None, None]
        for i in range(40):
            par = i % 2
            if i < 32:
                src_ = x_kv.ap()[128 * i:128 * (i + 1), :]
                dst = xn_kv_d.ap()[128 * i:128 * (i + 1), :]
            else:
                j = i - 32
                src_ = x_q.ap()[128 * j:128 * (j + 1), :]
                dst = xn_q_d.ap()[128 * j:128 * (j + 1), :]
            m_ld = Dsy.dma(f"ld_x{par}", xt[par], src_,
                           waits=[ln_v_mark[par], m_lam if i == 0 else None])
            V.op("reduce_sum", lnst[:, 0:1], xt[par], axis=AX.X,
                 waits=[m_ld])
            m_sq = S.op("activation", lnsq, xt[par], AF.Square,
                        accum_out=lnst[:, 1:2], waits=[m_ld, V.mark()])
            V.op("tensor_scalar", lnst[:, 2:3], lnst[:, 0:1], 1.0 / D, None,
                 ALU.mult, waits=[m_sq])
            V.op("tensor_tensor", lnst[:, 3:4], lnst[:, 2:3], lnst[:, 2:3],
                 ALU.mult)
            V.op("tensor_scalar", lnst[:, 4:5], lnst[:, 1:2], 1.0 / D, EPS,
                 ALU.mult, ALU.add)
            m_var = V.op("tensor_tensor", lnst[:, 5:6], lnst[:, 4:5],
                         lnst[:, 3:4], ALU.subtract)
            S.op("activation", lnst[:, 6:7], lnst[:, 5:6], AF.Ln,
                 waits=[m_var])
            m_rstd = S.op("activation", lnst[:, 7:8], lnst[:, 6:7], AF.Exp,
                          scale=-0.5)
            m_xn = V.op("tensor_scalar", xnt[par], xt[par],
                        lnst[:, 2:3], lnst[:, 7:8], ALU.subtract, ALU.mult,
                        waits=[m_rstd, sp_xn_mark[par]])
            ln_v_mark[par] = m_xn
            sp_xn_mark[par] = Dsc.dma(f"sp_xn{par}", dst, xnt[par],
                                      waits=[m_xn])
        m_ln_done = [sp_xn_mark[0], sp_xn_mark[1]]
        m_ln_v = V.mark()

        qkv_ps = [ps_a, ps_c]
        ps_war = [None, None]
        tr_bank_war = [None, None]
        xnT_war = [None, None]
        kq_war = [None, None]
        vtr_war = [None, None]
        vexev_war = [None, None]
        k_sp = v_sp = q_sp = None
        m_vex_init = None

        def qkv_pass(tiles, xn_src, m_w, mode):
            nonlocal k_sp, v_sp, q_sp, m_vex_init
            for jj, j in enumerate(tiles):
                par = jj % 2
                t0 = (j - 8) * 512 if mode == "q" else j * 512
                wlist = list(m_ln_done) + _aslist(xnT_war[par])
                for c0 in range(16):
                    Dsy.dma(f"ld_xnT{par}",
                            xnT_sb[par][:, 512 * c0:512 * (c0 + 1)],
                            xn_src.ap()[t0:t0 + 512, 128 * c0:128 * (c0 + 1)],
                            waits=wlist if c0 == 0 else (), transpose=True)
                m_xnT = p.mark(f"ld_xnT{par}")
                if mode == "q":
                    for c0 in range(16):
                        Dsc.dma("sp_xnqT",
                                xnqT_d.ap()[128 * c0:128 * (c0 + 1), t0:t0 + 512],
                                xnT_sb[par][:, 512 * c0:512 * (c0 + 1)],
                                waits=[m_xnT] if c0 == 0 else ())
                TE.wait(m_xnT)
                TE.wait(m_w)
                for cc in range(16):
                    ppi = cc % 2
                    pp = qkv_ps[ppi]
                    TE.wait(ps_war[ppi])
                    m_mm = None
                    for hc in range(16):
                        kw = dict(start=(hc == 0), stop=(hc == 15))
                        lhsT = wk_sb[:, 2048 * hc + 128 * cc:
                                     2048 * hc + 128 * (cc + 1)]
                        rhs = xnT_sb[par][:, 512 * hc:512 * (hc + 1)]
                        if hc == 15:
                            m_mm = TE.mile("matmul", pp[:], lhsT, rhs, **kw)
                        else:
                            TE.mm(pp[:], lhsT, rhs, **kw)
                    bufp = cc % 2
                    if mode in ("k", "q"):
                        m_ev = S.op("copy", qkv_ev[bufp], pp[:],
                                    waits=[m_mm, kq_war[bufp],
                                           m_ln_v if (jj == 0 and cc < 2) else None])
                        ps_war[ppi] = m_ev
                        if mode == "q":
                            m_sp = Dsc.dma("sp_q", q12T_d.ap()[cc, :, t0:t0 + 512],
                                           qkv_ev[bufp], waits=[m_ev])
                            q_sp = m_sp
                        else:
                            m_sp = Dsc.dma("sp_k", k12T_d.ap()[cc, :, t0:t0 + 512],
                                           qkv_ev[bufp], waits=[m_ev])
                            k_sp = m_sp
                        kq_war[bufp] = m_sp
                    else:
                        h = cc
                        m_ev = S.op("copy", vtr_ev[bufp], pp[:],
                                    waits=[m_mm, vtr_war[bufp]])
                        ps_war[ppi] = m_ev
                        cp_marks = [None] * 4
                        for kk in range(4):
                            bank = [ps_b, ps_d][kk % 2]
                            dstps = bank[:, 64 * (kk // 2):
                                         64 * (kk // 2 + 1)].bitcast(BF16)
                            if kk == 0:
                                TE.wait(m_ev)
                                TE.wait(m_ini)
                            TE.wait(cp_marks[kk - 2] if kk >= 2
                                    else tr_bank_war[kk % 2])
                            m_tr = TE.mile("transpose", dstps,
                                           vtr_ev[bufp][:, 128 * kk:128 * (kk + 1)],
                                           id_b[:])
                            cp_marks[kk] = S.op(
                                "copy", vex_ev[bufp][:, 132 * kk:132 * kk + 128],
                                dstps,
                                waits=[m_tr,
                                       vexev_war[bufp] if kk == 0 else None,
                                       m_vex_init if kk == 0 else None])
                        tr_bank_war[0] = cp_marks[2]
                        tr_bank_war[1] = cp_marks[3]
                        vtr_war[bufp] = cp_marks[3]
                        m_sp = Dsc.dma(
                            "sp_v", vext_d.ap()[h, :, 4 * j:4 * (j + 1), :],
                            vex_ev[bufp].rearrange("p (a f) -> p a f", a=4),
                            waits=[cp_marks[3]])
                        vexev_war[bufp] = m_sp
                        v_sp = m_sp
                xnT_war[par] = TE.fence()

        m_vex_init = None
        for i2 in range(2):
            V.op("memset", vex_ev[i2], 0.0, waits=[m_ln_v])
            for kk in range(4):
                m_vex_init = V.op("memset",
                                  vex_ev[i2][:, 132 * kk + 128:132 * kk + 129],
                                  1.0)
        qkv_pass(list(range(8)), xn_kv_d, m_wk, "k")
        m_wv = Dgp.dma("ld_w1", wk_sb, wqkv_v.ap(), waits=TE.fence())
        qkv_pass(list(range(8)), xn_kv_d, m_wv, "v")
        m_wq2 = Dgp.dma("ld_w2", wk_sb, wqkv_q.ap(), waits=TE.fence())
        qkv_pass([8, 9], xn_q_d, m_wq2, "q")
        m_wp = Dgp.dma("ld_w3", wk_sb, wproj.ap(), waits=TE.fence())

        def off(mat, sub):
            return (sub // 2 + 2 * mat) * 512 + (sub % 2) * 129

        m_qkv_done = [p.mark("sp_k"), p.mark("sp_v"), p.mark("sp_q"),
                      p.mark("sp_xnqT")]
        msk_sb = v8(AR1, 0, 32768, BF16, 32 * 512)
        m_msk = Dsy.dma("ld_msk", msk_sb, masks.ap(), waits=m_qkv_done)
        m_msk_last = None

        hd_war = [None, None]
        exp_war = {}
        av_war = {}
        ep_copies = []
        att_sp = [None, None]
        first_att = True
        OBASE = [0, 512, 1024, 1536]

        for h in range(NH):
            hb = h % 2
            m_k = Dsy.dma(f"ld_k{hb}", k12[hb], k12T_d.ap()[h],
                          waits=[k_sp, v_sp, q_sp] + _aslist(hd_war[hb]))
            m_v = Dsy.dma(f"ld_v{hb}",
                          vex[hb].rearrange("p (a f) -> p a f", a=32),
                          vext_d.ap()[h])
            m_q = Dsy.dma(f"ld_q{hb}", q12[hb], q12T_d.ap()[h])
            for jt in range(4):
                prev_ep = list(ep_copies)
                ep_copies = []
                msk_marks = {}
                last = None

                def emit_scores(kc):
                    sp = kc % 2
                    pst = [ps_ab, ps_cd][sp]
                    pbuf = p12[(kc % 4) // 2][:, 512 * sp:512 * (sp + 1)]
                    if kc == 0:
                        TE.wait(m_k)
                        TE.wait(m_q)
                        TE.wait(m_v)
                        for mk in prev_ep:
                            TE.wait(mk)
                    if first_att and kc < 2:
                        TE.wait(ps_war[sp])
                        TE.wait(tr_bank_war[0])
                        TE.wait(tr_bank_war[1])
                    TE.wait(exp_war.get(sp))
                    TE.mm(pst[:, 0:256], k12[hb][0:64, 128 * kc:128 * (kc + 1)],
                          q12[hb][0:64, 256 * jt:256 * (jt + 1)],
                          start=True, stop=True)
                    m_s2 = TE.mile("matmul", pst[:, 512:768],
                                   k12[hb][64:128, 128 * kc:128 * (kc + 1)],
                                   q12[hb][64:128, 256 * jt:256 * (jt + 1)],
                                   start=True, stop=True)
                    sview = pst[:].rearrange("p (b c) -> p b c", b=2)[:, :, 0:256]
                    m_exp = S.op("activation",
                                 pbuf.rearrange("p (b c) -> p b c", b=2),
                                 sview, AF.Exp, bias=-12.0, scale=0.125,
                                 waits=[m_s2, av_war.get(kc % 4)])
                    exp_war[sp] = m_exp
                    base = PC[jt] - 8
                    if kc >= base:
                        mslice = msk_sb[:, (jt * 8 + kc - base) * 512:
                                        (jt * 8 + kc - base + 1) * 512]
                        m2 = V.op("tensor_tensor", pbuf, pbuf, mslice,
                                  ALU.mult, noserial=True,
                                  waits=[m_exp, m_msk, av_war.get(kc % 4)])
                        msk_marks[kc] = m2
                    else:
                        msk_marks[kc] = m_exp

                def emit_av(kc):
                    nonlocal last
                    sp = kc % 2
                    pbuf = p12[(kc % 4) // 2][:, 512 * sp:512 * (sp + 1)]
                    TE.wait(msk_marks[kc])
                    for mat in range(2):
                        for sub in range(2):
                            o = OBASE[2 * mat + sub]
                            kw = dict(start=(kc == 0), stop=(kc == PC[jt] - 1))
                            lhsT = pbuf[:, 256 * mat + 128 * sub:
                                        256 * mat + 128 * (sub + 1)]
                            rhs = vex[hb][:, 132 * kc:132 * kc + 129]
                            if (mat, sub) == (1, 1):
                                last = TE.mile("matmul", ps_big[:, o:o + 129],
                                               lhsT, rhs, **kw)
                            else:
                                TE.mm(ps_big[:, o:o + 129], lhsT, rhs, **kw)
                    av_war[kc % 4] = last

                for kc in range(PC[jt] + 1):
                    if kc < PC[jt]:
                        emit_scores(kc)
                    if kc >= 1:
                        emit_av(kc - 1)
                m_msk_last = msk_marks[PC[jt] - 1]
                first_att = False
                m_av_done = last
                eps = []
                for sub in range(2):
                    c0 = 8 * sub
                    o1 = OBASE[sub]
                    o2 = OBASE[2 + sub]
                    t2v = scr[0][:, 128 * sub:128 * (sub + 1)]
                    dfv = scr[1][:, 128 * sub:128 * (sub + 1)]
                    sqv = scr[2][:, 128 * sub:128 * (sub + 1)]
                    onv = (scr[3][:, 128 * sub:128 * (sub + 1)]
                           .bitcast(BF16)[:, 0:128])
                    V.op("reciprocal", ep_st[:, c0:c0 + 1],
                         ps_big[:, o1 + 128:o1 + 129],
                         waits=[m_av_done, m_lam] if sub == 0 else None)
                    V.op("reciprocal", ep_st[:, c0 + 1:c0 + 2],
                         ps_big[:, o2 + 128:o2 + 129])
                    V.op("tensor_tensor", ep_st[:, c0 + 2:c0 + 3],
                         ep_st[:, c0 + 1:c0 + 2],
                         lam_sb[:, 0:1], ALU.mult)
                    V.op("tensor_scalar", t2v,
                         ps_big[:, o2:o2 + 128],
                         ep_st[:, c0 + 2:c0 + 3], None, ALU.mult)
                    m_diff = V.op("scalar_tensor_tensor", dfv,
                                  ps_big[:, o1:o1 + 128],
                                  ep_st[:, c0:c0 + 1], t2v, ALU.mult,
                                  ALU.subtract)
                    S.op("activation", sqv, dfv, AF.Square,
                         accum_out=ep_st[:, c0 + 3:c0 + 4], waits=[m_diff])
                    S.op("activation", ep_st[:, c0 + 4:c0 + 5],
                         ep_st[:, c0 + 3:c0 + 4], AF.Ln,
                         bias=EPS, scale=1.0 / VD)
                    m_rs = S.op("activation", ep_st[:, c0 + 5:c0 + 6],
                                ep_st[:, c0 + 4:c0 + 5],
                                AF.Exp, bias=math.log(ONE_M_LI), scale=-0.5)
                    eps.append((m_rs, dfv, onv, c0))
                for sub in range(2):
                    m_rs, dfv, onv, c0 = eps[sub]
                    stp = sub % 2
                    m_on = V.op("tensor_scalar", onv, dfv,
                                ep_st[:, c0 + 5:c0 + 6], None, ALU.mult,
                                waits=[m_rs])
                    bank = [ps_b, ps_d][stp]
                    dstps = bank[:, 384:448].bitcast(BF16)
                    TE.wait(m_on)
                    TE.wait(tr_bank_war[stp])
                    m_tr = TE.mile("transpose", dstps, onv, id_b[:])
                    m_cp = V.op("tensor_copy", att_st[stp][:], dstps,
                                waits=[m_tr, att_sp[stp]])
                    qs = 2 * jt + sub
                    att_sp[stp] = Dsc.dma("sp_att",
                                          attnT_d.ap()[qs, :, h, :],
                                          att_st[stp][:], waits=[m_cp])
                    ep_copies.append(m_cp)
                    tr_bank_war[stp] = m_cp
            hd_war[hb] = TE.fence()

        gu_sc_sb = v8(AR2, 16384, 32768, BF16, NF)
        ds_sc_sb = v8(AR2, 0, 16384, BF16, 8192)
        m_gusc = Dgp.dma("ld_gusc", gu_sc_sb, gu_sc.ap(),
                         waits=_aslist([hd_war[0], hd_war[1]]))

        m_att_all = p.mark("sp_att")
        aq_war = [None, None]
        sev_war = None
        m_sp_s = None
        prev_guard = list(ep_copies)
        hd_fence = TE.fence()
        for qs in range(8):
            bufp = qs % 2
            m_aq = Dsy.dma(f"ld_aq{bufp}",
                           att_qs[bufp].rearrange("p (a f) -> p a f", a=16),
                           attnT_d.ap()[qs],
                           waits=[m_att_all] + (hd_fence if qs < 2 else [])
                                 + _aslist(aq_war[bufp]))
            TE.wait(m_aq)
            if qs == 0:
                TE.wait(m_wp)
            for mk in prev_guard:
                TE.wait(mk)
            m_pj = None
            for h in range(NH):
                for c4 in range(4):
                    kw = dict(start=(h == 0), stop=(h == 15))
                    lhsT = att_qs[bufp][:, 128 * h:128 * (h + 1)]
                    rhs = wk_sb[:, 2048 * h + 512 * c4:2048 * h + 512 * (c4 + 1)]
                    if (h, c4) == (15, 3):
                        m_pj = TE.mile("matmul",
                                       ps_big[:, 512 * c4:512 * (c4 + 1)],
                                       lhsT, rhs, **kw)
                    else:
                        TE.mm(ps_big[:, 512 * c4:512 * (c4 + 1)], lhsT, rhs,
                              **kw)
            aq_war[bufp] = TE.fence()
            m_ev = V.op("tensor_copy", s_ev[0], ps_big[:],
                        waits=[m_pj, sev_war])
            m_sp_s = Dsc.dma("sp_s", s_d.ap()[128 * qs:128 * (qs + 1), :],
                             s_ev[0], waits=[m_ev])
            sev_war = m_sp_s
            prev_guard = [m_ev]
        m_proj_ev = prev_guard[0]

        m_p5_te = TE.fence()
        mb_v = V.mark()
        mb_s = S.mark()
        m_dssc = Dgp.dma("ld_dssc", ds_sc_sb, ds_sc.ap(),
                         waits=[m_sp_s, m_proj_ev] + _aslist(aq_war[0])
                               + _aslist(aq_war[1]))

        xnqA = v8(AR1, 0, 32768, BF16, 16 * NQ)
        glA = [v8(AR1, 32768 + 4096 * i, 4096, F32, NQ) for i in range(4)]
        c4A = [v8(AR3, 16384, 2048, I16, 1024),
               v8(AR1, 49152, 2048, I16, 1024)]
        s15A = [v8(AR3, 18432, 2048, I16, 1024),
                v8(AR1, 51200, 2048, I16, 1024)]
        rA = [v8(AR3, 20480, 2048, BF16, 1024),
              v8(AR1, 59392, 2048, BF16, 1024)]
        scb2A = [v8(AR1, 53248, 1024, BF16, 512),
                 v8(AR1, 54272, 1024, BF16, 512)]
        ievA = [v8(AR1, 55296, 2048, BF16, NQ), v8(AR1, 57344, 2048, BF16, NQ)]
        pkA = [[v8(AR3, 0, 4096, U8, 4096), v8(AR3, 4096, 4096, U8, 4096)],
               [v8(AR3, 8192, 4096, U8, 4096), v8(AR3, 12288, 4096, U8, 4096)]]
        gubuf = [[v8(AR0, 0, 16384, BF16, 8192),
                  v8(AR0, 16384, 16384, BF16, 8192)],
                 [v8(AR0, 32768, 16384, BF16, 8192),
                  v8(AR0, 49152, 16384, BF16, 8192)]]
        gubuf_i = [[v8(AR0, 0, 16384, I16, 8192),
                    v8(AR0, 16384, 16384, I16, 8192)],
                   [v8(AR0, 32768, 16384, I16, 8192),
                    v8(AR0, 49152, 16384, I16, 8192)]]

        for g in range(16):
            J2, par2 = divmod(g, 2)
            src_ = xnqT_d.ap()[256 * J2 + par2:256 * J2 + 256:2, 0:NQ]
            w0 = [p.mark("sp_xnqT"), m_msk_last] if g == 0 else []
            Dsy.dma("ld_xnq", xnqA[:, NQ * g:NQ * (g + 1)], src_, waits=w0)
        m_xnq = p.mark("ld_xnq")

        pmA = [dict(), dict()]
        guready = {}
        fenceA = {}
        pkA_marks = {}
        ispill = {}
        ievA_war = [None, None]
        psA_war = {0: [mb_v, mb_s], 1: [mb_v, mb_s]}
        sqA_war = [mb_v]

        def ffnA_s1(s, u):
            mat, J = u % 2, u // 2
            pm = pmA[mat]
            dbuf = s % 2
            pk = pkA[mat][dbuf][:, 512 * J:512 * (J + 1)]
            c4, s15, scb2 = c4A[mat], s15A[mat], scb2A[mat]
            src_sc = (gu_sc_sb[64 * mat + 8 * J:64 * mat + 8 * J + 8,
                               512 * s:512 * (s + 1)]
                      .unsqueeze(1).broadcast_to([8, 16, 512]))
            m_sc = Dsc.dma(f"ld_scA{mat}", scb2, src_sc,
                           waits=[pm.get("mul1"),
                                  m_gusc if s == 0 and J == 0 else None])
            m_chi = S.op("activation", c4[:, 512:1024], pk, AF.Identity,
                         scale=0.0625, bias=-0.499999,
                         waits=[pm.get("xb"), pkA_marks[(mat, dbuf)]])
            m_clo = V.op("scalar_tensor_tensor", c4[:, 0:512], c4[:, 512:1024],
                         -16.0, pk, ALU.mult, ALU.add, noserial=True,
                         waits=[m_chi, pm.get("xb")])
            m_s15 = V.op("tensor_scalar", s15, c4, 3, 15,
                         ALU.logical_shift_right, ALU.logical_shift_left,
                         noserial=True, waits=[m_clo, pm.get("orr")])
            m_and = V.op("tensor_single_scalar", c4, c4, 7, ALU.bitwise_and,
                         noserial=True, waits=[m_s15])
            xb_i = gubuf_i[mat][dbuf][:, 1024 * J:1024 * (J + 1)]
            wl = [m_and] + _aslist(fenceA.get(s - 2, m_p5_te))
            m_xb = V.op("tensor_scalar", xb_i, c4, 64, 16128,
                        ALU.mult, ALU.add, noserial=True, waits=wl)
            pm.update(sc=m_sc, chi=m_chi, clo=m_clo, s15=m_s15, xb=m_xb)

        def ffnA_s2(s, u):
            mat, J = u % 2, u // 2
            pm = pmA[mat]
            dbuf = s % 2
            r, s15, scb2 = rA[mat], s15A[mat], scb2A[mat]
            xb_i = gubuf_i[mat][dbuf][:, 1024 * J:1024 * (J + 1)]
            xb_bf = gubuf[mat][dbuf][:, 1024 * J:1024 * (J + 1)]
            m_r = S.op("activation", r, xb_i, AF.Relu, bias=63.5,
                       scale=-1.0 / 256.0, waits=[pm["xb"], pm.get("val")])
            m_val = V.op("tensor_tensor", xb_bf, xb_bf, r, ALU.subtract,
                         noserial=True, waits=[m_r])
            m_or = V.op("tensor_tensor", xb_i, xb_i, s15, ALU.bitwise_or,
                        noserial=True, waits=[m_val])
            m_m0 = V.op("tensor_tensor", xb_bf[:, 0:512], xb_bf[:, 0:512],
                        scb2, ALU.mult, noserial=True,
                        waits=[m_or, pm["sc"]])
            m_m1 = V.op("tensor_tensor", xb_bf[:, 512:1024],
                        xb_bf[:, 512:1024], scb2, ALU.mult, noserial=True,
                        waits=[m_m0])
            pm.update(r=m_r, val=m_val, orr=m_or, mul1=m_m1)
            guready.setdefault(s, []).append(m_m1)

        unitsA = []
        dqA = {"i1": 0, "i2": 0}

        def emitA_unit():
            if dqA["i1"] < len(unitsA):
                ffnA_s1(*unitsA[dqA["i1"]])
                dqA["i1"] += 1
            lim = dqA["i1"] - (1 if dqA["i1"] < len(unitsA) else 0)
            if dqA["i2"] < lim:
                ffnA_s2(*unitsA[dqA["i2"]])
                dqA["i2"] += 1

        def emitA_chunk(s, ch):
            dbuf = s % 2
            par = ch % 2
            C = 4 * s + ch
            if ch == 0:
                for mk in guready[s]:
                    TE.wait(mk)
                TE.wait(m_xnq)
            for mk in _aslist(psA_war[par]):
                TE.wait(mk)
            m_g = m_u2 = None
            for mat in range(2):
                for th in range(2):
                    for g in range(16):
                        kw = dict(start=(g == 0), stop=(g == 15))
                        off = 1024 * (g // 2) + 512 * (g % 2) + 128 * ch
                        lhsT = gubuf[mat][dbuf][:, off:off + 128]
                        rhs = xnqA[:, NQ * g + 512 * th:NQ * g + 512 * (th + 1)]
                        if par == 0:
                            dst = [ps_ab, ps_cd][mat][:, 512 * th:512 * (th + 1)]
                        else:
                            dst = ps_big[:, 1024 * mat + 512 * th:
                                         1024 * mat + 512 * (th + 1)]
                        if (th, g) == (1, 15):
                            mk = TE.mile("matmul", dst, lhsT, rhs, **kw)
                            if mat == 0:
                                m_g = mk
                            else:
                                m_u2 = mk
                        else:
                            TE.mm(dst, lhsT, rhs, **kw)
            pgf = ps_ab[:, 0:1024] if par == 0 else ps_big[:, 0:1024]
            puf = ps_cd[:, 0:1024] if par == 0 else ps_big[:, 1024:2048]
            m_sq2 = S.op("activation", glA[0], pgf, AF.Square,
                         waits=[m_g] + _aslist(sqA_war[0]))
            m_v2 = V.op("tensor_scalar", glA[1], glA[0], GC3, GC1,
                        ALU.mult, ALU.add, waits=[m_sq2])
            m_v3 = V.op("tensor_tensor", glA[2], glA[1], pgf, ALU.mult)
            m_t = S.op("activation", glA[3], glA[2], AF.Tanh, waits=[m_v3])
            m_up = S.op("activation", glA[0], puf, AF.Identity, scale=0.5,
                        waits=[m_u2, m_v2])
            m_pm = V.op("tensor_tensor", glA[1], pgf, glA[0], ALU.mult,
                        waits=[m_up])
            m_int = V.op("scalar_tensor_tensor", ievA[ch % 2], glA[3], 1.0,
                         glA[1], ALU.add, ALU.mult,
                         waits=[m_t, ievA_war[ch % 2]])
            m_isp = Dgp.dma(f"sp_int{ch % 2}", inter_d.ap()[C], ievA[ch % 2],
                            waits=[m_int])
            ispill[C] = m_isp
            ievA_war[ch % 2] = m_isp
            psA_war[par] = [m_pm, m_up]
            sqA_war[0] = m_pm

        pkA_war = [[None, None], [None, None]]
        for it in range(33):
            s = it
            prev = it - 1
            if s < 32:
                dbuf = s % 2
                for mat in range(2):
                    pkA_marks[(mat, dbuf)] = Dsy.dma(
                        f"ld_pkA{mat}{dbuf}", pkA[mat][dbuf],
                        [gate_pk3, up_pk3][mat].ap()[s],
                        waits=_aslist(pkA_war[mat][dbuf]))
                unitsA.extend((s, u) for u in range(16))
            for step in range(4):
                for _ in range(4):
                    emitA_unit()
                if prev >= 0:
                    emitA_chunk(prev, step)
            if s < 32:
                for mat in range(2):
                    pkA_war[mat][s % 2] = [pmA[mat]["chi"], pmA[mat]["clo"]]
            if prev >= 0:
                fenceA[prev] = TE.fence()
        while dqA["i2"] < len(unitsA):
            emitA_unit()
        mA_v = V.mark()
        mA_s = S.mark()
        m_scA_done = [p.mark("ld_scA0"), p.mark("ld_scA1")]

        oacc = [v8(AR0, 8192 * t, 8192, F32, D) for t in range(8)]
        wdbB = [v8(AR1, 0, 16384, BF16, 8192),
                v8(AR1, 16384, 16384, BF16, 8192)]
        ibufB = [v8(AR1, 32768, 8192, BF16, 4096),
                 v8(AR1, 40960, 8192, BF16, 4096)]
        pkB = [v8(AR3, 0, 4096, U8, 4096), v8(AR3, 4096, 4096, U8, 4096)]
        xbB_i = [v8(AR3, 8192, 4096, I16, 2048),
                 v8(AR3, 12288, 4096, I16, 2048)]
        xbB_bf = [v8(AR3, 8192, 4096, BF16, 2048),
                  v8(AR3, 12288, 4096, BF16, 2048)]
        c4B = [v8(AR1, 49152, 4096, I16, 2048), v8(AR2, 16384, 4096, I16, 2048)]
        s15B = [v8(AR1, 53248, 4096, I16, 2048),
                v8(AR2, 20480, 4096, I16, 2048)]
        rB = [v8(AR3, 16384, 4096, BF16, 2048), v8(AR2, 45056, 4096, BF16, 2048)]
        scb2B = [v8(AR2, 24576, 2048, BF16, 1024),
                 v8(AR2, 26624, 2048, BF16, 1024)]
        finxB = v8(AR2, 28672, 8192, F32, D)
        finsB = v8(AR2, 36864, 8192, F32, D)
        finlB = v8(AR1, 57344, 8192, F32, D)

        m_arB = [mA_v, mA_s, ispill.get(126), ispill.get(127)] + m_scA_done

        pmB = [dict(), dict()]
        wdready = {}
        fenceB = {}
        pkB_marks = {}
        pkB_war = [None, None]
        ibufm = {}
        accB_war = {0: [mA_v, mA_s], 1: [mA_v, mA_s]}
        oaccV_war = [None] * 8

        def ffnB_s1(b, u):
            j, cc = u // 2, u % 2
            L = u % 2
            pm = pmB[L]
            dbuf = b % 2
            Jp = 2 * b + j
            pk = pkB[dbuf][:, 1024 * u:1024 * (u + 1)]
            p0 = (8 * Jp) % 128
            c0 = 2048 * (Jp // 16) + 1024 * cc
            src_sc = (ds_sc_sb[p0:p0 + 8, c0:c0 + 1024]
                      .unsqueeze(1).broadcast_to([8, 16, 1024]))
            wl = [pm.get("mul1")]
            if b == 0:
                wl += [m_dssc] + m_arB
            m_sc = Dgp.dma(f"ld_scB{L}", scb2B[L], src_sc, waits=wl)
            wl = [pm.get("xb"), pkB_marks[dbuf]]
            if b == 0:
                wl += m_arB
            m_chi = S.op("activation", c4B[L][:, 1024:2048], pk, AF.Identity,
                         scale=0.0625, bias=-0.499999, waits=wl)
            m_clo = V.op("scalar_tensor_tensor", c4B[L][:, 0:1024],
                         c4B[L][:, 1024:2048], -16.0, pk, ALU.mult, ALU.add,
                         noserial=True, waits=[m_chi, pm.get("xb")])
            m_s15 = V.op("tensor_scalar", s15B[L], c4B[L], 3, 15,
                         ALU.logical_shift_right, ALU.logical_shift_left,
                         noserial=True,
                         waits=[m_clo, pm.get("orr")]
                               + (m_arB if b == 0 else []))
            m_and = V.op("tensor_single_scalar", c4B[L], c4B[L], 7,
                         ALU.bitwise_and, noserial=True, waits=[m_s15])
            m_xb = V.op("tensor_scalar", xbB_i[L], c4B[L], 64, 16128,
                        ALU.mult, ALU.add, noserial=True,
                        waits=[m_and, pm.get("mul1")]
                              + (m_arB if b == 0 else []))
            pm.update(sc=m_sc, chi=m_chi, clo=m_clo, s15=m_s15, xb=m_xb)

        def ffnB_s2(b, u):
            j, cc = u // 2, u % 2
            L = u % 2
            pm = pmB[L]
            dbuf = b % 2
            m_r = S.op("activation", rB[L], xbB_i[L], AF.Relu, bias=63.5,
                       scale=-1.0 / 256.0,
                       waits=[pm["xb"], pm.get("val")]
                             + (m_arB if b == 0 else []))
            m_val = V.op("tensor_tensor", xbB_bf[L], xbB_bf[L], rB[L],
                         ALU.subtract, noserial=True, waits=[m_r])
            m_or = V.op("tensor_tensor", xbB_i[L], xbB_i[L], s15B[L],
                        ALU.bitwise_or, noserial=True, waits=[m_val])
            d0 = wdbB[dbuf][:, 2048 * 2 * j + 1024 * cc:
                            2048 * 2 * j + 1024 * (cc + 1)]
            d1 = wdbB[dbuf][:, 2048 * (2 * j + 1) + 1024 * cc:
                            2048 * (2 * j + 1) + 1024 * (cc + 1)]
            wl = [m_or, pm["sc"]]
            wl += _aslist(fenceA[31] if b < 2 else fenceB.get(b - 2))
            m_m0 = V.op("tensor_tensor", d0, xbB_bf[L][:, 0:1024], scb2B[L],
                        ALU.mult, noserial=True, waits=wl)
            m_m1 = V.op("tensor_tensor", d1, xbB_bf[L][:, 1024:2048],
                        scb2B[L], ALU.mult, noserial=True, waits=[m_m0])
            pm.update(r=m_r, val=m_val, orr=m_or, mul1=m_m1)
            wdready.setdefault(b, []).append(m_m1)

        unitsB = []
        dqB = {"i1": 0, "i2": 0}

        def emitB_unit():
            if dqB["i1"] < len(unitsB):
                ffnB_s1(*unitsB[dqB["i1"]])
                dqB["i1"] += 1
            lim = dqB["i1"] - (1 if dqB["i1"] < len(unitsB) else 0)
            if dqB["i2"] < lim:
                ffnB_s2(*unitsB[dqB["i2"]])
                dqB["i2"] += 1

        def emitB_mm(b, tc):
            dbuf = b % 2
            par = tc % 2
            if tc == 0:
                for mk in wdready[b]:
                    TE.wait(mk)
                for mk in ibufm[b]:
                    TE.wait(mk)
            for mk in _aslist(accB_war[par]):
                TE.wait(mk)
            m_mm = None
            for g in range(4):
                kw = dict(start=(g == 0), stop=(g == 3))
                lhsT = ibufB[dbuf][:, 1024 * g + 128 * tc:
                                   1024 * g + 128 * (tc + 1)]
                for hq in range(4):
                    rhs = wdbB[dbuf][:, 2048 * g + 512 * hq:
                                     2048 * g + 512 * (hq + 1)]
                    if par == 0:
                        dst = ps_big[:, 512 * hq:512 * (hq + 1)]
                    else:
                        dst = [ps_ab, ps_cd][hq // 2][:, 512 * (hq % 2):
                                                      512 * (hq % 2 + 1)]
                    if (g, hq) == (3, 3):
                        m_mm = TE.mile("matmul", dst, lhsT, rhs, **kw)
                    else:
                        TE.mm(dst, lhsT, rhs, **kw)
            if par == 0:
                pairs = [(oacc[tc], ps_big[:, 0:2048])]
            else:
                pairs = [(oacc[tc][:, 0:1024], ps_ab[:, 0:1024]),
                         (oacc[tc][:, 1024:2048], ps_cd[:, 0:1024])]
            wl = [m_mm, oaccV_war[tc]]
            if b == 0:
                wl += _aslist(fenceA.get(30)) + _aslist(fenceA[31])
            m_acc = None
            for dsl, sf in pairs:
                if b == 0:
                    m_acc = V.op("tensor_copy", dsl, sf, waits=wl)
                else:
                    m_acc = V.op("tensor_tensor", dsl, dsl, sf, ALU.add,
                                 waits=wl)
                wl = []
            accB_war[par] = [m_acc]
            oaccV_war[tc] = m_acc

        for it in range(33):
            b = it
            prevb = it - 1
            if b < 32:
                dbuf = b % 2
                pkB_marks[dbuf] = Dsy.dma(
                    f"ld_pkB{dbuf}", pkB[dbuf], down_pk3.ap()[b],
                    waits=_aslist(pkB_war[dbuf])
                          + (m_arB if b < 2 else []))
                ibm = []
                for g in range(4):
                    wl = [ispill[4 * b + g]]
                    if g == 0:
                        wl += _aslist(fenceB.get(b - 2))
                        if b < 2:
                            wl += [mA_v]
                    ibm.append(Dsc.dma(f"ld_int{dbuf}",
                                       ibufB[dbuf][:, 1024 * g:1024 * (g + 1)],
                                       inter_d.ap()[4 * b + g], waits=wl))
                ibufm[b] = ibm
                unitsB.extend((b, u) for u in range(4))
            for tc in range(8):
                if tc % 2 == 0:
                    emitB_unit()
                if prevb >= 0:
                    emitB_mm(prevb, tc)
            if b < 32:
                pkB_war[b % 2] = [pmB[0]["chi"], pmB[0]["clo"],
                                  pmB[1]["chi"], pmB[1]["clo"]]
            if prevb >= 0:
                fenceB[prevb] = TE.fence()
        while dqB["i2"] < len(unitsB):
            emitB_unit()

        m_fx_war = None
        m_fs_war = None
        m_sil_war = None
        for tc in range(8):
            wl = [m_fx_war]
            if tc == 0:
                wl += m_scA_done
            m_fx = Dsy.dma("ld_fx", finxB, x_q.ap()[128 * tc:128 * (tc + 1), :],
                           waits=wl)
            m_fs = Dsy.dma("ld_fs", finsB, s_d.ap()[128 * tc:128 * (tc + 1), :],
                           waits=[m_sp_s if tc == 0 else None, m_fs_war])
            wl = [oaccV_war[tc], m_sil_war]
            if tc == 0:
                wl += [ispill.get(127), mA_v, mA_s]
            m_sil = S.op("activation", finlB, oacc[tc], AF.Silu, waits=wl)
            m_o1 = V.op("tensor_tensor", oacc[tc], finlB, finsB, ALU.mult,
                        waits=[m_sil, m_fs])
            m_o2 = V.op("tensor_tensor", oacc[tc], oacc[tc], finxB, ALU.add,
                        waits=[m_fx])
            Dsc.dma("sp_out", out.ap()[128 * tc:128 * (tc + 1), :], oacc[tc],
                    waits=[m_o2])
            m_fx_war = m_o2
            m_fs_war = m_o1
            m_sil_war = m_o1

        p.wait("sync", p.mark("sp_out"))

        with nc.Block() as block:
            @block.tensor
            def _(eng):
                for f in p.body["tensor"]:
                    f(eng)

            @block.vector
            def _(eng):
                for f in p.body["vector"]:
                    f(eng)

            @block.scalar
            def _(eng):
                for f in p.body["scalar"]:
                    f(eng)

            @block.gpsimd
            def _(eng):
                for f in p.body["gpsimd"]:
                    f(eng)

            @block.sync
            def _(eng):
                for f in p.body["sync"]:
                    f(eng)

        res = run_bass_kernel_spmd(nc, in_maps, core_ids=list(range(N_CORES)),
                                   trace=trace)
    return res


def kernel(**inputs):
    in_maps = _host_inputs(inputs)
    res = build_and_run(in_maps)
    full = np.empty((2, T, D), np.float32)
    for c in range(N_CORES):
        b, q = divmod(c, 4)
        oc = np.asarray(res.results[c]["out"])
        for jt, t in enumerate(TILES[q]):
            full[b, 256 * t:256 * (t + 1)] = oc[256 * jt:256 * (jt + 1)]
    return full
